# version 26
# baseline (speedup 1.0000x reference)
import math
import contextlib
import numpy as np
import ml_dtypes

import concourse.bass as bass
import concourse.mybir as mybir
from concourse.bass_utils import run_bass_kernel_spmd
from concourse.dve_op import DveOp
from concourse.dve_spec import (
    Spec, Src0, Src1, C0, C1, C2, Zero, One, maxx, minn, lower as dve_lower,
)
import concourse.dve_ops as dve_ops_mod
from concourse.dve_ops import OPS as DVE_OPS

F32 = mybir.dt.float32
BF16 = mybir.dt.bfloat16
I32 = mybir.dt.int32
AL = mybir.AluOpType
AF = mybir.ActivationFunctionType


def _register_op(name, spec, reference, replace=None):
    if name in dve_ops_mod._SUB_OPCODE_FOR_NAME:
        for o in DVE_OPS:
            if o.name == name:
                return o
        raise RuntimeError(f"{name}: row registered but op missing")
    shas = {}
    for ver in ("v3", "v4"):
        shas[ver] = dve_lower(spec, ver=ver, name=name).sha(ver)
    op = DveOp(name, spec, subdim=False, uops_sha=shas, reference=reference)
    if replace is not None:
        idx = next(i for i, o in enumerate(DVE_OPS) if o.name == replace)
        DVE_OPS[idx] = op
        row = dve_ops_mod._CUSTOM_DVE_ROW_BASE + idx
    else:
        DVE_OPS.append(op)
        row = dve_ops_mod._CUSTOM_DVE_ROW_BASE + len(DVE_OPS) - 1
    assert row < 0x20
    dve_ops_mod.CUSTOM_DVE_SPECS[name] = spec
    dve_ops_mod.CUSTOM_DVE_REFS[name] = reference
    dve_ops_mod._SUB_OPCODE_FOR_NAME[name] = row
    return op


_w = maxx(Src0, C0)
PAIR_OP = _register_op(
    "MK_PWLPAIR",
    Spec(body=(_w * C1 + Src1) + minn(_w, C0 + One) * C2),
    lambda in0, in1, s0, s1, imm2: (
        np.maximum(in0, s0) * s1 + in1
        + np.minimum(np.maximum(in0, s0), s0 + 1.0) * imm2),
)

_r = (Src0 - Src1) + C0
_g3 = _r >= C0
_s2 = _r - _g3 * C0
_g2 = _s2 >= C2
QF1_OP = _register_op(
    "MK_QF1",
    Spec(body=_s2 - _g2 * C2),
    lambda in0, in1, s0, s1, imm2: (
        lambda r: (lambda s2: s2 - (s2 >= imm2) * imm2)(r - (r >= s0) * s0)
    )((in0 - in1) + s0),
    replace="DVEHASH32L_SEED",
)

_gg1 = Src0 >= C0
_ss4 = Src0 - _gg1 * C0
_gg0 = _ss4 >= C1
QF2_OP = _register_op(
    "MK_QF2",
    Spec(body=Src1 - (_ss4 - _gg0)),
    lambda in0, in1, s0, s1, imm2: (
        lambda s4: in1 - (s4 - (s4 >= s1))
    )(in0 - (in0 >= s0) * s0),
    replace="DVEHASH32L_BODY",
)

FLOOR_OP = _register_op(
    "MK_FLOOR",
    Spec(body=((Src0 >= Src1) + Src1) - One),
    lambda in0, in1, s0, s1, imm2: ((in0 >= in1) + in1) - 1.0,
)


def _pwl_consts():
    SQ2 = math.sqrt(2.0)
    U = [256.0 * (j * 0.25) * 0.5 * math.erf((j * 0.25) / SQ2) for j in range(17)]
    sig = [U[j + 1] - U[j] for j in range(16)]
    pairs, kconst = [], 0.0
    for k in range(8):
        beta = sig[2 * k + 1] - (sig[2 * k - 1] if k > 0 else 0.0)
        alpha = sig[2 * k] - sig[2 * k + 1]
        c = 2.0 * k
        pairs.append((c, float(beta), float(alpha)))
        kconst += c * (beta + alpha)
    b0 = -4096.0 - kconst + 64.5
    return pairs, float(b0)


PAIRS, B0 = _pwl_consts()

D = 1024
E = 2048
FF = 4096
LTOK = 2048
TA = 128
NCA = LTOK // TA
TB = 256
NCB = LTOK // TB
NFM = FF // 128
DVE_CONV_ROWS = 16


class Ctr:

    def __init__(self):
        self.v = {}
        self.logs = {}
        self._rec = []
        self._rep = None

    def add(self, name, amt=1):
        if self._rep is not None:
            r = self._rep[0][self._rep[1]]
            self._rep[1] += 1
            return r
        self.v[name] = self.v.get(name, 0) + amt
        r = self.v[name]
        if self._rec:
            self._rec[-1].append(r)
        return r

    def get(self, name):
        if self._rep is not None:
            r = self._rep[0][self._rep[1]]
            self._rep[1] += 1
            return r
        r = self.v.get(name, 0)
        if self._rec:
            self._rec[-1].append(r)
        return r

    def begin_record(self, idx):
        lst = []
        self.logs[idx] = lst
        self._rec.append(lst)

    def end_record(self):
        self._rec.pop()

    def begin_replay(self, idx):
        self._rep = [self.logs[idx], 0]

    def end_replay(self, idx):
        assert self._rep[1] == len(self.logs[idx]), \
            f"replay divergence in body {idx}"
        self._rep = None


class _DummyRes:
    def then_inc(self, *a, **k):
        return self


class _DummyEng:
    def __getattr__(self, n):
        def f(*a, **k):
            return _DummyRes()
        return f


class _SecBlock:

    def __init__(self, C):
        self.C = C
        self.bodies = []

    def _deco(self, eng_name):
        def deco(fn):
            idx = len(self.bodies)
            self.bodies.append((eng_name, fn))
            self.C.begin_record(idx)
            fn(_DummyEng())
            self.C.end_record()
        return deco

    @property
    def sync(self):
        return self._deco("sync")

    @property
    def tensor(self):
        return self._deco("tensor")

    @property
    def scalar(self):
        return self._deco("scalar")

    @property
    def vector(self):
        return self._deco("vector")

    @property
    def gpsimd(self):
        return self._deco("gpsimd")

    def emit(self, real_block):
        for eng_name in ["sync", "tensor", "scalar", "vector", "gpsimd"]:
            todo = [(i, f) for i, (e, f) in enumerate(self.bodies)
                    if e == eng_name]
            if not todo:
                continue

            def run(engine, todo=todo):
                for i, f in todo:
                    self.C.begin_replay(i)
                    f(engine)
                    self.C.end_replay(i)

            getattr(real_block, eng_name)(run)


def build_nc():
    nc = bass.Bass("TRN2", target_bir_lowering=False,
                   dynamic_dma_scratch_size=2048)
    dtt = lambda n, s, t, k: nc.dram_tensor(n, s, t, kind=k).ap()
    x_d = dtt("x", [LTOK, D], F32, "ExternalInput")
    w_in_d = dtt("w_in", [D, 2 * E], F32, "ExternalInput")
    w_conv_d = dtt("w_conv", [128, 64], F32, "ExternalInput")
    b_conv_d = dtt("b_conv", [128, 16], F32, "ExternalInput")
    w_sel_d = dtt("w_sel", [E, 96], F32, "ExternalInput")
    w_dtb_d = dtt("w_dtb", [128, E], F32, "ExternalInput")
    w_out_d = dtt("w_out", [E, D], F32, "ExternalInput")
    w1_d = dtt("w1", [D, FF], F32, "ExternalInput")
    b1_d = dtt("b1", [128, 32], F32, "ExternalInput")
    w2_d = dtt("w2", [FF, D], F32, "ExternalInput")
    b2_d = dtt("b2", [128, 8], F32, "ExternalInput")
    idb_d = dtt("idb", [128, 128], BF16, "ExternalInput")
    idf_d = dtt("idf", [128, 128], F32, "ExternalInput")
    out_d = dtt("out", [LTOK, D], F32, "ExternalOutput")

    C = Ctr()
    es = contextlib.ExitStack()
    with es:
        def sb(name, shape, t=F32):
            return es.enter_context(nc.sbuf_tensor(name, shape, t))

        wbigA = sb("wbigA", [128, 8 * (2 * E)], BF16)
        wbigB = sb("wbigB", [128, 32 * D], BF16)
        xn_sb = sb("xn_sb", [128, 2 * D], BF16)
        xnT_sb = sb("xnT_sb", [128, 2 * D], BF16)
        siluz = sb("siluz", [128, E], BF16)
        xa_p = sb("xa_p", [128, 16 * 128], BF16)
        bcd_rhs = sb("bcd_rhs", [128, 128], BF16)
        y_all = sb("y_all", [128, NFM * TB], BF16)
        idb = sb("idb_sb", [128, 128], BF16)

        xs = sb("xs", [128, 2 * D], F32)
        xc = sb("xc", [128, 16 * 131], F32)
        wq_p = sb("wq_p", [128, 2 * TB], F32)
        Xq_p = sb("Xq_p", [128, 4 * TB], F32)
        tt_p = sb("tt_p", [128, 2 * TB], F32)
        acc_p = sb("acc_p", [128, 8 * TB], F32)
        xq2_t = sb("xq2_t", [128, 2 * TB], F32)
        conv_t = sb("conv_t", [128, 4 * 128], F32)
        s3_t = conv_t
        out_sb = sb("out_sb", [128, 2 * D], F32)
        f_all = sb("f_all", [128, 8 * TB], F32)
        attn_fm = f_all
        stage = sb("stage", [128, 1024], F32)
        idf = sb("idf_sb", [128, 128], F32)
        smalls = sb("smalls", [128, 16], F32)
        kk_p = es.enter_context(nc.sbuf_tensor("kk_p", [128, 2 * TB], I32))
        k2_p = es.enter_context(nc.sbuf_tensor("k2_p", [128, 2 * TB], I32))
        w_conv_sb = sb("w_conv_sb", [128, 64], F32)
        b_conv_sb = sb("b_conv_sb", [128, 16], F32)
        c1_sb = sb("c1_sb", [128, 32], F32)
        c2_sb = sb("c2_sb", [128, 8], F32)
        consts = sb("consts", [128, 8], F32)

        pool = es.enter_context(nc.psum_tensor("pool", [128, 3584], F32))
        ps_tr = es.enter_context(nc.psum_tensor("ps_tr", [128, 1024], BF16))

        sems = {}
        for sname in ["ldsm", "ldx0", "ldx1", "lds0", "lds1", "ldo",
                      "pe", "sc", "ve", "gp", "sta0", "sta1", "stb0", "stb1",
                      "lds2", "lds3", "lds4", "lds5", "lds6", "lds7"]:
            sems[sname] = es.enter_context(nc.semaphore(sname))
        real_block = es.enter_context(nc.Block())
        block = _SecBlock(C)
        LDSM = sems["ldsm"]
        LDX = [sems["ldx0"], sems["ldx1"]]
        LDS = [sems[f"lds{k}"] for k in range(8)]
        PE, SC, VE, GP = sems["pe"], sems["sc"], sems["ve"], sems["gp"]
        STA = [sems["sta0"], sems["sta1"]]
        STB = [sems["stb0"], sems["stb1"]]
        LDO = sems["ldo"]

        def w_in_kt(k):
            return wbigA[:, k * (2 * E):(k + 1) * (2 * E)]

        w1_kt = w_in_kt

        def w_out_kt(k):
            return wbigB[:, k * D:(k + 1) * D]

        w2_kt = w_out_kt

        def w_sel_kt(k):
            return wbigB[:, 16 * D + k * 96: 16 * D + (k + 1) * 96]

        dtb_off = 16 * D + 16 * 96

        def dtb_ap():
            return wbigB[:, dtb_off: dtb_off + E]

        @block.sync
        def _(sync):
            for (dst, src) in [(idb[:], idb_d[:]), (idf[:], idf_d[:]),
                               (w_conv_sb[:], w_conv_d[:]),
                               (b_conv_sb[:], b_conv_d[:]),
                               (out_sb[:, 0:32], b1_d[:]),
                               (out_sb[:, 32:40], b2_d[:])]:
                sync.dma_start(out=dst, in_=src).then_inc(LDSM, 16)
                C.add("ldsm", 16)
        ld_small = C.get("ldsm")

        @block.vector
        def _(vector):
            for col, val in [(0, 256.0), (1, 1.0 / 64), (2, -128.0),
                             (3, 0.5), (4, B0), (5, 1e-5),
                             (6, -0.25), (7, -32.0)]:
                vector.memset(consts[:, col:col + 1], float(val))
            vector.memset(bcd_rhs[64:65, :], 1.0)
            for r in range(16):
                vector.memset(xc[:, r * 131: r * 131 + 3], 0.0)
            vector.wait_ge(LDSM, ld_small)
            vector.tensor_scalar(out=c1_sb[:], in0=out_sb[:, 0:32], scalar1=256.0,
                                 scalar2=8192.5, op0=AL.mult, op1=AL.add)
            vector.tensor_scalar(out=c2_sb[:], in0=out_sb[:, 32:40], scalar1=256.0,
                                 scalar2=8192.5, op0=AL.mult, op1=AL.add)\
                  .then_inc(VE, 1)
            C.add("ve")
        ve_consts = C.get("ve")

        @block.scalar
        def _(scalar):
            scalar.wait_ge(VE, ve_consts)
            scalar.wait_ge(LDSM, ld_small)

        @block.tensor
        def _(tensor):
            tensor.wait_ge(VE, ve_consts)
            tensor.wait_ge(LDSM, ld_small)

        @block.gpsimd
        def _(g):
            g.wait_ge(VE, ve_consts)
            g.wait_ge(LDSM, ld_small)

        for i0 in range(2):
            @block.sync
            def _(sync, i=i0):
                sl = i % 2
                sync.dma_start(out=xs[:, sl * D:(sl + 1) * D],
                               in_=x_d[i * TA:(i + 1) * TA, :])\
                    .then_inc(LDX[sl], 16)
                C.v[f"ldx_A{i}"] = C.add(f"ldx{sl}", 16)
                C.v[f"ldxsl_A{i}"] = sl

        def slot_wA(idx, cols):
            s = idx % 8
            if s < 2:
                return stage[:, s * 512: s * 512 + cols]
            if s < 6:
                return acc_p[:, (s - 2) * 512: (s - 2) * 512 + cols]
            return Xq_p[:, (s - 6) * 512: (s - 6) * 512 + cols]

        def slot_w1(idx, cols):
            s = idx % 6
            if s < 2:
                return stage[:, s * 512: s * 512 + cols]
            return xc[:, (s - 2) * 512: (s - 2) * 512 + cols]

        def slot_w2(idx, cols):
            s = idx % 6
            if s < 4:
                return f_all[:, s * 512: s * 512 + cols]
            return stage[:, (s - 4) * 512: (s - 4) * 512 + cols]

        def stage_dmas(pieces, tag, gp_base, slot_fn, nslot, extra=None):
            for idx, (src_ap, dst_ap, cols, pe_gate) in enumerate(pieces):
                @block.sync
                def _(sync, src_ap=src_ap, idx=idx, cols=cols, tag=tag,
                      gp_base=gp_base, slot_fn=slot_fn, nslot=nslot,
                      extra=extra):
                    s = idx % nslot
                    prev = C.get(f"lds{s}")
                    if prev > 0:
                        sync.wait_ge(LDS[s], prev)
                    if idx >= nslot:
                        sync.wait_ge(GP, gp_base + idx - nslot + 1)
                    if extra is not None and idx in extra:
                        for sem, val in extra[idx]:
                            sync.wait_ge(sem, val)
                    sync.dma_start(out=slot_fn(idx, cols), in_=src_ap)\
                        .then_inc(LDS[s], 16)
                    C.v[f"ld_{tag}_{idx}"] = C.add(f"lds{s}", 16)

        def stage_casts(pieces, tag, slot_fn, nslot, gates=None):
            @block.gpsimd
            def _(g, tag=tag, slot_fn=slot_fn, nslot=nslot):
                for idx, (src_ap, dst_ap, cols, pe_gate) in enumerate(pieces):
                    if gates is not None:
                        pe_gate = gates[idx]
                    g.wait_ge(LDS[idx % nslot], C.get(f"ld_{tag}_{idx}"))
                    if pe_gate:
                        g.wait_ge(PE, pe_gate)
                    g.tensor_copy(dst_ap, slot_fn(idx, cols)).then_inc(GP, 1)
                    C.v[f"gp_{tag}_{idx}"] = C.add("gp")

        def stage_pieces(pieces, tag):
            stage_dmas(pieces, tag, C.get("gp"), slot_wA, 8)
            stage_casts(pieces, tag, slot_wA, 8)

        piecesA = []
        for q in range(8):
            for k in range(8):
                piecesA.append((w_in_d[k * 128:(k + 1) * 128, q * 512:(q + 1) * 512],
                                w_in_kt(k)[:, q * 512:(q + 1) * 512], 512, 0))
        nw_in = len(piecesA)
        for k in range(16):
            for q in range(2):
                piecesA.append((w_out_d[k * 128:(k + 1) * 128, q * 512:(q + 1) * 512],
                                w_out_kt(k)[:, q * 512:(q + 1) * 512], 512, 0))
        for k in range(16):
            piecesA.append((w_sel_d[k * 128:(k + 1) * 128, :], w_sel_kt(k), 96, 0))
        for q in range(4):
            piecesA.append((w_dtb_d[:, q * 512:(q + 1) * 512],
                            dtb_ap()[:, q * 512:(q + 1) * 512], 512, 0))
        n_outsel = len(piecesA)
        for k in range(20, 32):
            for q in range(2):
                piecesA.append((w2_d[k * 128:(k + 1) * 128, q * 512:(q + 1) * 512],
                                w2_kt(k)[:, q * 512:(q + 1) * 512], 512, 0))
        stage_pieces(piecesA, "wA")
        gp_w_in_q = [C.get(f"gp_wA_{q * 8 + 7}") for q in range(8)]
        gp_w_in = C.get(f"gp_wA_{nw_in - 1}")
        gp_w_outsel = C.get(f"gp_wA_{n_outsel - 1}")

        piecesB = []
        for q in range(8):
            for k in range(8):
                piecesB.append((w1_d[k * 128:(k + 1) * 128, q * 512:(q + 1) * 512],
                                w1_kt(k)[:, q * 512:(q + 1) * 512], 512, 0))
        for k in range(20):
            for q in range(2):
                piecesB.append((w2_d[k * 128:(k + 1) * 128, q * 512:(q + 1) * 512],
                                w2_kt(k)[:, q * 512:(q + 1) * 512], 512, 0))

        def ln_chunk(ptile_ap, xn_dst_ap, tag, prev_var_tag, xn_guard):
            mu = smalls[:, 0:1]; var = smalls[:, 1:2]
            sd = smalls[:, 7:8]; rstd = smalls[:, 6:7]

            @block.vector
            def _(vector):
                vector.wait_ge(LDX[C.get(f"ldxsl_{tag}")], C.get(f"ldx_{tag}"))
                if C.get(f"sc_rstd_{prev_var_tag}"):
                    vector.wait_ge(SC, C.get(f"sc_rstd_{prev_var_tag}"))
                vector.drain()
                vector.bn_stats(out=smalls[:, 4:10], in_=ptile_ap[:, 0:512])
                vector.bn_stats(out=smalls[:, 10:16], in_=ptile_ap[:, 512:1024])
                vector.drain()
                vector.bn_aggr(out=smalls[:, 0:2], in_=smalls[:, 4:16])\
                      .then_inc(VE, 1)
                C.v[f"ve_var_{tag}"] = C.add("ve")

            @block.scalar
            def _(scalar):
                scalar.wait_ge(VE, C.get(f"ve_var_{tag}"))
                scalar.activation(out=sd, in_=var, func=AF.Sqrt,
                                  bias=consts[:, 5:6]).then_inc(SC, 1)
                C.v[f"sc_rstd_{tag}"] = C.add("sc")

            @block.vector
            def _(vector):
                vector.wait_ge(SC, C.get(f"sc_rstd_{tag}"))
                vector.reciprocal(rstd, sd)
                vector.drain()
                if xn_guard:
                    vector.wait_ge(PE, xn_guard)
                vector.tensor_scalar(out=xn_dst_ap, in0=ptile_ap, scalar1=mu,
                                     scalar2=rstd, op0=AL.subtract, op1=AL.mult)\
                      .then_inc(VE, 1)
                C.v[f"ve_xn_{tag}"] = C.add("ve")

        def emit_transposes2(src_aps, dst_aps, wait_ve, tagp, contig_dst=None):
            n = len(src_aps)

            @block.tensor
            def _(tensor):
                tensor.wait_ge(VE, wait_ve)
                if C.get("last_trev"):
                    tensor.wait_ge(SC, C.get("last_trev"))
                last = None
                for d in range(n):
                    last = tensor.transpose(out=ps_tr[:, d * 128:(d + 1) * 128],
                                            in_=src_aps[d], identity=idb[:])
                last.then_inc(PE, 1)
                pe_val = C.add("pe")
                for d in range(n):
                    C.v[f"{tagp}_{d}"] = pe_val

            @block.scalar
            def _(scalar):
                scalar.wait_ge(PE, C.get(f"{tagp}_{n-1}"))
                if contig_dst is not None:
                    scalar.activation(out=contig_dst, in_=ps_tr[:, 0:n * 128],
                                      func=AF.Copy).then_inc(SC, 1)
                else:
                    last = None
                    for d in range(n):
                        last = scalar.activation(
                            out=dst_aps[d], in_=ps_tr[:, d * 128:(d + 1) * 128],
                            func=AF.Copy)
                    last.then_inc(SC, 1)
                C.v["last_trev"] = C.add("sc")
            C.v[f"{tagp}_last"] = C.get("last_trev")

        def emit_x_load(i):
            sl = i % 2
            tag = f"A{i}"

            @block.sync
            def _(sync, i=i, sl=sl, tag=tag):
                sync.wait_ge(VE, C.get(f"ve_epi_A{i-2}"))
                sync.dma_start(out=xs[:, sl * D:(sl + 1) * D],
                               in_=x_d[i * TA:(i + 1) * TA, :])\
                    .then_inc(LDX[sl], 16)
                C.v[f"ldx_{tag}"] = C.add(f"ldx{sl}", 16)
                C.v[f"ldxsl_{tag}"] = sl

        def emit_ln_A(i):
            sl = i % 2
            tag = f"A{i}"
            xn_guard = C.get(f"trA{i-2}_last_pe") if i >= 2 else 0
            ln_chunk(xs[:, sl * D:(sl + 1) * D], xn_sb[:, sl * D:(sl + 1) * D],
                     tag, f"A{i-1}", xn_guard)

        def emit_tr_A(i):
            sl = i % 2
            tag = f"A{i}"
            srcs = [xn_sb[:, sl * D + d * 128: sl * D + (d + 1) * 128]
                    for d in range(8)]
            dsts = [xnT_sb[:, sl * 1024 + d * 128: sl * 1024 + (d + 1) * 128]
                    for d in range(8)]
            emit_transposes2(srcs, dsts, C.get(f"ve_xn_{tag}"), f"trA{i}",
                             contig_dst=xnT_sb[:, sl * 1024:(sl + 1) * 1024])
            C.v[f"trA{i}_last_pe"] = C.get("pe")

        def emit_waves(i, wvs):
            sl = i % 2
            for wv in wvs:
                @block.tensor
                def _(tensor, i=i, sl=sl, wv=wv):
                    if wv == 0:
                        tensor.wait_ge(SC, C.get(f"trA{i}_last"))
                    if i == 0:
                        tensor.wait_ge(GP, gp_w_in_q[wv])
                    gk = (f"sc_wave_{i}_{wv-2}" if wv >= 2
                          else f"sc_wave_{i-1}_{wv+6}")
                    if C.get(gk):
                        tensor.wait_ge(SC, C.get(gk))
                    last = None
                    for mq in range(4):
                        m = wv * 4 + mq
                        for kt in range(8):
                            last = tensor.matmul(
                                out=pool[:, (wv % 2) * 512 + mq * 128:
                                         (wv % 2) * 512 + (mq + 1) * 128],
                                lhsT=w_in_kt(kt)[:, m * 128:(m + 1) * 128],
                                rhs=xnT_sb[:, sl * 1024 + kt * 128:
                                           sl * 1024 + (kt + 1) * 128],
                                start=(kt == 0), stop=(kt == 7))
                    last.then_inc(PE, 1)
                    C.v[f"pe_wave_{i}_{wv}"] = C.add("pe")

                @block.scalar
                def _(scalar, i=i, wv=wv):
                    scalar.wait_ge(PE, C.get(f"pe_wave_{i}_{wv}"))
                    if wv == 4 and C.get(f"pe_attn_done_{i-1}"):
                        scalar.wait_ge(PE, C.get(f"pe_attn_done_{i-1}"))
                    if wv <= 3 and C.get(f"cv_{i-1}_{4*wv+2}"):
                        scalar.wait_ge(VE, C.get(f"cv_{i-1}_{4*wv+2}"))
                    last = None
                    for mq in range(4):
                        m = wv * 4 + mq
                        src = pool[:, (wv % 2) * 512 + mq * 128:
                                   (wv % 2) * 512 + (mq + 1) * 128]
                        if m < 16:
                            last = scalar.activation(
                                out=xc[:, m * 131 + 3: m * 131 + 131],
                                in_=src, func=AF.Copy)
                        else:
                            last = scalar.activation(
                                out=siluz[:, (m - 16) * 128:(m - 15) * 128],
                                in_=src, func=AF.Silu)
                    last.then_inc(SC, 1)
                    C.v[f"sc_wave_{i}_{wv}"] = C.add("sc")

        def emit_conv(i):
            for r in range(0, 16, 2):
                def conv_body(eng, i=i, r=r):
                    eng.wait_ge(SC, C.get(f"sc_wave_{i}_{(r + 1) // 4}"))
                    if r == 0 and C.get(f"pe_fa_{i-1}"):
                        eng.wait_ge(PE, C.get(f"pe_fa_{i-1}"))
                    eng.drain()
                    cts = {}
                    for rr in (r, r + 1):
                        base = rr * 131
                        sa = 2 * (rr % 2)
                        cta = conv_t[:, sa * 128:(sa + 1) * 128]
                        ctb = conv_t[:, (sa + 1) * 128:(sa + 2) * 128]
                        cts[rr] = (cta, ctb)
                        eng.tensor_scalar(out=cta,
                                          in0=xc[:, base + 3: base + 131],
                                          scalar1=w_conv_sb[:, rr * 4 + 3: rr * 4 + 4],
                                          scalar2=b_conv_sb[:, rr:rr + 1],
                                          op0=AL.mult, op1=AL.add)
                        eng.tensor_scalar(out=ctb,
                                          in0=xc[:, base + 2: base + 130],
                                          scalar1=w_conv_sb[:, rr * 4 + 2: rr * 4 + 3],
                                          scalar2=None, op0=AL.mult)
                    eng.drain()
                    for rr in (r, r + 1):
                        base = rr * 131
                        cta, ctb = cts[rr]
                        eng.scalar_tensor_tensor(
                            out=cta, in0=xc[:, base + 1: base + 129],
                            scalar=w_conv_sb[:, rr * 4 + 1: rr * 4 + 2],
                            in1=cta, op0=AL.mult, op1=AL.add)
                        eng.scalar_tensor_tensor(
                            out=ctb, in0=xc[:, base: base + 128],
                            scalar=w_conv_sb[:, rr * 4: rr * 4 + 1],
                            in1=ctb, op0=AL.mult, op1=AL.add)
                    eng.drain()
                    for rr in (r, r + 1):
                        cta, ctb = cts[rr]
                        eng.tensor_tensor(out=f_all[:, rr * 128:(rr + 1) * 128],
                                          in0=cta, in1=ctb, op=AL.add)
                    last = None
                    for rr in (r, r + 1):
                        last = eng.tensor_copy(xc[:, rr * 131: rr * 131 + 3],
                                               xc[:, rr * 131 + 128: rr * 131 + 131])
                    last.then_inc(VE, 1)
                    C.v[f"cv_{i}_{r}"] = C.v[f"cv_{i}_{r+1}"] = C.add("ve")

                block.vector(conv_body)

                @block.scalar
                def _(scalar, i=i, r=r):
                    scalar.wait_ge(VE, C.get(f"cv_{i}_{r}"))
                    if r == 0 and C.get(f"pe_attn_done_{i-1}"):
                        scalar.wait_ge(PE, C.get(f"pe_attn_done_{i-1}"))
                    last = None
                    for rr in (r, r + 1):
                        last = scalar.activation(
                            out=xa_p[:, rr * 128:(rr + 1) * 128],
                            in_=f_all[:, rr * 128:(rr + 1) * 128],
                            func=AF.Silu)
                    last.then_inc(SC, 1)
                    C.v[f"sc_xa_{i}_{r}"] = C.v[f"sc_xa_{i}_{r+1}"] = C.add("sc")

            @block.vector
            def _(vector, i=i):
                vector.wait_ge(SC, C.get(f"sc_xa_{i}_14"))
                vector.tensor_tensor(out=siluz[:], in0=xa_p[:], in1=siluz[:],
                                     op=AL.mult).then_inc(VE, 1)
                C.v[f"ve_oa_{i}_15"] = C.add("ve")

        def emit_attn(i):
            @block.tensor
            def _(tensor, i=i):
                tensor.wait_ge(SC, C.get(f"sc_xa_{i}_15"))
                tensor.wait_ge(VE, C.get(f"ve_oa_{i}_15"))
                if i == 0:
                    tensor.wait_ge(GP, gp_w_outsel)
                if C.get(f"ve_atf_{i-1}"):
                    tensor.wait_ge(VE, C.get(f"ve_atf_{i-1}"))
                if C.get(f"ve_bcdev_{i-1}"):
                    tensor.wait_ge(VE, C.get(f"ve_bcdev_{i-1}"))
                if C.get(f"ve_epi_A{i-1}"):
                    tensor.wait_ge(VE, C.get(f"ve_epi_A{i-1}"))
                for r in range(16):
                    tensor.matmul(out=pool[0:32, 2048:2176],
                                  lhsT=w_sel_kt(r)[:, 0:32],
                                  rhs=xa_p[:, r * 128:(r + 1) * 128],
                                  start=(r == 0), stop=(r == 15))
                last = None
                for r in range(16):
                    last = tensor.matmul(out=pool[0:64, 2176:2304],
                                         lhsT=w_sel_kt(r)[:, 32:96],
                                         rhs=xa_p[:, r * 128:(r + 1) * 128],
                                         start=(r == 0), stop=(r == 15))
                last.then_inc(PE, 1)
                C.v[f"pe_bcd_{i}"] = C.add("pe")
                for o in range(8):
                    last = None
                    for r in range(16):
                        last = tensor.matmul(
                            out=pool[:, 1024 + o * 128: 1024 + (o + 1) * 128],
                            lhsT=w_out_kt(r)[:, o * 128:(o + 1) * 128],
                            rhs=siluz[:, r * 128:(r + 1) * 128],
                            start=(r == 0), stop=(r == 15))
                    last.then_inc(PE, 1)
                    C.v[f"pe_attn_{i}_{o}"] = C.add("pe")
                C.v[f"pe_attn_done_{i}"] = C.get("pe")

            @block.vector
            def _(vector, i=i):
                vector.wait_ge(PE, C.get(f"pe_bcd_{i}"))
                vector.tensor_copy(bcd_rhs[0:64, :], pool[0:64, 2176:2304])\
                      .then_inc(VE, 1)
                C.v[f"ve_bcdev_{i}"] = C.add("ve")

            @block.tensor
            def _(tensor, i=i):
                tensor.wait_ge(VE, C.get(f"ve_bcdev_{i}"))
                last = None
                for m in range(16):
                    last = tensor.matmul(out=pool[:, 2304:2432],
                                         lhsT=dtb_ap()[0:65, m * 128:(m + 1) * 128],
                                         rhs=bcd_rhs[0:65, :],
                                         start=True, stop=True)
                last.then_inc(PE, 1)
                C.v[f"pe_delta_{i}"] = C.add("pe")

        def emit_tail(i):
            sl = i % 2
            @block.vector
            def _(vector, i=i):
                last = None
                for o in range(8):
                    vector.wait_ge(PE, C.get(f"pe_attn_{i}_{(o // 4) * 4 + 3}"))
                    last = vector.tensor_copy(attn_fm[:, o * 128:(o + 1) * 128],
                                              pool[:, 1024 + o * 128:
                                                   1024 + (o + 1) * 128])
                last.then_inc(VE, 1)
                C.v[f"ve_atf_{i}"] = C.add("ve")

            @block.tensor
            def _(tensor, i=i):
                tensor.wait_ge(VE, C.get(f"ve_atf_{i}"))
                if C.get(f"ve_epi_A{i-1}"):
                    tensor.wait_ge(VE, C.get(f"ve_epi_A{i-1}"))
                last = None
                for o in range(8):
                    last = tensor.transpose(
                        out=pool[:, 2560 + o * 128: 2560 + (o + 1) * 128],
                        in_=attn_fm[:, o * 128:(o + 1) * 128],
                        identity=idf[:])
                last.then_inc(PE, 1)
                C.v[f"pe_fa_{i}"] = C.add("pe")

            @block.vector
            def _(vector, i=i, sl=sl):
                vector.wait_ge(PE, C.get(f"pe_fa_{i}"))
                if i >= 2:
                    vector.wait_ge(STA[sl], C.get(f"st_A{i-2}"))
                vector.tensor_tensor(out=out_sb[:, sl * D:(sl + 1) * D],
                                     in0=pool[:, 2560:3584],
                                     in1=xs[:, sl * D:(sl + 1) * D],
                                     op=AL.add).then_inc(VE, 1)
                C.v[f"ve_epi_A{i}"] = C.add("ve")

            @block.sync
            def _(sync, i=i, sl=sl):
                sync.wait_ge(VE, C.get(f"ve_epi_A{i}"))
                sync.dma_start(out=out_d[i * TA:(i + 1) * TA, :],
                               in_=out_sb[:, sl * D:(sl + 1) * D])\
                    .then_inc(STA[sl], 16)
                C.v[f"st_A{i}"] = C.add(f"sta{sl}", 16)

        emit_ln_A(0)
        emit_tr_A(0)
        emit_waves(0, (0, 1))
        def emit_x_load_B0(q):
            @block.sync
            def _(sync, q=q):
                sync.wait_ge(VE, C.get(f"ve_epi_A{14 + q}"))
                sync.dma_start(out=xs[:, q * D:(q + 1) * D],
                               in_=x_d[q * 128:(q + 1) * 128, :])\
                    .then_inc(LDX[q], 16)
                C.v[f"ldx_B0_{q}"] = C.add(f"ldx{q}", 16)
                C.v[f"ldxsl_B0_{q}"] = q

        for i in range(NCA + 1):
            if i >= 1:
                emit_tail(i - 1)
            if i == NCA:
                emit_x_load_B0(0)
                emit_x_load_B0(1)
                gpB0 = C.get("gp")
                extra1 = {0: [(GP, C.get("gp_wA_136"))],
                          1: [(GP, C.get("gp_wA_137"))],
                          2: [(VE, C.get(f"cv_{NCA-1}_14"))]}
                stage_dmas(piecesB[:64], "wB", gpB0, slot_w1, 6, extra=extra1)
                extra2 = {0: [(PE, C.get(f"pe_fa_{NCA-1}"))],
                          4: [(GP, gpB0 + 61)],
                          5: [(GP, gpB0 + 62)]}
                stage_dmas(piecesB[64:], "wB2", gpB0 + 64, slot_w2, 6,
                           extra=extra2)
                break
            if 2 <= i + 1 < NCA:
                emit_x_load(i + 1)
            if i + 1 < NCA:
                emit_ln_A(i + 1)
            emit_waves(i, (2, 3, 4, 5, 6, 7))
            if i + 1 < NCA:
                emit_tr_A(i + 1)
            emit_conv(i)
            if i + 1 < NCA:
                emit_waves(i + 1, (0, 1))
            emit_attn(i)

        gatesB1 = [C.get(f"pe_wave_{NCA-1}_{q}") for q in range(8)
                   for _k in range(8)]
        gatesB2 = [C.get(f"pe_delta_{NCA-1}")] * 40
        stage_casts(piecesB[:64], "wB", slot_w1, 6, gatesB1)
        stage_casts(piecesB[64:], "wB2", slot_w2, 6, gatesB2)
        gp_wB_q = [C.get(f"gp_wB_{q * 8 + 7}") for q in range(8)]
        gp_wB = C.get("gp")

        CQ = consts[:, 0:1]
        CI64 = consts[:, 1:2]
        CM128 = consts[:, 2:3]
        CH = consts[:, 3:4]
        CB0 = consts[:, 4:5]

        for j in range(NCB):
            for q in range(2):
                sl = (2 * j + q) % 2
                tag = f"B{j}_{q}"

                if j > 0:
                    @block.sync
                    def _(sync, j=j, q=q, sl=sl, tag=tag):
                        gk = f"ve_xn_B{j-1}_{q}"
                        sync.wait_ge(VE, C.get(gk))
                        sync.dma_start(out=xs[:, sl * D:(sl + 1) * D],
                                       in_=x_d[j * TB + q * 128:
                                               j * TB + (q + 1) * 128, :])\
                            .then_inc(LDX[sl], 16)
                        C.v[f"ldx_{tag}"] = C.add(f"ldx{sl}", 16)
                        C.v[f"ldxsl_{tag}"] = sl

                xn_guard = (C.get(f"trB{j-1}_{q}_last_pe") if j >= 1
                            else C.get(f"trA{14 + q}_last_pe"))
                prev_var = (f"B{j}_{q-1}" if q == 1 else
                            (f"B{j-1}_1" if j >= 1 else "A15"))
                ln_chunk(xs[:, sl * D:(sl + 1) * D], xn_sb[:, q * D:(q + 1) * D],
                         tag, prev_var, xn_guard)

                srcs = [xn_sb[:, q * D + d * 128: q * D + (d + 1) * 128]
                        for d in range(8)]
                dsts = [xnT_sb[:, d * TB + q * 128: d * TB + (q + 1) * 128]
                        for d in range(8)]
                @block.tensor
                def _(tensor, j=j, q=q):
                    if j >= 1 and q == 0:
                        tensor.wait_ge(PE, C.get(f"pe_h_{j-1}_{NFM-1}"))
                emit_transposes2(srcs, dsts, C.get(f"ve_xn_{tag}"), f"trB{j}_{q}")
                C.v[f"trB{j}_{q}_last_pe"] = C.get("pe")

            def wq_s(i):
                return wq_p[:, (i % 2) * TB:(i % 2 + 1) * TB]

            def XqP(f):
                return Xq_p[:, (f % 4) * TB:(f % 4) * TB + 2 * TB]

            def accP(f, half):
                s = (f // 2) % 2
                return acc_p[:, s * 4 * TB + half * 2 * TB:
                             s * 4 * TB + (half + 1) * 2 * TB]

            wqP = wq_p[:, 0:2 * TB]
            kkP = kk_p[:, 0:2 * TB]
            ttP = tt_p[:, 0:2 * TB]
            k2P = k2_p[:, 0:2 * TB]
            s3P = s3_t[:, 0:2 * TB]
            xq2P = xq2_t[:, 0:2 * TB]

            for g in range(0, NFM + 6, 2):
                @block.tensor
                def _(tensor, j=j, g=g):
                    if g < NFM:
                        gk = (f"sc_w_{j}_{g-2}" if g >= 2
                              else f"sc_w_{j-1}_{g+NFM-2}")
                        if C.get(gk):
                            tensor.wait_ge(SC, C.get(gk))
                        if g == 0:
                            tensor.wait_ge(SC, C.get(f"trB{j}_1_last"))
                            if j == 0:
                                tensor.wait_ge(SC, C.get(f"sc_wave_{NCA-1}_7"))
                            if C.get(f"sc_w4_{j-1}_7"):
                                tensor.wait_ge(SC, C.get(f"sc_w4_{j-1}_7"))
                        if j == 0:
                            tensor.wait_ge(GP, gp_wB_q[g // 4])
                        for fg in (g, g + 1):
                            hs = (fg % 2) * 512
                            last = None
                            for kt in range(8):
                                last = tensor.matmul(
                                    out=pool[:, hs: hs + TB],
                                    lhsT=w1_kt(kt)[:, fg * 128:(fg + 1) * 128],
                                    rhs=xnT_sb[:, kt * TB:(kt + 1) * TB],
                                    start=(kt == 0), stop=(kt == 7))
                            last.then_inc(PE, 1)
                            C.v[f"pe_h_{j}_{fg}"] = C.add("pe")

                @block.scalar
                def _(scalar, j=j, g=g):
                    if g < NFM:
                        if g >= 2:
                            scalar.wait_ge(VE, C.get(f"ve_X_{j}_{g-2}"))
                        elif C.get(f"ve_fq_{j-1}_6"):
                            scalar.wait_ge(VE, C.get(f"ve_fq_{j-1}_6"))
                        for fg in (g, g + 1):
                            scalar.wait_ge(PE, C.get(f"pe_h_{j}_{fg}"))
                            scalar.activation(out=wq_s(fg),
                                              in_=pool[:, (fg % 2) * 512:
                                                       (fg % 2) * 512 + TB],
                                              func=AF.Identity, scale=CQ,
                                              bias=c1_sb[:, fg:fg + 1])
                        scalar.drain()
                        scalar.copy(out=kkP, in_=wqP).then_inc(SC, 1)
                        C.v[f"sc_w_{j}_{g}"] = C.v[f"sc_w_{j}_{g+1}"] = C.add("sc")
                    f2 = g - 2
                    if 0 <= f2 < NFM:
                        if f2 >= 2:
                            scalar.wait_ge(VE, C.get(f"ve_p_{j}_{f2-2}"))
                        else:
                            scalar.wait_ge(VE, C.get(f"ve_X_{j}_{0}"))
                        scalar.activation(out=ttP, in_=XqP(f2),
                                          func=AF.Abs, scale=CI64, bias=CM128)
                        scalar.activation(out=accP(f2, 0), in_=XqP(f2),
                                          func=AF.Identity, scale=CH,
                                          bias=CB0).then_inc(SC, 1)
                        C.v[f"sc_t_{j}_{f2}"] = C.v[f"sc_t_{j}_{f2+1}"] = C.add("sc")
                    f3 = g - 4
                    if 0 <= f3 < NFM:
                        scalar.wait_ge(VE, C.get(f"ve_p_{j}_{f3}"))
                        scalar.copy(out=k2P, in_=accP(f3, 0)).then_inc(SC, 1)
                        C.v[f"sc_k2_{j}_{f3}"] = C.v[f"sc_k2_{j}_{f3+1}"] = C.add("sc")

                @block.vector
                def _(vector, j=j, g=g):
                    f3 = g - 4
                    f2 = g - 2
                    have3 = 0 <= f3 < NFM
                    have1 = g < NFM
                    have2 = 0 <= f2 < NFM
                    if have3:
                        vector.wait_ge(SC, C.get(f"sc_k2_{j}_{f3}"))
                        if f3 >= 2:
                            vector.wait_ge(SC, C.get(f"sc_y_{j}_{f3-2}"))
                        elif C.get(f"sc_f_{j-1}_7"):
                            vector.wait_ge(SC, C.get(f"sc_f_{j-1}_7"))
                    if have3 and not have2:
                        vector.drain()
                        vector._custom_dve(QF1_OP, out=s3P, in0=accP(f3, 0),
                                           in1=k2P, s0=8.0, imm2=4.0)
                        vector.drain()
                        vector._custom_dve(QF2_OP, out=xq2P, in0=s3P,
                                           in1=accP(f3, 0), s0=2.0, s1=1.0)\
                              .then_inc(VE, 1)
                        C.v[f"ve_q_{j}_{f3}"] = C.v[f"ve_q_{j}_{f3+1}"] = C.add("ve")
                    if have1 and not have2:
                        vector.wait_ge(SC, C.get(f"sc_w_{j}_{g}"))
                        vector.drain()
                        vector._custom_dve(QF1_OP, out=s3P, in0=wqP,
                                           in1=kkP, s0=8.0, imm2=4.0)
                        vector.drain()
                        vector._custom_dve(QF2_OP, out=XqP(g), in0=s3P,
                                           in1=wqP, s0=2.0, s1=1.0)\
                              .then_inc(VE, 1)
                        C.v[f"ve_X_{j}_{g}"] = C.v[f"ve_X_{j}_{g+1}"] = C.add("ve")
                    if have2:
                        vector.wait_ge(SC, C.get(f"sc_t_{j}_{f2}"))
                        cur, nxt = accP(f2, 0), accP(f2, 1)
                        last = None
                        for k, (c, beta, alpha) in enumerate(PAIRS):
                            if k > 0:
                                vector.drain()
                            last = vector._custom_dve(
                                PAIR_OP, out=nxt, in0=ttP, in1=cur,
                                s0=float(c), s1=float(beta), imm2=float(alpha))
                            cur, nxt = nxt, cur
                            if k == 0 and have3:
                                vector._custom_dve(QF1_OP, out=s3P,
                                                   in0=accP(f3, 0), in1=k2P,
                                                   s0=8.0, imm2=4.0)
                            elif k == 1 and have3:
                                vector._custom_dve(QF2_OP, out=xq2P, in0=s3P,
                                                   in1=accP(f3, 0),
                                                   s0=2.0, s1=1.0)\
                                      .then_inc(VE, 1)
                                C.v[f"ve_q_{j}_{f3}"] = C.v[f"ve_q_{j}_{f3+1}"] \
                                    = C.add("ve")
                            elif k == 5 and have1:
                                vector.wait_ge(SC, C.get(f"sc_w_{j}_{g}"))
                                vector._custom_dve(QF1_OP, out=s3P, in0=wqP,
                                                   in1=kkP, s0=8.0, imm2=4.0)
                            elif k == 6 and have1:
                                vector._custom_dve(QF2_OP, out=XqP(g), in0=s3P,
                                                   in1=wqP, s0=2.0, s1=1.0)\
                                      .then_inc(VE, 1)
                                C.v[f"ve_X_{j}_{g}"] = C.v[f"ve_X_{j}_{g+1}"] \
                                    = C.add("ve")
                        last.then_inc(VE, 1)
                        C.v[f"ve_p_{j}_{f2}"] = C.v[f"ve_p_{j}_{f2+1}"] = C.add("ve")

                @block.scalar
                def _(scalar, j=j, g=g):
                    f3 = g - 4
                    if 0 <= f3 < NFM:
                        scalar.wait_ge(VE, C.get(f"ve_q_{j}_{f3}"))
                        if f3 == 0 and C.get(f"pe_w2_{j-1}_7"):
                            scalar.wait_ge(PE, C.get(f"pe_w2_{j-1}_7"))
                        scalar.activation(out=y_all[:, f3 * TB:(f3 + 2) * TB],
                                          in_=xq2P, func=AF.Identity,
                                          scale=1.0 / 256, bias=consts[:, 6:7])\
                              .then_inc(SC, 1)
                        C.v[f"sc_y_{j}_{f3}"] = C.v[f"sc_y_{j}_{f3+1}"] = C.add("sc")


            for o in range(8):
                @block.tensor
                def _(tensor, j=j, o=o):
                    if o == 0:
                        tensor.wait_ge(SC, C.get(f"sc_y_{j}_{NFM-1}"))
                        if C.get(f"sc_w4_{j-1}_7"):
                            tensor.wait_ge(SC, C.get(f"sc_w4_{j-1}_7"))
                        if j == 0:
                            tensor.wait_ge(VE, C.get(f"ve_epi_A{NCA-1}"))
                            tensor.wait_ge(GP, gp_wB)
                    last = None
                    for f3 in range(NFM):
                        last = tensor.matmul(
                            out=pool[:, 1024 + o * TB: 1024 + (o + 1) * TB],
                            lhsT=w2_kt(f3)[:, o * 128:(o + 1) * 128],
                            rhs=y_all[:, f3 * TB:(f3 + 1) * TB],
                            start=(f3 == 0), stop=(f3 == NFM - 1))
                    last.then_inc(PE, 1)
                    C.v[f"pe_w2_{j}_{o}"] = C.add("pe")

            for op in range(0, 8, 2):
                @block.scalar
                def _(scalar, j=j, op=op):
                    scalar.wait_ge(PE, C.get(f"pe_w2_{j}_{op+1}"))
                    if op >= 2:
                        scalar.wait_ge(VE, C.get(f"ve_fq_{j}_{op-2}"))
                    else:
                        scalar.wait_ge(VE, C.get(f"ve_X_{j}_{NFM-1}"))
                    for o in (op, op + 1):
                        scalar.activation(out=wq_s(o),
                                          in_=pool[:, 1024 + o * TB:
                                                   1024 + (o + 1) * TB],
                                          func=AF.Identity, scale=CQ,
                                          bias=c2_sb[:, o:o + 1])
                    scalar.drain()
                    scalar.copy(out=kkP, in_=wqP).then_inc(SC, 1)
                    C.v[f"sc_w4_{j}_{op}"] = C.v[f"sc_w4_{j}_{op+1}"] = C.add("sc")

                @block.vector
                def _(vector, j=j, op=op):
                    vector.wait_ge(SC, C.get(f"sc_w4_{j}_{op}"))
                    if op == 0:
                        vector.wait_ge(SC, C.get(f"sc_y_{j}_{NFM-1}"))
                    else:
                        vector.wait_ge(SC, C.get(f"sc_f_{j}_{op-2}"))
                    vector.drain()
                    vector._custom_dve(QF1_OP, out=s3P, in0=wqP,
                                       in1=kkP, s0=8.0, imm2=4.0)
                    vector.drain()
                    vector._custom_dve(QF2_OP, out=xq2P, in0=s3P,
                                       in1=wqP, s0=2.0, s1=1.0)\
                          .then_inc(VE, 1)
                    C.v[f"ve_fq_{j}_{op}"] = C.v[f"ve_fq_{j}_{op+1}"] = C.add("ve")

                @block.scalar
                def _(scalar, j=j, op=op):
                    scalar.wait_ge(VE, C.get(f"ve_fq_{j}_{op}"))
                    if op == 0 and j == 0:
                        scalar.wait_ge(GP, C.get("gp_wB2_39"))
                    if op == 0 and C.get(f"pe_finB_{j-1}_1_1"):
                        scalar.wait_ge(PE, C.get(f"pe_finB_{j-1}_1_1"))
                    scalar.activation(out=f_all[:, op * TB:(op + 2) * TB],
                                      in_=xq2P, func=AF.Identity,
                                      scale=1.0 / 256, bias=consts[:, 7:8])\
                          .then_inc(SC, 1)
                    C.v[f"sc_f_{j}_{op}"] = C.v[f"sc_f_{j}_{op+1}"] = C.add("sc")

            for tt in range(2):
                a_chunk = 2 * j + tt
                @block.sync
                def _(sync, j=j, tt=tt, a_chunk=a_chunk):
                    sync.wait_ge(STA[a_chunk % 2], C.get(f"st_A{a_chunk}"))
                    if a_chunk == 0:
                        sync.wait_ge(GP, gp_wB)
                    if C.get(f"ve_outh_{2*j+tt-1}_done"):
                        sync.wait_ge(VE, C.get(f"ve_outh_{2*j+tt-1}_done"))
                    sync.dma_start(out=stage[:],
                                   in_=out_d[j * TB + tt * 128:
                                             j * TB + (tt + 1) * 128, :])\
                        .then_inc(LDO, 16)
                    C.v[f"ldo_{a_chunk}"] = C.add("ldo", 16)

                for half in range(2):
                    @block.tensor
                    def _(tensor, j=j, tt=tt, half=half):
                        tensor.wait_ge(SC, C.get(f"sc_f_{j}_7"))
                        gk = (f"ve_outh_{j}_{tt}_{half-1}" if half == 1 else
                              (f"ve_outh_{j}_{tt-1}_1" if tt == 1 else
                               f"ve_outh_{j-1}_1_1"))
                        if C.get(gk):
                            tensor.wait_ge(VE, C.get(gk))
                        last = None
                        for oo in range(4):
                            o = half * 4 + oo
                            last = tensor.transpose(
                                out=pool[:, 3072 + oo * 128: 3072 + (oo + 1) * 128],
                                in_=f_all[:, o * TB + tt * 128:
                                          o * TB + tt * 128 + 128],
                                identity=idf[:])
                        last.then_inc(PE, 1)
                        C.v[f"pe_finB_{j}_{tt}_{half}"] = C.add("pe")

                    @block.vector
                    def _(vector, j=j, tt=tt, half=half, a_chunk=a_chunk):
                        vector.wait_ge(PE, C.get(f"pe_finB_{j}_{tt}_{half}"))
                        sl = (2 * j + tt) % 2
                        if half == 0:
                            vector.wait_ge(LDO, C.get(f"ldo_{a_chunk}"))
                        if half == 0 and C.get(f"st_B{2*j+tt-2}"):
                            vector.wait_ge(STB[sl], C.get(f"st_B{2*j+tt-2}"))
                        elif half == 0:
                            vector.wait_ge(STA[sl], C.get(f"st_A{14 + sl}"))
                        vector.tensor_tensor(
                            out=out_sb[:, sl * D + half * 512: sl * D + (half + 1) * 512],
                            in0=pool[:, 3072:3584],
                            in1=stage[:, half * 512:(half + 1) * 512],
                            op=AL.add).then_inc(VE, 1)
                        C.v[f"ve_outh_{j}_{tt}_{half}"] = C.add("ve")
                        if half == 1:
                            C.v[f"ve_outh_{a_chunk}_done"] = C.get("ve")

                @block.sync
                def _(sync, j=j, tt=tt, a_chunk=a_chunk):
                    sync.wait_ge(VE, C.get(f"ve_outh_{j}_{tt}_1"))
                    sl = (2 * j + tt) % 2
                    sync.dma_start(out=out_d[j * TB + tt * 128:
                                             j * TB + (tt + 1) * 128, :],
                                   in_=out_sb[:, sl * D:(sl + 1) * D])\
                        .then_inc(STB[sl], 16)
                    C.v[f"st_B{a_chunk}"] = C.add(f"stb{sl}", 16)

        @block.vector
        def _(vector):
            vector.wait_ge(STA[0], C.get("sta0"))
            vector.wait_ge(STA[1], C.get("sta1"))
            vector.wait_ge(STB[0], C.get("stb0"))
            vector.wait_ge(STB[1], C.get("stb1"))

        block.emit(real_block)

    return nc


_NC_CACHE = None


def kernel(L0_attn_x, L0_attn_w_in, L0_attn_w_conv, L0_attn_b_conv,
           L0_attn_w_sel, L0_attn_w_dt, L0_attn_b_dt, L0_attn_w_out,
           L0_attn_A, L0_attn_Dp, L0_ffn_w1, L0_ffn_b1, L0_ffn_w2, L0_ffn_b2):
    global _NC_CACHE
    x = np.ascontiguousarray(np.asarray(L0_attn_x, dtype=np.float32))
    w_in = np.ascontiguousarray(np.asarray(L0_attn_w_in, np.float32))
    w_conv = np.ascontiguousarray(
        np.asarray(L0_attn_w_conv, np.float32).reshape(E, 4)
        .reshape(16, 128, 4).transpose(1, 0, 2).reshape(128, 64))
    b_conv = np.ascontiguousarray(
        np.asarray(L0_attn_b_conv, np.float32).reshape(16, 128).T)
    w_sel = np.ascontiguousarray(np.asarray(L0_attn_w_sel, np.float32))
    w_dtb = np.zeros((128, E), np.float32)
    w_dtb[0:64, :] = np.asarray(L0_attn_w_dt, np.float32)
    w_dtb[64, :] = np.asarray(L0_attn_b_dt, np.float32)
    w_out = np.ascontiguousarray(np.asarray(L0_attn_w_out, np.float32))
    w1 = np.ascontiguousarray(np.asarray(L0_ffn_w1, np.float32))
    b1 = np.ascontiguousarray(np.asarray(L0_ffn_b1, np.float32).reshape(32, 128).T)
    w2 = np.ascontiguousarray(np.asarray(L0_ffn_w2, np.float32))
    b2 = np.ascontiguousarray(np.asarray(L0_ffn_b2, np.float32).reshape(8, 128).T)
    idb = np.eye(128, dtype=ml_dtypes.bfloat16)
    idf = np.eye(128, dtype=np.float32)

    if _NC_CACHE is None:
        nc = build_nc()
        mybir.codegen_inst_isa_subclasses(nc)
        _NC_CACHE = nc
    nc = _NC_CACHE

    in_maps = []
    for c in range(8):
        in_maps.append({
            "x": np.ascontiguousarray(x[c]),
            "w_in": w_in, "w_conv": w_conv, "b_conv": b_conv,
            "w_sel": w_sel, "w_dtb": w_dtb, "w_out": w_out,
            "w1": w1, "b1": b1, "w2": w2, "b2": b2,
            "idb": idb, "idf": idf,
        })
    res = run_bass_kernel_spmd(nc, in_maps, core_ids=list(range(8)))
    out = np.stack([np.asarray(r["out"], dtype=np.float32)
                    for r in res.results], axis=0)
    return out


# version 28
# speedup vs baseline: 1.0233x; 1.0233x over previous
import math
import contextlib
import numpy as np
import ml_dtypes

import concourse.bass as bass
import concourse.mybir as mybir
from concourse.bass_utils import run_bass_kernel_spmd
from concourse.dve_op import DveOp
from concourse.dve_spec import (
    Spec, Src0, Src1, C0, C1, C2, Zero, One, maxx, minn, lower as dve_lower,
)
import concourse.dve_ops as dve_ops_mod
from concourse.dve_ops import OPS as DVE_OPS

F32 = mybir.dt.float32
BF16 = mybir.dt.bfloat16
I32 = mybir.dt.int32
AL = mybir.AluOpType
AF = mybir.ActivationFunctionType


def _register_op(name, spec, reference, replace=None):
    if name in dve_ops_mod._SUB_OPCODE_FOR_NAME:
        for o in DVE_OPS:
            if o.name == name:
                return o
        raise RuntimeError(f"{name}: row registered but op missing")
    shas = {}
    for ver in ("v3", "v4"):
        shas[ver] = dve_lower(spec, ver=ver, name=name).sha(ver)
    op = DveOp(name, spec, subdim=False, uops_sha=shas, reference=reference)
    if replace is not None:
        idx = next(i for i, o in enumerate(DVE_OPS) if o.name == replace)
        DVE_OPS[idx] = op
        row = dve_ops_mod._CUSTOM_DVE_ROW_BASE + idx
    else:
        DVE_OPS.append(op)
        row = dve_ops_mod._CUSTOM_DVE_ROW_BASE + len(DVE_OPS) - 1
    assert row < 0x20
    dve_ops_mod.CUSTOM_DVE_SPECS[name] = spec
    dve_ops_mod.CUSTOM_DVE_REFS[name] = reference
    dve_ops_mod._SUB_OPCODE_FOR_NAME[name] = row
    return op


_w = maxx(Src0, C0)
PAIR_OP = _register_op(
    "MK_PWLPAIR",
    Spec(body=(_w * C1 + Src1) + minn(_w, C0 + One) * C2),
    lambda in0, in1, s0, s1, imm2: (
        np.maximum(in0, s0) * s1 + in1
        + np.minimum(np.maximum(in0, s0), s0 + 1.0) * imm2),
)

_r = (Src0 - Src1) + C0
_g3 = _r >= C0
_s2 = _r - _g3 * C0
_g2 = _s2 >= C2
QF1_OP = _register_op(
    "MK_QF1",
    Spec(body=_s2 - _g2 * C2),
    lambda in0, in1, s0, s1, imm2: (
        lambda r: (lambda s2: s2 - (s2 >= imm2) * imm2)(r - (r >= s0) * s0)
    )((in0 - in1) + s0),
    replace="DVEHASH32L_SEED",
)

_gg1 = Src0 >= C0
_ss4 = Src0 - _gg1 * C0
_gg0 = _ss4 >= C1
QF2_OP = _register_op(
    "MK_QF2",
    Spec(body=Src1 - (_ss4 - _gg0)),
    lambda in0, in1, s0, s1, imm2: (
        lambda s4: in1 - (s4 - (s4 >= s1))
    )(in0 - (in0 >= s0) * s0),
    replace="DVEHASH32L_BODY",
)

FLOOR_OP = _register_op(
    "MK_FLOOR",
    Spec(body=((Src0 >= Src1) + Src1) - One),
    lambda in0, in1, s0, s1, imm2: ((in0 >= in1) + in1) - 1.0,
)


def _pwl_consts():
    SQ2 = math.sqrt(2.0)
    U = [256.0 * (j * 0.25) * 0.5 * math.erf((j * 0.25) / SQ2) for j in range(17)]
    sig = [U[j + 1] - U[j] for j in range(16)]
    pairs, kconst = [], 0.0
    for k in range(8):
        beta = sig[2 * k + 1] - (sig[2 * k - 1] if k > 0 else 0.0)
        alpha = sig[2 * k] - sig[2 * k + 1]
        c = 2.0 * k
        pairs.append((c, float(beta), float(alpha)))
        kconst += c * (beta + alpha)
    b0 = -4096.0 - kconst + 64.5
    return pairs, float(b0)


PAIRS, B0 = _pwl_consts()

D = 1024
E = 2048
FF = 4096
LTOK = 2048
TA = 128
NCA = LTOK // TA
TB = 256
NCB = LTOK // TB
NFM = FF // 128
DVE_CONV_ROWS = 16


class Ctr:

    def __init__(self):
        self.v = {}
        self.logs = {}
        self._rec = []
        self._rep = None

    def add(self, name, amt=1):
        if self._rep is not None:
            r = self._rep[0][self._rep[1]]
            self._rep[1] += 1
            return r
        self.v[name] = self.v.get(name, 0) + amt
        r = self.v[name]
        if self._rec:
            self._rec[-1].append(r)
        return r

    def get(self, name):
        if self._rep is not None:
            r = self._rep[0][self._rep[1]]
            self._rep[1] += 1
            return r
        r = self.v.get(name, 0)
        if self._rec:
            self._rec[-1].append(r)
        return r

    def begin_record(self, idx):
        lst = []
        self.logs[idx] = lst
        self._rec.append(lst)

    def end_record(self):
        self._rec.pop()

    def begin_replay(self, idx):
        self._rep = [self.logs[idx], 0]

    def end_replay(self, idx):
        assert self._rep[1] == len(self.logs[idx]), \
            f"replay divergence in body {idx}"
        self._rep = None


class _DummyRes:
    def then_inc(self, *a, **k):
        return self


class _DummyEng:
    def __getattr__(self, n):
        def f(*a, **k):
            return _DummyRes()
        return f


class _SecBlock:

    def __init__(self, C):
        self.C = C
        self.bodies = []

    def _deco(self, eng_name):
        def deco(fn):
            idx = len(self.bodies)
            self.bodies.append((eng_name, fn))
            self.C.begin_record(idx)
            fn(_DummyEng())
            self.C.end_record()
        return deco

    @property
    def sync(self):
        return self._deco("sync")

    @property
    def tensor(self):
        return self._deco("tensor")

    @property
    def scalar(self):
        return self._deco("scalar")

    @property
    def vector(self):
        return self._deco("vector")

    @property
    def gpsimd(self):
        return self._deco("gpsimd")

    def emit(self, real_block):
        for eng_name in ["sync", "tensor", "scalar", "vector", "gpsimd"]:
            todo = [(i, f) for i, (e, f) in enumerate(self.bodies)
                    if e == eng_name]
            if not todo:
                continue

            def run(engine, todo=todo):
                for i, f in todo:
                    self.C.begin_replay(i)
                    f(engine)
                    self.C.end_replay(i)

            getattr(real_block, eng_name)(run)


def build_nc():
    nc = bass.Bass("TRN2", target_bir_lowering=False,
                   dynamic_dma_scratch_size=2048)
    dtt = lambda n, s, t, k: nc.dram_tensor(n, s, t, kind=k).ap()
    x_d = dtt("x", [LTOK, D], F32, "ExternalInput")
    w_in_d = dtt("w_in", [D, 2 * E], F32, "ExternalInput")
    w_conv_d = dtt("w_conv", [128, 64], F32, "ExternalInput")
    b_conv_d = dtt("b_conv", [128, 16], F32, "ExternalInput")
    w_sel_d = dtt("w_sel", [E, 96], F32, "ExternalInput")
    w_dtb_d = dtt("w_dtb", [128, E], F32, "ExternalInput")
    w_out_d = dtt("w_out", [E, D], F32, "ExternalInput")
    w1_d = dtt("w1", [D, FF], F32, "ExternalInput")
    b1_d = dtt("b1", [128, 32], F32, "ExternalInput")
    w2_d = dtt("w2", [FF, D], F32, "ExternalInput")
    b2_d = dtt("b2", [128, 8], F32, "ExternalInput")
    idb_d = dtt("idb", [128, 128], BF16, "ExternalInput")
    idf_d = dtt("idf", [128, 128], F32, "ExternalInput")
    out_d = dtt("out", [LTOK, D], F32, "ExternalOutput")

    C = Ctr()
    es = contextlib.ExitStack()
    with es:
        def sb(name, shape, t=F32):
            return es.enter_context(nc.sbuf_tensor(name, shape, t))

        wbigA = sb("wbigA", [128, 8 * (2 * E)], BF16)
        wbigB = sb("wbigB", [128, 32 * D], BF16)
        xn_sb = sb("xn_sb", [128, 2 * D], BF16)
        xnT_sb = sb("xnT_sb", [128, 2 * D], BF16)
        siluz = sb("siluz", [128, E], BF16)
        xa_p = sb("xa_p", [128, 16 * 128], BF16)
        bcd_rhs = sb("bcd_rhs", [128, 128], BF16)
        y_all = sb("y_all", [128, NFM * TB], BF16)
        idb = sb("idb_sb", [128, 128], BF16)

        xs = sb("xs", [128, 2 * D], F32)
        xc = sb("xc", [128, 16 * 131], F32)
        wq_p = sb("wq_p", [128, 2 * TB], F32)
        Xq_p = sb("Xq_p", [128, 4 * TB], F32)
        tt_p = sb("tt_p", [128, 2 * TB], F32)
        acc_p = sb("acc_p", [128, 8 * TB], F32)
        xq2_t = sb("xq2_t", [128, 2 * TB], F32)
        conv_t = sb("conv_t", [128, 4 * 128], F32)
        s3_t = conv_t
        out_sb = sb("out_sb", [128, 2 * D], F32)
        f_all = sb("f_all", [128, 8 * TB], F32)
        attn_fm = f_all
        stage = sb("stage", [128, 1024], F32)
        idf = sb("idf_sb", [128, 128], F32)
        smalls = sb("smalls", [128, 16], F32)
        kk_p = es.enter_context(nc.sbuf_tensor("kk_p", [128, 2 * TB], I32))
        k2_p = es.enter_context(nc.sbuf_tensor("k2_p", [128, 2 * TB], I32))
        w_conv_sb = sb("w_conv_sb", [128, 64], F32)
        b_conv_sb = sb("b_conv_sb", [128, 16], F32)
        c1_sb = sb("c1_sb", [128, 32], F32)
        c2_sb = sb("c2_sb", [128, 8], F32)
        consts = sb("consts", [128, 8], F32)

        pool = es.enter_context(nc.psum_tensor("pool", [128, 3584], F32))
        ps_tr = es.enter_context(nc.psum_tensor("ps_tr", [128, 1024], BF16))

        sems = {}
        for sname in ["ldsm", "ldx0", "ldx1", "lds0", "lds1", "ldo",
                      "pe", "sc", "ve", "gp", "sta0", "sta1", "stb0", "stb1",
                      "lds2", "lds3", "lds4", "lds5", "lds6", "lds7",
                      "scc"]:
            sems[sname] = es.enter_context(nc.semaphore(sname))
        real_block = es.enter_context(nc.Block())
        block = _SecBlock(C)
        LDSM = sems["ldsm"]
        LDX = [sems["ldx0"], sems["ldx1"]]
        LDS = [sems[f"lds{k}"] for k in range(8)]
        PE, SC, VE, GP = sems["pe"], sems["sc"], sems["ve"], sems["gp"]
        SCC = sems["scc"]
        STA = [sems["sta0"], sems["sta1"]]
        STB = [sems["stb0"], sems["stb1"]]
        LDO = sems["ldo"]

        def w_in_kt(k):
            return wbigA[:, k * (2 * E):(k + 1) * (2 * E)]

        w1_kt = w_in_kt

        def w_out_kt(k):
            return wbigB[:, k * D:(k + 1) * D]

        w2_kt = w_out_kt

        def w_sel_kt(k):
            return wbigB[:, 16 * D + k * 96: 16 * D + (k + 1) * 96]

        dtb_off = 16 * D + 16 * 96

        def dtb_ap():
            return wbigB[:, dtb_off: dtb_off + E]

        @block.sync
        def _(sync):
            for (dst, src) in [(idb[:], idb_d[:]), (idf[:], idf_d[:]),
                               (w_conv_sb[:], w_conv_d[:]),
                               (b_conv_sb[:], b_conv_d[:]),
                               (out_sb[:, 0:32], b1_d[:]),
                               (out_sb[:, 32:40], b2_d[:])]:
                sync.dma_start(out=dst, in_=src).then_inc(LDSM, 16)
                C.add("ldsm", 16)
        ld_small = C.get("ldsm")

        @block.vector
        def _(vector):
            for col, val in [(0, 256.0), (1, 1.0 / 64), (2, -128.0),
                             (3, 0.5), (4, B0), (5, 1e-5),
                             (6, -0.25), (7, -32.0)]:
                vector.memset(consts[:, col:col + 1], float(val))
            vector.memset(bcd_rhs[64:65, :], 1.0)
            for r in range(16):
                vector.memset(xc[:, r * 131: r * 131 + 3], 0.0)
            vector.wait_ge(LDSM, ld_small)
            vector.tensor_scalar(out=c1_sb[:], in0=out_sb[:, 0:32], scalar1=256.0,
                                 scalar2=8192.5, op0=AL.mult, op1=AL.add)
            vector.tensor_scalar(out=c2_sb[:], in0=out_sb[:, 32:40], scalar1=256.0,
                                 scalar2=8192.5, op0=AL.mult, op1=AL.add)\
                  .then_inc(VE, 1)
            C.add("ve")
        ve_consts = C.get("ve")

        @block.scalar
        def _(scalar):
            scalar.wait_ge(VE, ve_consts)
            scalar.wait_ge(LDSM, ld_small)

        @block.tensor
        def _(tensor):
            tensor.wait_ge(VE, ve_consts)
            tensor.wait_ge(LDSM, ld_small)

        @block.gpsimd
        def _(g):
            g.wait_ge(VE, ve_consts)
            g.wait_ge(LDSM, ld_small)

        for i0 in range(2):
            @block.sync
            def _(sync, i=i0):
                sl = i % 2
                sync.dma_start(out=xs[:, sl * D:(sl + 1) * D],
                               in_=x_d[i * TA:(i + 1) * TA, :])\
                    .then_inc(LDX[sl], 16)
                C.v[f"ldx_A{i}"] = C.add(f"ldx{sl}", 16)
                C.v[f"ldxsl_A{i}"] = sl

        def slot_wA(idx, cols):
            s = idx % 8
            if s < 2:
                return stage[:, s * 512: s * 512 + cols]
            if s < 6:
                return acc_p[:, (s - 2) * 512: (s - 2) * 512 + cols]
            return Xq_p[:, (s - 6) * 512: (s - 6) * 512 + cols]

        def slot_w1(idx, cols):
            s = idx % 6
            if s < 2:
                return stage[:, s * 512: s * 512 + cols]
            return xc[:, (s - 2) * 512: (s - 2) * 512 + cols]

        def slot_w2(idx, cols):
            s = idx % 6
            if s < 4:
                return f_all[:, s * 512: s * 512 + cols]
            return stage[:, (s - 4) * 512: (s - 4) * 512 + cols]

        def stage_dmas(pieces, tag, gp_base, slot_fn, nslot, extra=None,
                       split_scc_base=None):
            for idx, (src_ap, dst_ap, cols, pe_gate) in enumerate(pieces):
                @block.sync
                def _(sync, src_ap=src_ap, idx=idx, cols=cols, tag=tag,
                      gp_base=gp_base, slot_fn=slot_fn, nslot=nslot,
                      extra=extra, split_scc_base=split_scc_base):
                    s = idx % nslot
                    prev = C.get(f"lds{s}")
                    if prev > 0:
                        sync.wait_ge(LDS[s], prev)
                    if idx >= nslot:
                        k = idx - nslot
                        if split_scc_base is None:
                            sync.wait_ge(GP, gp_base + k + 1)
                        elif k % 2 == 0:
                            sync.wait_ge(GP, gp_base + k // 2 + 1)
                        else:
                            sync.wait_ge(SCC, split_scc_base + (k + 1) // 2)
                    if extra is not None and idx in extra:
                        for sem, val in extra[idx]:
                            sync.wait_ge(sem, val)
                    sync.dma_start(out=slot_fn(idx, cols), in_=src_ap)\
                        .then_inc(LDS[s], 16)
                    C.v[f"ld_{tag}_{idx}"] = C.add(f"lds{s}", 16)

        def stage_casts(pieces, tag, slot_fn, nslot, gates=None, split=False):
            @block.gpsimd
            def _(g, tag=tag, slot_fn=slot_fn, nslot=nslot):
                for idx, (src_ap, dst_ap, cols, pe_gate) in enumerate(pieces):
                    if split and idx % 2 == 1:
                        continue
                    if gates is not None:
                        pe_gate = gates[idx]
                    g.wait_ge(LDS[idx % nslot], C.get(f"ld_{tag}_{idx}"))
                    if pe_gate:
                        g.wait_ge(PE, pe_gate)
                    g.tensor_copy(dst_ap, slot_fn(idx, cols)).then_inc(GP, 1)
                    C.v[f"gp_{tag}_{idx}"] = C.add("gp")
            if not split:
                return

            @block.scalar
            def _(scalar, tag=tag, slot_fn=slot_fn, nslot=nslot):
                for idx, (src_ap, dst_ap, cols, pe_gate) in enumerate(pieces):
                    if idx % 2 == 0:
                        continue
                    if gates is not None:
                        pe_gate = gates[idx]
                    scalar.wait_ge(LDS[idx % nslot], C.get(f"ld_{tag}_{idx}"))
                    if pe_gate:
                        scalar.wait_ge(PE, pe_gate)
                    scalar.copy(out=dst_ap, in_=slot_fn(idx, cols))\
                          .then_inc(SCC, 1)
                    C.v[f"scc_{tag}_{idx}"] = C.add("scc")

        def stage_pieces(pieces, tag):
            stage_dmas(pieces, tag, C.get("gp"), slot_wA, 8)
            stage_casts(pieces, tag, slot_wA, 8)

        piecesA = []
        for q in range(8):
            for k in range(8):
                piecesA.append((w_in_d[k * 128:(k + 1) * 128, q * 512:(q + 1) * 512],
                                w_in_kt(k)[:, q * 512:(q + 1) * 512], 512, 0))
        nw_in = len(piecesA)
        for k in range(16):
            for q in range(2):
                piecesA.append((w_out_d[k * 128:(k + 1) * 128, q * 512:(q + 1) * 512],
                                w_out_kt(k)[:, q * 512:(q + 1) * 512], 512, 0))
        for k in range(16):
            piecesA.append((w_sel_d[k * 128:(k + 1) * 128, :], w_sel_kt(k), 96, 0))
        for q in range(4):
            piecesA.append((w_dtb_d[:, q * 512:(q + 1) * 512],
                            dtb_ap()[:, q * 512:(q + 1) * 512], 512, 0))
        n_outsel = len(piecesA)
        for k in range(20, 32):
            for q in range(2):
                piecesA.append((w2_d[k * 128:(k + 1) * 128, q * 512:(q + 1) * 512],
                                w2_kt(k)[:, q * 512:(q + 1) * 512], 512, 0))
        stage_pieces(piecesA, "wA")
        gp_w_in_q = [C.get(f"gp_wA_{q * 8 + 7}") for q in range(8)]
        gp_w_in = C.get(f"gp_wA_{nw_in - 1}")
        gp_w_outsel = C.get(f"gp_wA_{n_outsel - 1}")

        piecesB = []
        for q in range(8):
            for k in range(8):
                piecesB.append((w1_d[k * 128:(k + 1) * 128, q * 512:(q + 1) * 512],
                                w1_kt(k)[:, q * 512:(q + 1) * 512], 512, 0))
        for k in range(20):
            for q in range(2):
                piecesB.append((w2_d[k * 128:(k + 1) * 128, q * 512:(q + 1) * 512],
                                w2_kt(k)[:, q * 512:(q + 1) * 512], 512, 0))

        def ln_chunk(ptile_ap, xn_dst_ap, tag, prev_var_tag, xn_guard):
            mu = smalls[:, 0:1]; var = smalls[:, 1:2]
            sd = smalls[:, 7:8]; rstd = smalls[:, 6:7]

            @block.vector
            def _(vector):
                vector.wait_ge(LDX[C.get(f"ldxsl_{tag}")], C.get(f"ldx_{tag}"))
                if C.get(f"sc_rstd_{prev_var_tag}"):
                    vector.wait_ge(SC, C.get(f"sc_rstd_{prev_var_tag}"))
                vector.drain()
                vector.bn_stats(out=smalls[:, 4:10], in_=ptile_ap[:, 0:512])
                vector.bn_stats(out=smalls[:, 10:16], in_=ptile_ap[:, 512:1024])
                vector.drain()
                vector.bn_aggr(out=smalls[:, 0:2], in_=smalls[:, 4:16])\
                      .then_inc(VE, 1)
                C.v[f"ve_var_{tag}"] = C.add("ve")

            @block.scalar
            def _(scalar):
                scalar.wait_ge(VE, C.get(f"ve_var_{tag}"))
                scalar.activation(out=sd, in_=var, func=AF.Sqrt,
                                  bias=consts[:, 5:6]).then_inc(SC, 1)
                C.v[f"sc_rstd_{tag}"] = C.add("sc")

            @block.vector
            def _(vector):
                vector.wait_ge(SC, C.get(f"sc_rstd_{tag}"))
                vector.reciprocal(rstd, sd)
                vector.drain()
                if xn_guard:
                    vector.wait_ge(PE, xn_guard)
                vector.tensor_scalar(out=xn_dst_ap, in0=ptile_ap, scalar1=mu,
                                     scalar2=rstd, op0=AL.subtract, op1=AL.mult)\
                      .then_inc(VE, 1)
                C.v[f"ve_xn_{tag}"] = C.add("ve")

        def emit_transposes2(src_aps, dst_aps, wait_ve, tagp, contig_dst=None):
            n = len(src_aps)

            @block.tensor
            def _(tensor):
                tensor.wait_ge(VE, wait_ve)
                if C.get("last_trev"):
                    tensor.wait_ge(SC, C.get("last_trev"))
                last = None
                for d in range(n):
                    last = tensor.transpose(out=ps_tr[:, d * 128:(d + 1) * 128],
                                            in_=src_aps[d], identity=idb[:])
                last.then_inc(PE, 1)
                pe_val = C.add("pe")
                for d in range(n):
                    C.v[f"{tagp}_{d}"] = pe_val

            @block.scalar
            def _(scalar):
                scalar.wait_ge(PE, C.get(f"{tagp}_{n-1}"))
                if contig_dst is not None:
                    scalar.activation(out=contig_dst, in_=ps_tr[:, 0:n * 128],
                                      func=AF.Copy).then_inc(SC, 1)
                else:
                    last = None
                    for d in range(n):
                        last = scalar.activation(
                            out=dst_aps[d], in_=ps_tr[:, d * 128:(d + 1) * 128],
                            func=AF.Copy)
                    last.then_inc(SC, 1)
                C.v["last_trev"] = C.add("sc")
            C.v[f"{tagp}_last"] = C.get("last_trev")

        def emit_x_load(i):
            sl = i % 2
            tag = f"A{i}"

            @block.sync
            def _(sync, i=i, sl=sl, tag=tag):
                sync.wait_ge(VE, C.get(f"ve_epi_A{i-2}"))
                sync.dma_start(out=xs[:, sl * D:(sl + 1) * D],
                               in_=x_d[i * TA:(i + 1) * TA, :])\
                    .then_inc(LDX[sl], 16)
                C.v[f"ldx_{tag}"] = C.add(f"ldx{sl}", 16)
                C.v[f"ldxsl_{tag}"] = sl

        def emit_ln_A(i):
            sl = i % 2
            tag = f"A{i}"
            xn_guard = C.get(f"trA{i-2}_last_pe") if i >= 2 else 0
            ln_chunk(xs[:, sl * D:(sl + 1) * D], xn_sb[:, sl * D:(sl + 1) * D],
                     tag, f"A{i-1}", xn_guard)

        def emit_tr_A(i):
            sl = i % 2
            tag = f"A{i}"
            srcs = [xn_sb[:, sl * D + d * 128: sl * D + (d + 1) * 128]
                    for d in range(8)]
            dsts = [xnT_sb[:, sl * 1024 + d * 128: sl * 1024 + (d + 1) * 128]
                    for d in range(8)]
            emit_transposes2(srcs, dsts, C.get(f"ve_xn_{tag}"), f"trA{i}",
                             contig_dst=xnT_sb[:, sl * 1024:(sl + 1) * 1024])
            C.v[f"trA{i}_last_pe"] = C.get("pe")

        def emit_waves(i, wvs):
            sl = i % 2
            for wv in wvs:
                @block.tensor
                def _(tensor, i=i, sl=sl, wv=wv):
                    if wv == 0:
                        tensor.wait_ge(SC, C.get(f"trA{i}_last"))
                    if i == 0:
                        tensor.wait_ge(GP, gp_w_in_q[wv])
                    gk = (f"sc_wave_{i}_{wv-2}" if wv >= 2
                          else f"sc_wave_{i-1}_{wv+6}")
                    if C.get(gk):
                        tensor.wait_ge(SC, C.get(gk))
                    last = None
                    for mq in range(4):
                        m = wv * 4 + mq
                        for kt in range(8):
                            last = tensor.matmul(
                                out=pool[:, (wv % 2) * 512 + mq * 128:
                                         (wv % 2) * 512 + (mq + 1) * 128],
                                lhsT=w_in_kt(kt)[:, m * 128:(m + 1) * 128],
                                rhs=xnT_sb[:, sl * 1024 + kt * 128:
                                           sl * 1024 + (kt + 1) * 128],
                                start=(kt == 0), stop=(kt == 7))
                    last.then_inc(PE, 1)
                    C.v[f"pe_wave_{i}_{wv}"] = C.add("pe")

                @block.scalar
                def _(scalar, i=i, wv=wv):
                    scalar.wait_ge(PE, C.get(f"pe_wave_{i}_{wv}"))
                    if wv == 4 and C.get(f"pe_attn_done_{i-1}"):
                        scalar.wait_ge(PE, C.get(f"pe_attn_done_{i-1}"))
                    if wv <= 3 and C.get(f"cv_{i-1}_{4*wv+2}"):
                        scalar.wait_ge(VE, C.get(f"cv_{i-1}_{4*wv+2}"))
                    last = None
                    for mq in range(4):
                        m = wv * 4 + mq
                        src = pool[:, (wv % 2) * 512 + mq * 128:
                                   (wv % 2) * 512 + (mq + 1) * 128]
                        if m < 16:
                            last = scalar.activation(
                                out=xc[:, m * 131 + 3: m * 131 + 131],
                                in_=src, func=AF.Copy)
                        else:
                            last = scalar.activation(
                                out=siluz[:, (m - 16) * 128:(m - 15) * 128],
                                in_=src, func=AF.Silu)
                    last.then_inc(SC, 1)
                    C.v[f"sc_wave_{i}_{wv}"] = C.add("sc")

        def emit_conv(i):
            for r in range(0, 16, 2):
                def conv_body(eng, i=i, r=r):
                    eng.wait_ge(SC, C.get(f"sc_wave_{i}_{(r + 1) // 4}"))
                    if r == 0 and C.get(f"pe_fa_{i-1}"):
                        eng.wait_ge(PE, C.get(f"pe_fa_{i-1}"))
                    eng.drain()
                    cts = {}
                    for rr in (r, r + 1):
                        base = rr * 131
                        sa = 2 * (rr % 2)
                        cta = conv_t[:, sa * 128:(sa + 1) * 128]
                        ctb = conv_t[:, (sa + 1) * 128:(sa + 2) * 128]
                        cts[rr] = (cta, ctb)
                        eng.tensor_scalar(out=cta,
                                          in0=xc[:, base + 3: base + 131],
                                          scalar1=w_conv_sb[:, rr * 4 + 3: rr * 4 + 4],
                                          scalar2=b_conv_sb[:, rr:rr + 1],
                                          op0=AL.mult, op1=AL.add)
                        eng.tensor_scalar(out=ctb,
                                          in0=xc[:, base + 2: base + 130],
                                          scalar1=w_conv_sb[:, rr * 4 + 2: rr * 4 + 3],
                                          scalar2=None, op0=AL.mult)
                    eng.drain()
                    for rr in (r, r + 1):
                        base = rr * 131
                        cta, ctb = cts[rr]
                        eng.scalar_tensor_tensor(
                            out=cta, in0=xc[:, base + 1: base + 129],
                            scalar=w_conv_sb[:, rr * 4 + 1: rr * 4 + 2],
                            in1=cta, op0=AL.mult, op1=AL.add)
                        eng.scalar_tensor_tensor(
                            out=ctb, in0=xc[:, base: base + 128],
                            scalar=w_conv_sb[:, rr * 4: rr * 4 + 1],
                            in1=ctb, op0=AL.mult, op1=AL.add)
                    eng.drain()
                    for rr in (r, r + 1):
                        cta, ctb = cts[rr]
                        eng.tensor_tensor(out=f_all[:, rr * 128:(rr + 1) * 128],
                                          in0=cta, in1=ctb, op=AL.add)
                    last = None
                    for rr in (r, r + 1):
                        last = eng.tensor_copy(xc[:, rr * 131: rr * 131 + 3],
                                               xc[:, rr * 131 + 128: rr * 131 + 131])
                    last.then_inc(VE, 1)
                    C.v[f"cv_{i}_{r}"] = C.v[f"cv_{i}_{r+1}"] = C.add("ve")

                block.vector(conv_body)

                @block.scalar
                def _(scalar, i=i, r=r):
                    scalar.wait_ge(VE, C.get(f"cv_{i}_{r}"))
                    if r == 0 and C.get(f"pe_attn_done_{i-1}"):
                        scalar.wait_ge(PE, C.get(f"pe_attn_done_{i-1}"))
                    last = None
                    for rr in (r, r + 1):
                        last = scalar.activation(
                            out=xa_p[:, rr * 128:(rr + 1) * 128],
                            in_=f_all[:, rr * 128:(rr + 1) * 128],
                            func=AF.Silu)
                    last.then_inc(SC, 1)
                    C.v[f"sc_xa_{i}_{r}"] = C.v[f"sc_xa_{i}_{r+1}"] = C.add("sc")

            @block.vector
            def _(vector, i=i):
                vector.wait_ge(SC, C.get(f"sc_xa_{i}_14"))
                vector.tensor_tensor(out=siluz[:], in0=xa_p[:], in1=siluz[:],
                                     op=AL.mult).then_inc(VE, 1)
                C.v[f"ve_oa_{i}_15"] = C.add("ve")

        def emit_attn(i):
            @block.tensor
            def _(tensor, i=i):
                tensor.wait_ge(SC, C.get(f"sc_xa_{i}_15"))
                tensor.wait_ge(VE, C.get(f"ve_oa_{i}_15"))
                if i == 0:
                    tensor.wait_ge(GP, gp_w_outsel)
                if C.get(f"ve_atf_{i-1}"):
                    tensor.wait_ge(VE, C.get(f"ve_atf_{i-1}"))
                if C.get(f"ve_bcdev_{i-1}"):
                    tensor.wait_ge(VE, C.get(f"ve_bcdev_{i-1}"))
                if C.get(f"ve_epi_A{i-1}"):
                    tensor.wait_ge(VE, C.get(f"ve_epi_A{i-1}"))
                for r in range(16):
                    tensor.matmul(out=pool[0:32, 2048:2176],
                                  lhsT=w_sel_kt(r)[:, 0:32],
                                  rhs=xa_p[:, r * 128:(r + 1) * 128],
                                  start=(r == 0), stop=(r == 15))
                last = None
                for r in range(16):
                    last = tensor.matmul(out=pool[0:64, 2176:2304],
                                         lhsT=w_sel_kt(r)[:, 32:96],
                                         rhs=xa_p[:, r * 128:(r + 1) * 128],
                                         start=(r == 0), stop=(r == 15))
                last.then_inc(PE, 1)
                C.v[f"pe_bcd_{i}"] = C.add("pe")
                for o in range(8):
                    last = None
                    for r in range(16):
                        last = tensor.matmul(
                            out=pool[:, 1024 + o * 128: 1024 + (o + 1) * 128],
                            lhsT=w_out_kt(r)[:, o * 128:(o + 1) * 128],
                            rhs=siluz[:, r * 128:(r + 1) * 128],
                            start=(r == 0), stop=(r == 15))
                    last.then_inc(PE, 1)
                    C.v[f"pe_attn_{i}_{o}"] = C.add("pe")
                C.v[f"pe_attn_done_{i}"] = C.get("pe")

            @block.vector
            def _(vector, i=i):
                vector.wait_ge(PE, C.get(f"pe_bcd_{i}"))
                vector.tensor_copy(bcd_rhs[0:64, :], pool[0:64, 2176:2304])\
                      .then_inc(VE, 1)
                C.v[f"ve_bcdev_{i}"] = C.add("ve")

            @block.tensor
            def _(tensor, i=i):
                tensor.wait_ge(VE, C.get(f"ve_bcdev_{i}"))
                last = None
                for m in range(16):
                    last = tensor.matmul(out=pool[:, 2304:2432],
                                         lhsT=dtb_ap()[0:65, m * 128:(m + 1) * 128],
                                         rhs=bcd_rhs[0:65, :],
                                         start=True, stop=True)
                last.then_inc(PE, 1)
                C.v[f"pe_delta_{i}"] = C.add("pe")

        def emit_tail(i):
            sl = i % 2
            @block.vector
            def _(vector, i=i):
                last = None
                for o in range(8):
                    vector.wait_ge(PE, C.get(f"pe_attn_{i}_{(o // 4) * 4 + 3}"))
                    last = vector.tensor_copy(attn_fm[:, o * 128:(o + 1) * 128],
                                              pool[:, 1024 + o * 128:
                                                   1024 + (o + 1) * 128])
                last.then_inc(VE, 1)
                C.v[f"ve_atf_{i}"] = C.add("ve")

            @block.tensor
            def _(tensor, i=i):
                tensor.wait_ge(VE, C.get(f"ve_atf_{i}"))
                if C.get(f"ve_epi_A{i-1}"):
                    tensor.wait_ge(VE, C.get(f"ve_epi_A{i-1}"))
                last = None
                for o in range(8):
                    last = tensor.transpose(
                        out=pool[:, 2560 + o * 128: 2560 + (o + 1) * 128],
                        in_=attn_fm[:, o * 128:(o + 1) * 128],
                        identity=idf[:])
                last.then_inc(PE, 1)
                C.v[f"pe_fa_{i}"] = C.add("pe")

            @block.vector
            def _(vector, i=i, sl=sl):
                vector.wait_ge(PE, C.get(f"pe_fa_{i}"))
                if i >= 2:
                    vector.wait_ge(STA[sl], C.get(f"st_A{i-2}"))
                vector.tensor_tensor(out=out_sb[:, sl * D:(sl + 1) * D],
                                     in0=pool[:, 2560:3584],
                                     in1=xs[:, sl * D:(sl + 1) * D],
                                     op=AL.add).then_inc(VE, 1)
                C.v[f"ve_epi_A{i}"] = C.add("ve")

            @block.sync
            def _(sync, i=i, sl=sl):
                sync.wait_ge(VE, C.get(f"ve_epi_A{i}"))
                sync.dma_start(out=out_d[i * TA:(i + 1) * TA, :],
                               in_=out_sb[:, sl * D:(sl + 1) * D])\
                    .then_inc(STA[sl], 16)
                C.v[f"st_A{i}"] = C.add(f"sta{sl}", 16)

        emit_ln_A(0)
        emit_tr_A(0)
        emit_waves(0, (0, 1))
        def emit_x_load_B0(q):
            @block.sync
            def _(sync, q=q):
                sync.wait_ge(VE, C.get(f"ve_epi_A{14 + q}"))
                sync.dma_start(out=xs[:, q * D:(q + 1) * D],
                               in_=x_d[q * 128:(q + 1) * 128, :])\
                    .then_inc(LDX[q], 16)
                C.v[f"ldx_B0_{q}"] = C.add(f"ldx{q}", 16)
                C.v[f"ldxsl_B0_{q}"] = q

        for i in range(NCA + 1):
            if i >= 1:
                emit_tail(i - 1)
            if i == NCA:
                emit_x_load_B0(0)
                emit_x_load_B0(1)
                gpB0 = C.get("gp")
                extra1 = {0: [(GP, C.get("gp_wA_136"))],
                          1: [(GP, C.get("gp_wA_137"))],
                          2: [(VE, C.get(f"cv_{NCA-1}_14"))]}
                sccB0 = C.get("scc")
                stage_dmas(piecesB[:64], "wB", gpB0, slot_w1, 6, extra=extra1,
                           split_scc_base=sccB0)
                extra2 = {0: [(PE, C.get(f"pe_fa_{NCA-1}"))],
                          4: [(GP, gpB0 + 31)],
                          5: [(SCC, sccB0 + 31)]}
                stage_dmas(piecesB[64:], "wB2", gpB0 + 32, slot_w2, 6,
                           extra=extra2, split_scc_base=sccB0 + 32)
                break
            if 2 <= i + 1 < NCA:
                emit_x_load(i + 1)
            if i + 1 < NCA:
                emit_ln_A(i + 1)
            emit_waves(i, (2, 3, 4, 5, 6, 7))
            if i + 1 < NCA:
                emit_tr_A(i + 1)
            emit_conv(i)
            if i + 1 < NCA:
                emit_waves(i + 1, (0, 1))
            emit_attn(i)

        gatesB1 = [C.get(f"pe_wave_{NCA-1}_{q}") for q in range(8)
                   for _k in range(8)]
        gatesB2 = [C.get(f"pe_delta_{NCA-1}")] * 40
        stage_casts(piecesB[:64], "wB", slot_w1, 6, gatesB1, split=True)
        stage_casts(piecesB[64:], "wB2", slot_w2, 6, gatesB2, split=True)
        gp_wB_q = [[(GP, C.get(f"gp_wB_{q * 8 + 6}")),
                    (SCC, C.get(f"scc_wB_{q * 8 + 7}"))] for q in range(8)]
        gp_wB = [(GP, C.get("gp")), (SCC, C.get("scc"))]

        CQ = consts[:, 0:1]
        CI64 = consts[:, 1:2]
        CM128 = consts[:, 2:3]
        CH = consts[:, 3:4]
        CB0 = consts[:, 4:5]

        for j in range(NCB):
            for q in range(2):
                sl = (2 * j + q) % 2
                tag = f"B{j}_{q}"

                if j > 0:
                    @block.sync
                    def _(sync, j=j, q=q, sl=sl, tag=tag):
                        gk = f"ve_xn_B{j-1}_{q}"
                        sync.wait_ge(VE, C.get(gk))
                        sync.dma_start(out=xs[:, sl * D:(sl + 1) * D],
                                       in_=x_d[j * TB + q * 128:
                                               j * TB + (q + 1) * 128, :])\
                            .then_inc(LDX[sl], 16)
                        C.v[f"ldx_{tag}"] = C.add(f"ldx{sl}", 16)
                        C.v[f"ldxsl_{tag}"] = sl

                xn_guard = (C.get(f"trB{j-1}_{q}_last_pe") if j >= 1
                            else C.get(f"trA{14 + q}_last_pe"))
                prev_var = (f"B{j}_{q-1}" if q == 1 else
                            (f"B{j-1}_1" if j >= 1 else "A15"))
                ln_chunk(xs[:, sl * D:(sl + 1) * D], xn_sb[:, q * D:(q + 1) * D],
                         tag, prev_var, xn_guard)

                srcs = [xn_sb[:, q * D + d * 128: q * D + (d + 1) * 128]
                        for d in range(8)]
                dsts = [xnT_sb[:, d * TB + q * 128: d * TB + (q + 1) * 128]
                        for d in range(8)]
                @block.tensor
                def _(tensor, j=j, q=q):
                    if j >= 1 and q == 0:
                        tensor.wait_ge(PE, C.get(f"pe_h_{j-1}_{NFM-1}"))
                emit_transposes2(srcs, dsts, C.get(f"ve_xn_{tag}"), f"trB{j}_{q}")
                C.v[f"trB{j}_{q}_last_pe"] = C.get("pe")

            def wq_s(i):
                return wq_p[:, (i % 2) * TB:(i % 2 + 1) * TB]

            def XqP(f):
                return Xq_p[:, (f % 4) * TB:(f % 4) * TB + 2 * TB]

            def accP(f, half):
                s = (f // 2) % 2
                return acc_p[:, s * 4 * TB + half * 2 * TB:
                             s * 4 * TB + (half + 1) * 2 * TB]

            wqP = wq_p[:, 0:2 * TB]
            kkP = kk_p[:, 0:2 * TB]
            ttP = tt_p[:, 0:2 * TB]
            k2P = k2_p[:, 0:2 * TB]
            s3P = s3_t[:, 0:2 * TB]
            xq2P = xq2_t[:, 0:2 * TB]

            for g in range(0, NFM + 6, 2):
                @block.tensor
                def _(tensor, j=j, g=g):
                    if g < NFM:
                        gk = (f"sc_w_{j}_{g-2}" if g >= 2
                              else f"sc_w_{j-1}_{g+NFM-2}")
                        if C.get(gk):
                            tensor.wait_ge(SC, C.get(gk))
                        if g == 0:
                            tensor.wait_ge(SC, C.get(f"trB{j}_1_last"))
                            if j == 0:
                                tensor.wait_ge(SC, C.get(f"sc_wave_{NCA-1}_7"))
                            if C.get(f"sc_w4_{j-1}_7"):
                                tensor.wait_ge(SC, C.get(f"sc_w4_{j-1}_7"))
                        if j == 0:
                            for s_, v_ in gp_wB_q[g // 4]:
                                tensor.wait_ge(s_, v_)
                        for fg in (g, g + 1):
                            hs = (fg % 2) * 512
                            last = None
                            for kt in range(8):
                                last = tensor.matmul(
                                    out=pool[:, hs: hs + TB],
                                    lhsT=w1_kt(kt)[:, fg * 128:(fg + 1) * 128],
                                    rhs=xnT_sb[:, kt * TB:(kt + 1) * TB],
                                    start=(kt == 0), stop=(kt == 7))
                            last.then_inc(PE, 1)
                            C.v[f"pe_h_{j}_{fg}"] = C.add("pe")

                @block.gpsimd
                def _(gps, j=j, g=g):
                    f3 = g - 4
                    if 0 <= f3 < NFM:
                        gps.wait_ge(VE, C.get(f"ve_p_{j}_{f3}"))
                        gps.tensor_copy(k2P, accP(f3, 0)).then_inc(GP, 1)
                        C.v[f"gp_k2_{j}_{f3}"] = C.v[f"gp_k2_{j}_{f3+1}"] = C.add("gp")

                @block.scalar
                def _(scalar, j=j, g=g):
                    if g < NFM:
                        if g >= 2:
                            scalar.wait_ge(VE, C.get(f"ve_X_{j}_{g-2}"))
                        elif C.get(f"ve_fq_{j-1}_6"):
                            scalar.wait_ge(VE, C.get(f"ve_fq_{j-1}_6"))
                        for fg in (g, g + 1):
                            scalar.wait_ge(PE, C.get(f"pe_h_{j}_{fg}"))
                            scalar.activation(out=wq_s(fg),
                                              in_=pool[:, (fg % 2) * 512:
                                                       (fg % 2) * 512 + TB],
                                              func=AF.Identity, scale=CQ,
                                              bias=c1_sb[:, fg:fg + 1])
                        scalar.drain()
                        scalar.copy(out=kkP, in_=wqP).then_inc(SC, 1)
                        C.v[f"sc_w_{j}_{g}"] = C.v[f"sc_w_{j}_{g+1}"] = C.add("sc")
                    f2 = g - 2
                    if 0 <= f2 < NFM:
                        if f2 >= 2:
                            scalar.wait_ge(VE, C.get(f"ve_p_{j}_{f2-2}"))
                        else:
                            scalar.wait_ge(VE, C.get(f"ve_X_{j}_{0}"))
                        scalar.activation(out=ttP, in_=XqP(f2),
                                          func=AF.Abs, scale=CI64, bias=CM128)
                        scalar.activation(out=accP(f2, 0), in_=XqP(f2),
                                          func=AF.Identity, scale=CH,
                                          bias=CB0).then_inc(SC, 1)
                        C.v[f"sc_t_{j}_{f2}"] = C.v[f"sc_t_{j}_{f2+1}"] = C.add("sc")

                @block.vector
                def _(vector, j=j, g=g):
                    f3 = g - 4
                    f2 = g - 2
                    have3 = 0 <= f3 < NFM
                    have1 = g < NFM
                    have2 = 0 <= f2 < NFM
                    if have3:
                        vector.wait_ge(GP, C.get(f"gp_k2_{j}_{f3}"))
                        if f3 >= 2:
                            vector.wait_ge(SC, C.get(f"sc_y_{j}_{f3-2}"))
                        elif C.get(f"sc_f_{j-1}_7"):
                            vector.wait_ge(SC, C.get(f"sc_f_{j-1}_7"))
                    if have3 and not have2:
                        vector.drain()
                        vector._custom_dve(QF1_OP, out=s3P, in0=accP(f3, 0),
                                           in1=k2P, s0=8.0, imm2=4.0)
                        vector.drain()
                        vector._custom_dve(QF2_OP, out=xq2P, in0=s3P,
                                           in1=accP(f3, 0), s0=2.0, s1=1.0)\
                              .then_inc(VE, 1)
                        C.v[f"ve_q_{j}_{f3}"] = C.v[f"ve_q_{j}_{f3+1}"] = C.add("ve")
                    if have1 and not have2:
                        vector.wait_ge(SC, C.get(f"sc_w_{j}_{g}"))
                        vector.drain()
                        vector._custom_dve(QF1_OP, out=s3P, in0=wqP,
                                           in1=kkP, s0=8.0, imm2=4.0)
                        vector.drain()
                        vector._custom_dve(QF2_OP, out=XqP(g), in0=s3P,
                                           in1=wqP, s0=2.0, s1=1.0)\
                              .then_inc(VE, 1)
                        C.v[f"ve_X_{j}_{g}"] = C.v[f"ve_X_{j}_{g+1}"] = C.add("ve")
                    if have2:
                        vector.wait_ge(SC, C.get(f"sc_t_{j}_{f2}"))
                        cur, nxt = accP(f2, 0), accP(f2, 1)
                        last = None
                        for k, (c, beta, alpha) in enumerate(PAIRS):
                            if k > 0:
                                vector.drain()
                            last = vector._custom_dve(
                                PAIR_OP, out=nxt, in0=ttP, in1=cur,
                                s0=float(c), s1=float(beta), imm2=float(alpha))
                            cur, nxt = nxt, cur
                            if k == 0 and have3:
                                vector._custom_dve(QF1_OP, out=s3P,
                                                   in0=accP(f3, 0), in1=k2P,
                                                   s0=8.0, imm2=4.0)
                            elif k == 1 and have3:
                                vector._custom_dve(QF2_OP, out=xq2P, in0=s3P,
                                                   in1=accP(f3, 0),
                                                   s0=2.0, s1=1.0)\
                                      .then_inc(VE, 1)
                                C.v[f"ve_q_{j}_{f3}"] = C.v[f"ve_q_{j}_{f3+1}"] \
                                    = C.add("ve")
                            elif k == 5 and have1:
                                vector.wait_ge(SC, C.get(f"sc_w_{j}_{g}"))
                                vector._custom_dve(QF1_OP, out=s3P, in0=wqP,
                                                   in1=kkP, s0=8.0, imm2=4.0)
                            elif k == 6 and have1:
                                vector._custom_dve(QF2_OP, out=XqP(g), in0=s3P,
                                                   in1=wqP, s0=2.0, s1=1.0)\
                                      .then_inc(VE, 1)
                                C.v[f"ve_X_{j}_{g}"] = C.v[f"ve_X_{j}_{g+1}"] \
                                    = C.add("ve")
                        last.then_inc(VE, 1)
                        C.v[f"ve_p_{j}_{f2}"] = C.v[f"ve_p_{j}_{f2+1}"] = C.add("ve")

                @block.scalar
                def _(scalar, j=j, g=g):
                    f3 = g - 4
                    if 0 <= f3 < NFM:
                        scalar.wait_ge(VE, C.get(f"ve_q_{j}_{f3}"))
                        if f3 == 0 and C.get(f"pe_w2_{j-1}_7"):
                            scalar.wait_ge(PE, C.get(f"pe_w2_{j-1}_7"))
                        scalar.activation(out=y_all[:, f3 * TB:(f3 + 2) * TB],
                                          in_=xq2P, func=AF.Identity,
                                          scale=1.0 / 256, bias=consts[:, 6:7])\
                              .then_inc(SC, 1)
                        C.v[f"sc_y_{j}_{f3}"] = C.v[f"sc_y_{j}_{f3+1}"] = C.add("sc")


            for o in range(8):
                @block.tensor
                def _(tensor, j=j, o=o):
                    if o == 0:
                        tensor.wait_ge(SC, C.get(f"sc_y_{j}_{NFM-1}"))
                        if C.get(f"sc_w4_{j-1}_7"):
                            tensor.wait_ge(SC, C.get(f"sc_w4_{j-1}_7"))
                        if j == 0:
                            tensor.wait_ge(VE, C.get(f"ve_epi_A{NCA-1}"))
                            for s_, v_ in gp_wB:
                                tensor.wait_ge(s_, v_)
                    last = None
                    for f3 in range(NFM):
                        last = tensor.matmul(
                            out=pool[:, 1024 + o * TB: 1024 + (o + 1) * TB],
                            lhsT=w2_kt(f3)[:, o * 128:(o + 1) * 128],
                            rhs=y_all[:, f3 * TB:(f3 + 1) * TB],
                            start=(f3 == 0), stop=(f3 == NFM - 1))
                    last.then_inc(PE, 1)
                    C.v[f"pe_w2_{j}_{o}"] = C.add("pe")

            for op in range(0, 8, 2):
                @block.scalar
                def _(scalar, j=j, op=op):
                    scalar.wait_ge(PE, C.get(f"pe_w2_{j}_{op+1}"))
                    if op >= 2:
                        scalar.wait_ge(VE, C.get(f"ve_fq_{j}_{op-2}"))
                    else:
                        scalar.wait_ge(VE, C.get(f"ve_X_{j}_{NFM-1}"))
                    for o in (op, op + 1):
                        scalar.activation(out=wq_s(o),
                                          in_=pool[:, 1024 + o * TB:
                                                   1024 + (o + 1) * TB],
                                          func=AF.Identity, scale=CQ,
                                          bias=c2_sb[:, o:o + 1])
                    scalar.drain()
                    scalar.copy(out=kkP, in_=wqP).then_inc(SC, 1)
                    C.v[f"sc_w4_{j}_{op}"] = C.v[f"sc_w4_{j}_{op+1}"] = C.add("sc")

                @block.vector
                def _(vector, j=j, op=op):
                    vector.wait_ge(SC, C.get(f"sc_w4_{j}_{op}"))
                    if op == 0:
                        vector.wait_ge(SC, C.get(f"sc_y_{j}_{NFM-1}"))
                    else:
                        vector.wait_ge(SC, C.get(f"sc_f_{j}_{op-2}"))
                    vector.drain()
                    vector._custom_dve(QF1_OP, out=s3P, in0=wqP,
                                       in1=kkP, s0=8.0, imm2=4.0)
                    vector.drain()
                    vector._custom_dve(QF2_OP, out=xq2P, in0=s3P,
                                       in1=wqP, s0=2.0, s1=1.0)\
                          .then_inc(VE, 1)
                    C.v[f"ve_fq_{j}_{op}"] = C.v[f"ve_fq_{j}_{op+1}"] = C.add("ve")

                @block.scalar
                def _(scalar, j=j, op=op):
                    scalar.wait_ge(VE, C.get(f"ve_fq_{j}_{op}"))
                    if op == 0 and j == 0:
                        scalar.wait_ge(GP, C.get("gp_wB2_38"))
                        scalar.wait_ge(SCC, C.get("scc_wB2_39"))
                    if op == 0 and C.get(f"pe_finB_{j-1}_1_1"):
                        scalar.wait_ge(PE, C.get(f"pe_finB_{j-1}_1_1"))
                    scalar.activation(out=f_all[:, op * TB:(op + 2) * TB],
                                      in_=xq2P, func=AF.Identity,
                                      scale=1.0 / 256, bias=consts[:, 7:8])\
                          .then_inc(SC, 1)
                    C.v[f"sc_f_{j}_{op}"] = C.v[f"sc_f_{j}_{op+1}"] = C.add("sc")

            for tt in range(2):
                a_chunk = 2 * j + tt
                @block.sync
                def _(sync, j=j, tt=tt, a_chunk=a_chunk):
                    sync.wait_ge(STA[a_chunk % 2], C.get(f"st_A{a_chunk}"))
                    if a_chunk == 0:
                        for s_, v_ in gp_wB:
                            sync.wait_ge(s_, v_)
                    if C.get(f"ve_outh_{2*j+tt-1}_done"):
                        sync.wait_ge(VE, C.get(f"ve_outh_{2*j+tt-1}_done"))
                    sync.dma_start(out=stage[:],
                                   in_=out_d[j * TB + tt * 128:
                                             j * TB + (tt + 1) * 128, :])\
                        .then_inc(LDO, 16)
                    C.v[f"ldo_{a_chunk}"] = C.add("ldo", 16)

                for half in range(2):
                    @block.tensor
                    def _(tensor, j=j, tt=tt, half=half):
                        tensor.wait_ge(SC, C.get(f"sc_f_{j}_7"))
                        gk = (f"ve_outh_{j}_{tt}_{half-1}" if half == 1 else
                              (f"ve_outh_{j}_{tt-1}_1" if tt == 1 else
                               f"ve_outh_{j-1}_1_1"))
                        if C.get(gk):
                            tensor.wait_ge(VE, C.get(gk))
                        last = None
                        for oo in range(4):
                            o = half * 4 + oo
                            last = tensor.transpose(
                                out=pool[:, 3072 + oo * 128: 3072 + (oo + 1) * 128],
                                in_=f_all[:, o * TB + tt * 128:
                                          o * TB + tt * 128 + 128],
                                identity=idf[:])
                        last.then_inc(PE, 1)
                        C.v[f"pe_finB_{j}_{tt}_{half}"] = C.add("pe")

                    @block.vector
                    def _(vector, j=j, tt=tt, half=half, a_chunk=a_chunk):
                        vector.wait_ge(PE, C.get(f"pe_finB_{j}_{tt}_{half}"))
                        sl = (2 * j + tt) % 2
                        if half == 0:
                            vector.wait_ge(LDO, C.get(f"ldo_{a_chunk}"))
                        if half == 0 and C.get(f"st_B{2*j+tt-2}"):
                            vector.wait_ge(STB[sl], C.get(f"st_B{2*j+tt-2}"))
                        elif half == 0:
                            vector.wait_ge(STA[sl], C.get(f"st_A{14 + sl}"))
                        vector.tensor_tensor(
                            out=out_sb[:, sl * D + half * 512: sl * D + (half + 1) * 512],
                            in0=pool[:, 3072:3584],
                            in1=stage[:, half * 512:(half + 1) * 512],
                            op=AL.add).then_inc(VE, 1)
                        C.v[f"ve_outh_{j}_{tt}_{half}"] = C.add("ve")
                        if half == 1:
                            C.v[f"ve_outh_{a_chunk}_done"] = C.get("ve")

                @block.sync
                def _(sync, j=j, tt=tt, a_chunk=a_chunk):
                    sync.wait_ge(VE, C.get(f"ve_outh_{j}_{tt}_1"))
                    sl = (2 * j + tt) % 2
                    sync.dma_start(out=out_d[j * TB + tt * 128:
                                             j * TB + (tt + 1) * 128, :],
                                   in_=out_sb[:, sl * D:(sl + 1) * D])\
                        .then_inc(STB[sl], 16)
                    C.v[f"st_B{a_chunk}"] = C.add(f"stb{sl}", 16)

        @block.vector
        def _(vector):
            vector.wait_ge(STA[0], C.get("sta0"))
            vector.wait_ge(STA[1], C.get("sta1"))
            vector.wait_ge(STB[0], C.get("stb0"))
            vector.wait_ge(STB[1], C.get("stb1"))

        block.emit(real_block)

    return nc


_NC_CACHE = None


def kernel(L0_attn_x, L0_attn_w_in, L0_attn_w_conv, L0_attn_b_conv,
           L0_attn_w_sel, L0_attn_w_dt, L0_attn_b_dt, L0_attn_w_out,
           L0_attn_A, L0_attn_Dp, L0_ffn_w1, L0_ffn_b1, L0_ffn_w2, L0_ffn_b2):
    global _NC_CACHE
    x = np.ascontiguousarray(np.asarray(L0_attn_x, dtype=np.float32))
    w_in = np.ascontiguousarray(np.asarray(L0_attn_w_in, np.float32))
    w_conv = np.ascontiguousarray(
        np.asarray(L0_attn_w_conv, np.float32).reshape(E, 4)
        .reshape(16, 128, 4).transpose(1, 0, 2).reshape(128, 64))
    b_conv = np.ascontiguousarray(
        np.asarray(L0_attn_b_conv, np.float32).reshape(16, 128).T)
    w_sel = np.ascontiguousarray(np.asarray(L0_attn_w_sel, np.float32))
    w_dtb = np.zeros((128, E), np.float32)
    w_dtb[0:64, :] = np.asarray(L0_attn_w_dt, np.float32)
    w_dtb[64, :] = np.asarray(L0_attn_b_dt, np.float32)
    w_out = np.ascontiguousarray(np.asarray(L0_attn_w_out, np.float32))
    w1 = np.ascontiguousarray(np.asarray(L0_ffn_w1, np.float32))
    b1 = np.ascontiguousarray(np.asarray(L0_ffn_b1, np.float32).reshape(32, 128).T)
    w2 = np.ascontiguousarray(np.asarray(L0_ffn_w2, np.float32))
    b2 = np.ascontiguousarray(np.asarray(L0_ffn_b2, np.float32).reshape(8, 128).T)
    idb = np.eye(128, dtype=ml_dtypes.bfloat16)
    idf = np.eye(128, dtype=np.float32)

    if _NC_CACHE is None:
        nc = build_nc()
        mybir.codegen_inst_isa_subclasses(nc)
        _NC_CACHE = nc
    nc = _NC_CACHE

    in_maps = []
    for c in range(8):
        in_maps.append({
            "x": np.ascontiguousarray(x[c]),
            "w_in": w_in, "w_conv": w_conv, "b_conv": b_conv,
            "w_sel": w_sel, "w_dtb": w_dtb, "w_out": w_out,
            "w1": w1, "b1": b1, "w2": w2, "b2": b2,
            "idb": idb, "idf": idf,
        })
    res = run_bass_kernel_spmd(nc, in_maps, core_ids=list(range(8)))
    out = np.stack([np.asarray(r["out"], dtype=np.float32)
                    for r in res.results], axis=0)
    return out


# version 29
# speedup vs baseline: 1.0250x; 1.0017x over previous
import math
import contextlib
import numpy as np
import ml_dtypes

import concourse.bass as bass
import concourse.mybir as mybir
from concourse.bass_utils import run_bass_kernel_spmd
from concourse.dve_op import DveOp
from concourse.dve_spec import (
    Spec, Src0, Src1, C0, C1, C2, Zero, One, maxx, minn, lower as dve_lower,
)
import concourse.dve_ops as dve_ops_mod
from concourse.dve_ops import OPS as DVE_OPS

F32 = mybir.dt.float32
BF16 = mybir.dt.bfloat16
I32 = mybir.dt.int32
AL = mybir.AluOpType
AF = mybir.ActivationFunctionType


def _register_op(name, spec, reference, replace=None):
    if name in dve_ops_mod._SUB_OPCODE_FOR_NAME:
        for o in DVE_OPS:
            if o.name == name:
                return o
        raise RuntimeError(f"{name}: row registered but op missing")
    shas = {}
    for ver in ("v3", "v4"):
        shas[ver] = dve_lower(spec, ver=ver, name=name).sha(ver)
    op = DveOp(name, spec, subdim=False, uops_sha=shas, reference=reference)
    if replace is not None:
        idx = next(i for i, o in enumerate(DVE_OPS) if o.name == replace)
        DVE_OPS[idx] = op
        row = dve_ops_mod._CUSTOM_DVE_ROW_BASE + idx
    else:
        DVE_OPS.append(op)
        row = dve_ops_mod._CUSTOM_DVE_ROW_BASE + len(DVE_OPS) - 1
    assert row < 0x20
    dve_ops_mod.CUSTOM_DVE_SPECS[name] = spec
    dve_ops_mod.CUSTOM_DVE_REFS[name] = reference
    dve_ops_mod._SUB_OPCODE_FOR_NAME[name] = row
    return op


_w = maxx(Src0, C0)
PAIR_OP = _register_op(
    "MK_PWLPAIR",
    Spec(body=(_w * C1 + Src1) + minn(_w, C0 + One) * C2),
    lambda in0, in1, s0, s1, imm2: (
        np.maximum(in0, s0) * s1 + in1
        + np.minimum(np.maximum(in0, s0), s0 + 1.0) * imm2),
)

_r = (Src0 - Src1) + C0
_g3 = _r >= C0
_s2 = _r - _g3 * C0
_g2 = _s2 >= C2
QF1_OP = _register_op(
    "MK_QF1",
    Spec(body=_s2 - _g2 * C2),
    lambda in0, in1, s0, s1, imm2: (
        lambda r: (lambda s2: s2 - (s2 >= imm2) * imm2)(r - (r >= s0) * s0)
    )((in0 - in1) + s0),
    replace="DVEHASH32L_SEED",
)

_gg1 = Src0 >= C0
_ss4 = Src0 - _gg1 * C0
_gg0 = _ss4 >= C1
QF2_OP = _register_op(
    "MK_QF2",
    Spec(body=Src1 - (_ss4 - _gg0)),
    lambda in0, in1, s0, s1, imm2: (
        lambda s4: in1 - (s4 - (s4 >= s1))
    )(in0 - (in0 >= s0) * s0),
    replace="DVEHASH32L_BODY",
)

FLOOR_OP = _register_op(
    "MK_FLOOR",
    Spec(body=((Src0 >= Src1) + Src1) - One),
    lambda in0, in1, s0, s1, imm2: ((in0 >= in1) + in1) - 1.0,
)


def _pwl_consts():
    SQ2 = math.sqrt(2.0)
    U = [256.0 * (j * 0.25) * 0.5 * math.erf((j * 0.25) / SQ2) for j in range(17)]
    sig = [U[j + 1] - U[j] for j in range(16)]
    pairs, kconst = [], 0.0
    for k in range(8):
        beta = sig[2 * k + 1] - (sig[2 * k - 1] if k > 0 else 0.0)
        alpha = sig[2 * k] - sig[2 * k + 1]
        c = 2.0 * k
        pairs.append((c, float(beta), float(alpha)))
        kconst += c * (beta + alpha)
    b0 = -4096.0 - kconst + 64.5
    return pairs, float(b0)


PAIRS, B0 = _pwl_consts()

D = 1024
E = 2048
FF = 4096
LTOK = 2048
TA = 128
NCA = LTOK // TA
TB = 256
NCB = LTOK // TB
NFM = FF // 128
DVE_CONV_ROWS = 16


class Ctr:

    def __init__(self):
        self.v = {}
        self.logs = {}
        self._rec = []
        self._rep = None

    def add(self, name, amt=1):
        if self._rep is not None:
            r = self._rep[0][self._rep[1]]
            self._rep[1] += 1
            return r
        self.v[name] = self.v.get(name, 0) + amt
        r = self.v[name]
        if self._rec:
            self._rec[-1].append(r)
        return r

    def get(self, name):
        if self._rep is not None:
            r = self._rep[0][self._rep[1]]
            self._rep[1] += 1
            return r
        r = self.v.get(name, 0)
        if self._rec:
            self._rec[-1].append(r)
        return r

    def begin_record(self, idx):
        lst = []
        self.logs[idx] = lst
        self._rec.append(lst)

    def end_record(self):
        self._rec.pop()

    def begin_replay(self, idx):
        self._rep = [self.logs[idx], 0]

    def end_replay(self, idx):
        assert self._rep[1] == len(self.logs[idx]), \
            f"replay divergence in body {idx}"
        self._rep = None


class _DummyRes:
    def then_inc(self, *a, **k):
        return self


class _DummyEng:
    def __getattr__(self, n):
        def f(*a, **k):
            return _DummyRes()
        return f


class _SecBlock:

    def __init__(self, C):
        self.C = C
        self.bodies = []

    def _deco(self, eng_name):
        def deco(fn):
            idx = len(self.bodies)
            self.bodies.append((eng_name, fn))
            self.C.begin_record(idx)
            fn(_DummyEng())
            self.C.end_record()
        return deco

    @property
    def sync(self):
        return self._deco("sync")

    @property
    def tensor(self):
        return self._deco("tensor")

    @property
    def scalar(self):
        return self._deco("scalar")

    @property
    def vector(self):
        return self._deco("vector")

    @property
    def gpsimd(self):
        return self._deco("gpsimd")

    def emit(self, real_block):
        for eng_name in ["sync", "tensor", "scalar", "vector", "gpsimd"]:
            todo = [(i, f) for i, (e, f) in enumerate(self.bodies)
                    if e == eng_name]
            if not todo:
                continue

            def run(engine, todo=todo):
                for i, f in todo:
                    self.C.begin_replay(i)
                    f(engine)
                    self.C.end_replay(i)

            getattr(real_block, eng_name)(run)


def build_nc():
    nc = bass.Bass("TRN2", target_bir_lowering=False,
                   dynamic_dma_scratch_size=2048)
    dtt = lambda n, s, t, k: nc.dram_tensor(n, s, t, kind=k).ap()
    x_d = dtt("x", [LTOK, D], F32, "ExternalInput")
    w_in_d = dtt("w_in", [D, 2 * E], F32, "ExternalInput")
    w_conv_d = dtt("w_conv", [128, 64], F32, "ExternalInput")
    b_conv_d = dtt("b_conv", [128, 16], F32, "ExternalInput")
    w_sel_d = dtt("w_sel", [E, 96], F32, "ExternalInput")
    w_dtb_d = dtt("w_dtb", [128, E], F32, "ExternalInput")
    w_out_d = dtt("w_out", [E, D], F32, "ExternalInput")
    w1_d = dtt("w1", [D, FF], F32, "ExternalInput")
    b1_d = dtt("b1", [128, 32], F32, "ExternalInput")
    w2_d = dtt("w2", [FF, D], F32, "ExternalInput")
    b2_d = dtt("b2", [128, 8], F32, "ExternalInput")
    idb_d = dtt("idb", [128, 128], BF16, "ExternalInput")
    idf_d = dtt("idf", [128, 128], F32, "ExternalInput")
    out_d = dtt("out", [LTOK, D], F32, "ExternalOutput")

    C = Ctr()
    es = contextlib.ExitStack()
    with es:
        def sb(name, shape, t=F32):
            return es.enter_context(nc.sbuf_tensor(name, shape, t))

        wbigA = sb("wbigA", [128, 8 * (2 * E)], BF16)
        wbigB = sb("wbigB", [128, 32 * D], BF16)
        xn_sb = sb("xn_sb", [128, 2 * D], BF16)
        xnT_sb = sb("xnT_sb", [128, 2 * D], BF16)
        siluz = sb("siluz", [128, E], BF16)
        xa_p = sb("xa_p", [128, 16 * 128], BF16)
        bcd_rhs = sb("bcd_rhs", [128, 128], BF16)
        y_all = sb("y_all", [128, NFM * TB], BF16)
        idb = sb("idb_sb", [128, 128], BF16)

        xs = sb("xs", [128, 2 * D], F32)
        xc = sb("xc", [128, 16 * 131], F32)
        wq_p = sb("wq_p", [128, 2 * TB], F32)
        Xq_p = sb("Xq_p", [128, 4 * TB], F32)
        tt_p = sb("tt_p", [128, 2 * TB], F32)
        acc_p = sb("acc_p", [128, 8 * TB], F32)
        xq2_t = sb("xq2_t", [128, 2 * TB], F32)
        conv_t = sb("conv_t", [128, 4 * 128], F32)
        s3_t = conv_t
        out_sb = sb("out_sb", [128, 2 * D], F32)
        f_all = sb("f_all", [128, 8 * TB], F32)
        attn_fm = f_all
        stage = sb("stage", [128, 1024], F32)
        idf = sb("idf_sb", [128, 128], F32)
        smalls = sb("smalls", [128, 16], F32)
        kk_p = es.enter_context(nc.sbuf_tensor("kk_p", [128, 2 * TB], I32))
        k2_p = es.enter_context(nc.sbuf_tensor("k2_p", [128, 2 * TB], I32))
        w_conv_sb = sb("w_conv_sb", [128, 64], F32)
        b_conv_sb = sb("b_conv_sb", [128, 16], F32)
        c1_sb = sb("c1_sb", [128, 32], F32)
        c2_sb = sb("c2_sb", [128, 8], F32)
        consts = sb("consts", [128, 8], F32)

        pool = es.enter_context(nc.psum_tensor("pool", [128, 3584], F32))
        ps_tr = es.enter_context(nc.psum_tensor("ps_tr", [128, 1024], BF16))

        sems = {}
        for sname in ["ldsm", "ldx0", "ldx1", "lds0", "lds1", "ldo",
                      "pe", "sc", "ve", "gp", "sta0", "sta1", "stb0", "stb1",
                      "lds2", "lds3", "lds4", "lds5", "lds6", "lds7",
                      "scc"]:
            sems[sname] = es.enter_context(nc.semaphore(sname))
        real_block = es.enter_context(nc.Block())
        block = _SecBlock(C)
        LDSM = sems["ldsm"]
        LDX = [sems["ldx0"], sems["ldx1"]]
        LDS = [sems[f"lds{k}"] for k in range(8)]
        PE, SC, VE, GP = sems["pe"], sems["sc"], sems["ve"], sems["gp"]
        SCC = sems["scc"]
        STA = [sems["sta0"], sems["sta1"]]
        STB = [sems["stb0"], sems["stb1"]]
        LDO = sems["ldo"]

        def w_in_kt(k):
            return wbigA[:, k * (2 * E):(k + 1) * (2 * E)]

        w1_kt = w_in_kt

        def w_out_kt(k):
            return wbigB[:, k * D:(k + 1) * D]

        w2_kt = w_out_kt

        def w_sel_kt(k):
            return wbigB[:, 16 * D + k * 96: 16 * D + (k + 1) * 96]

        dtb_off = 16 * D + 16 * 96

        def dtb_ap():
            return wbigB[:, dtb_off: dtb_off + E]

        @block.sync
        def _(sync):
            for (dst, src) in [(idb[:], idb_d[:]), (idf[:], idf_d[:]),
                               (w_conv_sb[:], w_conv_d[:]),
                               (b_conv_sb[:], b_conv_d[:]),
                               (out_sb[:, 0:32], b1_d[:]),
                               (out_sb[:, 32:40], b2_d[:])]:
                sync.dma_start(out=dst, in_=src).then_inc(LDSM, 16)
                C.add("ldsm", 16)
        ld_small = C.get("ldsm")

        @block.vector
        def _(vector):
            for col, val in [(0, 256.0), (1, 1.0 / 64), (2, -128.0),
                             (3, 0.5), (4, B0), (5, 1e-5),
                             (6, -0.25), (7, -32.0)]:
                vector.memset(consts[:, col:col + 1], float(val))
            vector.memset(bcd_rhs[64:65, :], 1.0)
            for r in range(16):
                vector.memset(xc[:, r * 131: r * 131 + 3], 0.0)
            vector.wait_ge(LDSM, ld_small)
            vector.tensor_scalar(out=c1_sb[:], in0=out_sb[:, 0:32], scalar1=256.0,
                                 scalar2=8192.5, op0=AL.mult, op1=AL.add)
            vector.tensor_scalar(out=c2_sb[:], in0=out_sb[:, 32:40], scalar1=256.0,
                                 scalar2=8192.5, op0=AL.mult, op1=AL.add)\
                  .then_inc(VE, 1)
            C.add("ve")
        ve_consts = C.get("ve")

        @block.scalar
        def _(scalar):
            scalar.wait_ge(VE, ve_consts)
            scalar.wait_ge(LDSM, ld_small)

        @block.tensor
        def _(tensor):
            tensor.wait_ge(VE, ve_consts)
            tensor.wait_ge(LDSM, ld_small)

        @block.gpsimd
        def _(g):
            g.wait_ge(VE, ve_consts)
            g.wait_ge(LDSM, ld_small)

        for i0 in range(2):
            @block.sync
            def _(sync, i=i0):
                sl = i % 2
                sync.dma_start(out=xs[:, sl * D:(sl + 1) * D],
                               in_=x_d[i * TA:(i + 1) * TA, :])\
                    .then_inc(LDX[sl], 16)
                C.v[f"ldx_A{i}"] = C.add(f"ldx{sl}", 16)
                C.v[f"ldxsl_A{i}"] = sl

        def slot_wA(idx, cols):
            s = idx % 8
            if s < 2:
                return stage[:, s * 512: s * 512 + cols]
            if s < 6:
                return acc_p[:, (s - 2) * 512: (s - 2) * 512 + cols]
            return Xq_p[:, (s - 6) * 512: (s - 6) * 512 + cols]

        def slot_w1(idx, cols):
            s = idx % 6
            if s < 2:
                return stage[:, s * 512: s * 512 + cols]
            return xc[:, (s - 2) * 512: (s - 2) * 512 + cols]

        def slot_w2(idx, cols):
            s = idx % 6
            if s < 4:
                return f_all[:, s * 512: s * 512 + cols]
            return stage[:, (s - 4) * 512: (s - 4) * 512 + cols]

        def stage_dmas(pieces, tag, gp_base, slot_fn, nslot, extra=None,
                       split_scc_base=None):
            for idx, (src_ap, dst_ap, cols, pe_gate) in enumerate(pieces):
                @block.sync
                def _(sync, src_ap=src_ap, idx=idx, cols=cols, tag=tag,
                      gp_base=gp_base, slot_fn=slot_fn, nslot=nslot,
                      extra=extra, split_scc_base=split_scc_base):
                    s = idx % nslot
                    prev = C.get(f"lds{s}")
                    if prev > 0:
                        sync.wait_ge(LDS[s], prev)
                    if idx >= nslot:
                        k = idx - nslot
                        if split_scc_base is None:
                            sync.wait_ge(GP, gp_base + k + 1)
                        elif k % 2 == 0:
                            sync.wait_ge(GP, gp_base + k // 2 + 1)
                        else:
                            sync.wait_ge(SCC, split_scc_base + (k + 1) // 2)
                    if extra is not None and idx in extra:
                        for sem, val in extra[idx]:
                            sync.wait_ge(sem, val)
                    sync.dma_start(out=slot_fn(idx, cols), in_=src_ap)\
                        .then_inc(LDS[s], 16)
                    C.v[f"ld_{tag}_{idx}"] = C.add(f"lds{s}", 16)

        def stage_casts(pieces, tag, slot_fn, nslot, gates=None, split=False):
            @block.gpsimd
            def _(g, tag=tag, slot_fn=slot_fn, nslot=nslot):
                for idx, (src_ap, dst_ap, cols, pe_gate) in enumerate(pieces):
                    if split and idx % 2 == 1:
                        continue
                    if gates is not None:
                        pe_gate = gates[idx]
                    g.wait_ge(LDS[idx % nslot], C.get(f"ld_{tag}_{idx}"))
                    if pe_gate:
                        g.wait_ge(PE, pe_gate)
                    g.tensor_copy(dst_ap, slot_fn(idx, cols)).then_inc(GP, 1)
                    C.v[f"gp_{tag}_{idx}"] = C.add("gp")
            if not split:
                return

            @block.scalar
            def _(scalar, tag=tag, slot_fn=slot_fn, nslot=nslot):
                for idx, (src_ap, dst_ap, cols, pe_gate) in enumerate(pieces):
                    if idx % 2 == 0:
                        continue
                    if gates is not None:
                        pe_gate = gates[idx]
                    scalar.wait_ge(LDS[idx % nslot], C.get(f"ld_{tag}_{idx}"))
                    if pe_gate:
                        scalar.wait_ge(PE, pe_gate)
                    scalar.copy(out=dst_ap, in_=slot_fn(idx, cols))\
                          .then_inc(SCC, 1)
                    C.v[f"scc_{tag}_{idx}"] = C.add("scc")

        def stage_pieces(pieces, tag):
            stage_dmas(pieces, tag, C.get("gp"), slot_wA, 8)
            stage_casts(pieces, tag, slot_wA, 8)

        piecesA = []
        for q in range(8):
            for k in range(8):
                piecesA.append((w_in_d[k * 128:(k + 1) * 128, q * 512:(q + 1) * 512],
                                w_in_kt(k)[:, q * 512:(q + 1) * 512], 512, 0))
        nw_in = len(piecesA)
        for k in range(16):
            for q in range(2):
                piecesA.append((w_out_d[k * 128:(k + 1) * 128, q * 512:(q + 1) * 512],
                                w_out_kt(k)[:, q * 512:(q + 1) * 512], 512, 0))
        for k in range(16):
            piecesA.append((w_sel_d[k * 128:(k + 1) * 128, :], w_sel_kt(k), 96, 0))
        for q in range(4):
            piecesA.append((w_dtb_d[:, q * 512:(q + 1) * 512],
                            dtb_ap()[:, q * 512:(q + 1) * 512], 512, 0))
        n_outsel = len(piecesA)
        for k in range(20, 32):
            for q in range(2):
                piecesA.append((w2_d[k * 128:(k + 1) * 128, q * 512:(q + 1) * 512],
                                w2_kt(k)[:, q * 512:(q + 1) * 512], 512, 0))
        stage_pieces(piecesA, "wA")
        gp_w_in_q = [C.get(f"gp_wA_{q * 8 + 7}") for q in range(8)]
        gp_w_in = C.get(f"gp_wA_{nw_in - 1}")
        gp_w_outsel = C.get(f"gp_wA_{n_outsel - 1}")

        piecesB = []
        for q in range(8):
            for k in range(8):
                piecesB.append((w1_d[k * 128:(k + 1) * 128, q * 512:(q + 1) * 512],
                                w1_kt(k)[:, q * 512:(q + 1) * 512], 512, 0))
        for k in range(20):
            for q in range(2):
                piecesB.append((w2_d[k * 128:(k + 1) * 128, q * 512:(q + 1) * 512],
                                w2_kt(k)[:, q * 512:(q + 1) * 512], 512, 0))

        def ln_chunk(ptile_ap, xn_dst_ap, tag, prev_var_tag, xn_guard):
            mu = smalls[:, 0:1]; var = smalls[:, 1:2]
            sd = smalls[:, 7:8]; rstd = smalls[:, 6:7]

            @block.vector
            def _(vector):
                vector.wait_ge(LDX[C.get(f"ldxsl_{tag}")], C.get(f"ldx_{tag}"))
                if C.get(f"sc_rstd_{prev_var_tag}"):
                    vector.wait_ge(SC, C.get(f"sc_rstd_{prev_var_tag}"))
                vector.drain()
                vector.bn_stats(out=smalls[:, 4:10], in_=ptile_ap[:, 0:512])
                vector.bn_stats(out=smalls[:, 10:16], in_=ptile_ap[:, 512:1024])
                vector.drain()
                vector.bn_aggr(out=smalls[:, 0:2], in_=smalls[:, 4:16])\
                      .then_inc(VE, 1)
                C.v[f"ve_var_{tag}"] = C.add("ve")

            @block.scalar
            def _(scalar):
                scalar.wait_ge(VE, C.get(f"ve_var_{tag}"))
                scalar.activation(out=sd, in_=var, func=AF.Sqrt,
                                  bias=consts[:, 5:6]).then_inc(SC, 1)
                C.v[f"sc_rstd_{tag}"] = C.add("sc")

            @block.vector
            def _(vector):
                vector.wait_ge(SC, C.get(f"sc_rstd_{tag}"))
                vector.reciprocal(rstd, sd)
                vector.drain()
                if xn_guard:
                    vector.wait_ge(PE, xn_guard)
                vector.tensor_scalar(out=xn_dst_ap, in0=ptile_ap, scalar1=mu,
                                     scalar2=rstd, op0=AL.subtract, op1=AL.mult)\
                      .then_inc(VE, 1)
                C.v[f"ve_xn_{tag}"] = C.add("ve")

        def emit_transposes2(src_aps, dst_aps, wait_ve, tagp, contig_dst=None):
            n = len(src_aps)

            @block.tensor
            def _(tensor):
                tensor.wait_ge(VE, wait_ve)
                if C.get("last_trev"):
                    tensor.wait_ge(SC, C.get("last_trev"))
                last = None
                for d in range(n):
                    last = tensor.transpose(out=ps_tr[:, d * 128:(d + 1) * 128],
                                            in_=src_aps[d], identity=idb[:])
                last.then_inc(PE, 1)
                pe_val = C.add("pe")
                for d in range(n):
                    C.v[f"{tagp}_{d}"] = pe_val

            @block.scalar
            def _(scalar):
                scalar.wait_ge(PE, C.get(f"{tagp}_{n-1}"))
                if contig_dst is not None:
                    scalar.activation(out=contig_dst, in_=ps_tr[:, 0:n * 128],
                                      func=AF.Copy).then_inc(SC, 1)
                else:
                    last = None
                    for d in range(n):
                        last = scalar.activation(
                            out=dst_aps[d], in_=ps_tr[:, d * 128:(d + 1) * 128],
                            func=AF.Copy)
                    last.then_inc(SC, 1)
                C.v["last_trev"] = C.add("sc")
            C.v[f"{tagp}_last"] = C.get("last_trev")

        def emit_x_load(i):
            sl = i % 2
            tag = f"A{i}"

            @block.sync
            def _(sync, i=i, sl=sl, tag=tag):
                sync.wait_ge(VE, C.get(f"ve_epi_A{i-2}"))
                sync.dma_start(out=xs[:, sl * D:(sl + 1) * D],
                               in_=x_d[i * TA:(i + 1) * TA, :])\
                    .then_inc(LDX[sl], 16)
                C.v[f"ldx_{tag}"] = C.add(f"ldx{sl}", 16)
                C.v[f"ldxsl_{tag}"] = sl

        def emit_ln_A(i):
            sl = i % 2
            tag = f"A{i}"
            xn_guard = C.get(f"trA{i-2}_last_pe") if i >= 2 else 0
            ln_chunk(xs[:, sl * D:(sl + 1) * D], xn_sb[:, sl * D:(sl + 1) * D],
                     tag, f"A{i-1}", xn_guard)

        def emit_tr_A(i):
            sl = i % 2
            tag = f"A{i}"
            srcs = [xn_sb[:, sl * D + d * 128: sl * D + (d + 1) * 128]
                    for d in range(8)]
            dsts = [xnT_sb[:, sl * 1024 + d * 128: sl * 1024 + (d + 1) * 128]
                    for d in range(8)]
            emit_transposes2(srcs, dsts, C.get(f"ve_xn_{tag}"), f"trA{i}",
                             contig_dst=xnT_sb[:, sl * 1024:(sl + 1) * 1024])
            C.v[f"trA{i}_last_pe"] = C.get("pe")

        def emit_waves(i, wvs):
            sl = i % 2
            for wv in wvs:
                @block.tensor
                def _(tensor, i=i, sl=sl, wv=wv):
                    if wv == 0:
                        tensor.wait_ge(SC, C.get(f"trA{i}_last"))
                    if i == 0:
                        tensor.wait_ge(GP, gp_w_in_q[wv])
                    gk = (f"sc_wave_{i}_{wv-2}" if wv >= 2
                          else f"sc_wave_{i-1}_{wv+6}")
                    if C.get(gk):
                        tensor.wait_ge(SC, C.get(gk))
                    last = None
                    for mq in range(4):
                        m = wv * 4 + mq
                        for kt in range(8):
                            last = tensor.matmul(
                                out=pool[:, (wv % 2) * 512 + mq * 128:
                                         (wv % 2) * 512 + (mq + 1) * 128],
                                lhsT=w_in_kt(kt)[:, m * 128:(m + 1) * 128],
                                rhs=xnT_sb[:, sl * 1024 + kt * 128:
                                           sl * 1024 + (kt + 1) * 128],
                                start=(kt == 0), stop=(kt == 7))
                    last.then_inc(PE, 1)
                    C.v[f"pe_wave_{i}_{wv}"] = C.add("pe")

                @block.scalar
                def _(scalar, i=i, wv=wv):
                    scalar.wait_ge(PE, C.get(f"pe_wave_{i}_{wv}"))
                    if wv == 4 and C.get(f"pe_attn_done_{i-1}"):
                        scalar.wait_ge(PE, C.get(f"pe_attn_done_{i-1}"))
                    if wv <= 3 and C.get(f"cv_{i-1}_{4*wv+2}"):
                        scalar.wait_ge(VE, C.get(f"cv_{i-1}_{4*wv+2}"))
                    last = None
                    for mq in range(4):
                        m = wv * 4 + mq
                        src = pool[:, (wv % 2) * 512 + mq * 128:
                                   (wv % 2) * 512 + (mq + 1) * 128]
                        if m < 16:
                            last = scalar.activation(
                                out=xc[:, m * 131 + 3: m * 131 + 131],
                                in_=src, func=AF.Copy)
                        else:
                            last = scalar.activation(
                                out=siluz[:, (m - 16) * 128:(m - 15) * 128],
                                in_=src, func=AF.Silu)
                    last.then_inc(SC, 1)
                    C.v[f"sc_wave_{i}_{wv}"] = C.add("sc")

        def emit_conv(i, pairs):
            for r in pairs:
                def conv_body(eng, i=i, r=r):
                    eng.wait_ge(SC, C.get(f"sc_wave_{i}_{(r + 1) // 4}"))
                    if r == 0 and C.get(f"pe_fa_{i-1}"):
                        eng.wait_ge(PE, C.get(f"pe_fa_{i-1}"))
                    eng.drain()
                    cts = {}
                    for rr in (r, r + 1):
                        base = rr * 131
                        sa = 2 * (rr % 2)
                        cta = conv_t[:, sa * 128:(sa + 1) * 128]
                        ctb = conv_t[:, (sa + 1) * 128:(sa + 2) * 128]
                        cts[rr] = (cta, ctb)
                        eng.tensor_scalar(out=cta,
                                          in0=xc[:, base + 3: base + 131],
                                          scalar1=w_conv_sb[:, rr * 4 + 3: rr * 4 + 4],
                                          scalar2=b_conv_sb[:, rr:rr + 1],
                                          op0=AL.mult, op1=AL.add)
                        eng.tensor_scalar(out=ctb,
                                          in0=xc[:, base + 2: base + 130],
                                          scalar1=w_conv_sb[:, rr * 4 + 2: rr * 4 + 3],
                                          scalar2=None, op0=AL.mult)
                    eng.drain()
                    for rr in (r, r + 1):
                        base = rr * 131
                        cta, ctb = cts[rr]
                        eng.scalar_tensor_tensor(
                            out=cta, in0=xc[:, base + 1: base + 129],
                            scalar=w_conv_sb[:, rr * 4 + 1: rr * 4 + 2],
                            in1=cta, op0=AL.mult, op1=AL.add)
                        eng.scalar_tensor_tensor(
                            out=ctb, in0=xc[:, base: base + 128],
                            scalar=w_conv_sb[:, rr * 4: rr * 4 + 1],
                            in1=ctb, op0=AL.mult, op1=AL.add)
                    eng.drain()
                    for rr in (r, r + 1):
                        cta, ctb = cts[rr]
                        eng.tensor_tensor(out=f_all[:, rr * 128:(rr + 1) * 128],
                                          in0=cta, in1=ctb, op=AL.add)
                    last = None
                    for rr in (r, r + 1):
                        last = eng.tensor_copy(xc[:, rr * 131: rr * 131 + 3],
                                               xc[:, rr * 131 + 128: rr * 131 + 131])
                    last.then_inc(VE, 1)
                    C.v[f"cv_{i}_{r}"] = C.v[f"cv_{i}_{r+1}"] = C.add("ve")

                block.vector(conv_body)

                @block.scalar
                def _(scalar, i=i, r=r):
                    scalar.wait_ge(VE, C.get(f"cv_{i}_{r}"))
                    if r == 0 and C.get(f"pe_attn_done_{i-1}"):
                        scalar.wait_ge(PE, C.get(f"pe_attn_done_{i-1}"))
                    last = None
                    for rr in (r, r + 1):
                        last = scalar.activation(
                            out=xa_p[:, rr * 128:(rr + 1) * 128],
                            in_=f_all[:, rr * 128:(rr + 1) * 128],
                            func=AF.Silu)
                    last.then_inc(SC, 1)
                    C.v[f"sc_xa_{i}_{r}"] = C.v[f"sc_xa_{i}_{r+1}"] = C.add("sc")

        def emit_conv_prod(i):
            @block.vector
            def _(vector, i=i):
                vector.wait_ge(SC, C.get(f"sc_xa_{i}_14"))
                vector.tensor_tensor(out=siluz[:], in0=xa_p[:], in1=siluz[:],
                                     op=AL.mult).then_inc(VE, 1)
                C.v[f"ve_oa_{i}_15"] = C.add("ve")

        def emit_attn(i):
            @block.tensor
            def _(tensor, i=i):
                tensor.wait_ge(SC, C.get(f"sc_xa_{i}_15"))
                tensor.wait_ge(VE, C.get(f"ve_oa_{i}_15"))
                if i == 0:
                    tensor.wait_ge(GP, gp_w_outsel)
                if C.get(f"ve_atf_{i-1}"):
                    tensor.wait_ge(VE, C.get(f"ve_atf_{i-1}"))
                if C.get(f"ve_bcdev_{i-1}"):
                    tensor.wait_ge(VE, C.get(f"ve_bcdev_{i-1}"))
                if C.get(f"ve_epi_A{i-1}"):
                    tensor.wait_ge(VE, C.get(f"ve_epi_A{i-1}"))
                for r in range(16):
                    tensor.matmul(out=pool[0:32, 2048:2176],
                                  lhsT=w_sel_kt(r)[:, 0:32],
                                  rhs=xa_p[:, r * 128:(r + 1) * 128],
                                  start=(r == 0), stop=(r == 15))
                last = None
                for r in range(16):
                    last = tensor.matmul(out=pool[0:64, 2176:2304],
                                         lhsT=w_sel_kt(r)[:, 32:96],
                                         rhs=xa_p[:, r * 128:(r + 1) * 128],
                                         start=(r == 0), stop=(r == 15))
                last.then_inc(PE, 1)
                C.v[f"pe_bcd_{i}"] = C.add("pe")
                for o in range(8):
                    last = None
                    for r in range(16):
                        last = tensor.matmul(
                            out=pool[:, 1024 + o * 128: 1024 + (o + 1) * 128],
                            lhsT=w_out_kt(r)[:, o * 128:(o + 1) * 128],
                            rhs=siluz[:, r * 128:(r + 1) * 128],
                            start=(r == 0), stop=(r == 15))
                    last.then_inc(PE, 1)
                    C.v[f"pe_attn_{i}_{o}"] = C.add("pe")
                C.v[f"pe_attn_done_{i}"] = C.get("pe")

            @block.vector
            def _(vector, i=i):
                vector.wait_ge(PE, C.get(f"pe_bcd_{i}"))
                vector.tensor_copy(bcd_rhs[0:64, :], pool[0:64, 2176:2304])\
                      .then_inc(VE, 1)
                C.v[f"ve_bcdev_{i}"] = C.add("ve")

            @block.tensor
            def _(tensor, i=i):
                tensor.wait_ge(VE, C.get(f"ve_bcdev_{i}"))
                last = None
                for m in range(16):
                    last = tensor.matmul(out=pool[:, 2304:2432],
                                         lhsT=dtb_ap()[0:65, m * 128:(m + 1) * 128],
                                         rhs=bcd_rhs[0:65, :],
                                         start=True, stop=True)
                last.then_inc(PE, 1)
                C.v[f"pe_delta_{i}"] = C.add("pe")

        def emit_tail(i):
            sl = i % 2
            @block.vector
            def _(vector, i=i):
                last = None
                for o in range(8):
                    vector.wait_ge(PE, C.get(f"pe_attn_{i}_{(o // 4) * 4 + 3}"))
                    last = vector.tensor_copy(attn_fm[:, o * 128:(o + 1) * 128],
                                              pool[:, 1024 + o * 128:
                                                   1024 + (o + 1) * 128])
                last.then_inc(VE, 1)
                C.v[f"ve_atf_{i}"] = C.add("ve")

            @block.tensor
            def _(tensor, i=i):
                tensor.wait_ge(VE, C.get(f"ve_atf_{i}"))
                if C.get(f"ve_epi_A{i-1}"):
                    tensor.wait_ge(VE, C.get(f"ve_epi_A{i-1}"))
                last = None
                for o in range(8):
                    last = tensor.transpose(
                        out=pool[:, 2560 + o * 128: 2560 + (o + 1) * 128],
                        in_=attn_fm[:, o * 128:(o + 1) * 128],
                        identity=idf[:])
                last.then_inc(PE, 1)
                C.v[f"pe_fa_{i}"] = C.add("pe")

            @block.vector
            def _(vector, i=i, sl=sl):
                vector.wait_ge(PE, C.get(f"pe_fa_{i}"))
                if i >= 2:
                    vector.wait_ge(STA[sl], C.get(f"st_A{i-2}"))
                vector.tensor_tensor(out=out_sb[:, sl * D:(sl + 1) * D],
                                     in0=pool[:, 2560:3584],
                                     in1=xs[:, sl * D:(sl + 1) * D],
                                     op=AL.add).then_inc(VE, 1)
                C.v[f"ve_epi_A{i}"] = C.add("ve")

            @block.sync
            def _(sync, i=i, sl=sl):
                sync.wait_ge(VE, C.get(f"ve_epi_A{i}"))
                sync.dma_start(out=out_d[i * TA:(i + 1) * TA, :],
                               in_=out_sb[:, sl * D:(sl + 1) * D])\
                    .then_inc(STA[sl], 16)
                C.v[f"st_A{i}"] = C.add(f"sta{sl}", 16)

        emit_ln_A(0)
        emit_tr_A(0)
        emit_waves(0, (0, 1))
        def emit_x_load_B0(q):
            @block.sync
            def _(sync, q=q):
                sync.wait_ge(VE, C.get(f"ve_epi_A{14 + q}"))
                sync.dma_start(out=xs[:, q * D:(q + 1) * D],
                               in_=x_d[q * 128:(q + 1) * 128, :])\
                    .then_inc(LDX[q], 16)
                C.v[f"ldx_B0_{q}"] = C.add(f"ldx{q}", 16)
                C.v[f"ldxsl_B0_{q}"] = q

        for i in range(NCA + 1):
            if i >= 1:
                emit_tail(i - 1)
            if i == NCA:
                emit_x_load_B0(0)
                emit_x_load_B0(1)
                gpB0 = C.get("gp")
                extra1 = {0: [(GP, C.get("gp_wA_136"))],
                          1: [(GP, C.get("gp_wA_137"))],
                          2: [(VE, C.get(f"cv_{NCA-1}_14"))]}
                sccB0 = C.get("scc")
                stage_dmas(piecesB[:64], "wB", gpB0, slot_w1, 6, extra=extra1,
                           split_scc_base=sccB0)
                extra2 = {0: [(PE, C.get(f"pe_fa_{NCA-1}"))],
                          4: [(GP, gpB0 + 31)],
                          5: [(SCC, sccB0 + 31)]}
                stage_dmas(piecesB[64:], "wB2", gpB0 + 32, slot_w2, 6,
                           extra=extra2, split_scc_base=sccB0 + 32)
                break
            if 2 <= i + 1 < NCA:
                emit_x_load(i + 1)
            if i + 1 < NCA:
                emit_ln_A(i + 1)
            emit_waves(i, (2, 3, 4, 5, 6, 7))
            if i + 1 < NCA:
                emit_tr_A(i + 1)
            emit_conv(i, (0, 2, 4, 6, 8, 10))
            if i + 1 < NCA:
                emit_waves(i + 1, (0, 1))
            emit_conv(i, (12, 14))
            emit_conv_prod(i)
            emit_attn(i)

        gatesB1 = [C.get(f"pe_wave_{NCA-1}_{q}") for q in range(8)
                   for _k in range(8)]
        gatesB2 = [C.get(f"pe_delta_{NCA-1}")] * 40
        stage_casts(piecesB[:64], "wB", slot_w1, 6, gatesB1, split=True)
        stage_casts(piecesB[64:], "wB2", slot_w2, 6, gatesB2, split=True)
        gp_wB_q = [[(GP, C.get(f"gp_wB_{q * 8 + 6}")),
                    (SCC, C.get(f"scc_wB_{q * 8 + 7}"))] for q in range(8)]
        gp_wB = [(GP, C.get("gp")), (SCC, C.get("scc"))]

        CQ = consts[:, 0:1]
        CI64 = consts[:, 1:2]
        CM128 = consts[:, 2:3]
        CH = consts[:, 3:4]
        CB0 = consts[:, 4:5]

        for j in range(NCB):
            for q in range(2):
                sl = (2 * j + q) % 2
                tag = f"B{j}_{q}"

                if j > 0:
                    @block.sync
                    def _(sync, j=j, q=q, sl=sl, tag=tag):
                        gk = f"ve_xn_B{j-1}_{q}"
                        sync.wait_ge(VE, C.get(gk))
                        sync.dma_start(out=xs[:, sl * D:(sl + 1) * D],
                                       in_=x_d[j * TB + q * 128:
                                               j * TB + (q + 1) * 128, :])\
                            .then_inc(LDX[sl], 16)
                        C.v[f"ldx_{tag}"] = C.add(f"ldx{sl}", 16)
                        C.v[f"ldxsl_{tag}"] = sl

                xn_guard = (C.get(f"trB{j-1}_{q}_last_pe") if j >= 1
                            else C.get(f"trA{14 + q}_last_pe"))
                prev_var = (f"B{j}_{q-1}" if q == 1 else
                            (f"B{j-1}_1" if j >= 1 else "A15"))
                ln_chunk(xs[:, sl * D:(sl + 1) * D], xn_sb[:, q * D:(q + 1) * D],
                         tag, prev_var, xn_guard)

                srcs = [xn_sb[:, q * D + d * 128: q * D + (d + 1) * 128]
                        for d in range(8)]
                dsts = [xnT_sb[:, d * TB + q * 128: d * TB + (q + 1) * 128]
                        for d in range(8)]
                @block.tensor
                def _(tensor, j=j, q=q):
                    if j >= 1 and q == 0:
                        tensor.wait_ge(PE, C.get(f"pe_h_{j-1}_{NFM-1}"))
                emit_transposes2(srcs, dsts, C.get(f"ve_xn_{tag}"), f"trB{j}_{q}")
                C.v[f"trB{j}_{q}_last_pe"] = C.get("pe")

            def wq_s(i):
                return wq_p[:, (i % 2) * TB:(i % 2 + 1) * TB]

            def XqP(f):
                return Xq_p[:, (f % 4) * TB:(f % 4) * TB + 2 * TB]

            def accP(f, half):
                s = (f // 2) % 2
                return acc_p[:, s * 4 * TB + half * 2 * TB:
                             s * 4 * TB + (half + 1) * 2 * TB]

            wqP = wq_p[:, 0:2 * TB]
            kkP = kk_p[:, 0:2 * TB]
            ttP = tt_p[:, 0:2 * TB]
            k2P = k2_p[:, 0:2 * TB]
            s3P = s3_t[:, 0:2 * TB]
            xq2P = xq2_t[:, 0:2 * TB]

            for g in range(0, NFM + 6, 2):
                @block.tensor
                def _(tensor, j=j, g=g):
                    if g < NFM:
                        gk = (f"sc_w_{j}_{g-2}" if g >= 2
                              else f"sc_w_{j-1}_{g+NFM-2}")
                        if C.get(gk):
                            tensor.wait_ge(SC, C.get(gk))
                        if g == 0:
                            tensor.wait_ge(SC, C.get(f"trB{j}_1_last"))
                            if j == 0:
                                tensor.wait_ge(SC, C.get(f"sc_wave_{NCA-1}_7"))
                            if C.get(f"sc_w4_{j-1}_7"):
                                tensor.wait_ge(SC, C.get(f"sc_w4_{j-1}_7"))
                        if j == 0:
                            for s_, v_ in gp_wB_q[g // 4]:
                                tensor.wait_ge(s_, v_)
                        for fg in (g, g + 1):
                            hs = (fg % 2) * 512
                            last = None
                            for kt in range(8):
                                last = tensor.matmul(
                                    out=pool[:, hs: hs + TB],
                                    lhsT=w1_kt(kt)[:, fg * 128:(fg + 1) * 128],
                                    rhs=xnT_sb[:, kt * TB:(kt + 1) * TB],
                                    start=(kt == 0), stop=(kt == 7))
                            last.then_inc(PE, 1)
                            C.v[f"pe_h_{j}_{fg}"] = C.add("pe")

                @block.gpsimd
                def _(gps, j=j, g=g):
                    f3 = g - 4
                    if 0 <= f3 < NFM:
                        gps.wait_ge(VE, C.get(f"ve_p_{j}_{f3}"))
                        gps.tensor_copy(k2P, accP(f3, 0)).then_inc(GP, 1)
                        C.v[f"gp_k2_{j}_{f3}"] = C.v[f"gp_k2_{j}_{f3+1}"] = C.add("gp")

                @block.scalar
                def _(scalar, j=j, g=g):
                    if g < NFM:
                        if g >= 2:
                            scalar.wait_ge(VE, C.get(f"ve_X_{j}_{g-2}"))
                        elif C.get(f"ve_fq_{j-1}_6"):
                            scalar.wait_ge(VE, C.get(f"ve_fq_{j-1}_6"))
                        for fg in (g, g + 1):
                            scalar.wait_ge(PE, C.get(f"pe_h_{j}_{fg}"))
                            scalar.activation(out=wq_s(fg),
                                              in_=pool[:, (fg % 2) * 512:
                                                       (fg % 2) * 512 + TB],
                                              func=AF.Identity, scale=CQ,
                                              bias=c1_sb[:, fg:fg + 1])
                        scalar.drain()
                        scalar.copy(out=kkP, in_=wqP).then_inc(SC, 1)
                        C.v[f"sc_w_{j}_{g}"] = C.v[f"sc_w_{j}_{g+1}"] = C.add("sc")
                    f2 = g - 2
                    if 0 <= f2 < NFM:
                        if f2 >= 2:
                            scalar.wait_ge(VE, C.get(f"ve_p_{j}_{f2-2}"))
                        else:
                            scalar.wait_ge(VE, C.get(f"ve_X_{j}_{0}"))
                        scalar.activation(out=ttP, in_=XqP(f2),
                                          func=AF.Abs, scale=CI64, bias=CM128)
                        scalar.activation(out=accP(f2, 0), in_=XqP(f2),
                                          func=AF.Identity, scale=CH,
                                          bias=CB0).then_inc(SC, 1)
                        C.v[f"sc_t_{j}_{f2}"] = C.v[f"sc_t_{j}_{f2+1}"] = C.add("sc")

                @block.vector
                def _(vector, j=j, g=g):
                    f3 = g - 4
                    f2 = g - 2
                    have3 = 0 <= f3 < NFM
                    have1 = g < NFM
                    have2 = 0 <= f2 < NFM
                    if have3:
                        vector.wait_ge(GP, C.get(f"gp_k2_{j}_{f3}"))
                        if f3 >= 2:
                            vector.wait_ge(SC, C.get(f"sc_y_{j}_{f3-2}"))
                        elif C.get(f"sc_f_{j-1}_7"):
                            vector.wait_ge(SC, C.get(f"sc_f_{j-1}_7"))
                    if have3 and not have2:
                        vector.drain()
                        vector._custom_dve(QF1_OP, out=s3P, in0=accP(f3, 0),
                                           in1=k2P, s0=8.0, imm2=4.0)
                        vector.drain()
                        vector._custom_dve(QF2_OP, out=xq2P, in0=s3P,
                                           in1=accP(f3, 0), s0=2.0, s1=1.0)\
                              .then_inc(VE, 1)
                        C.v[f"ve_q_{j}_{f3}"] = C.v[f"ve_q_{j}_{f3+1}"] = C.add("ve")
                    if have1 and not have2:
                        vector.wait_ge(SC, C.get(f"sc_w_{j}_{g}"))
                        vector.drain()
                        vector._custom_dve(QF1_OP, out=s3P, in0=wqP,
                                           in1=kkP, s0=8.0, imm2=4.0)
                        vector.drain()
                        vector._custom_dve(QF2_OP, out=XqP(g), in0=s3P,
                                           in1=wqP, s0=2.0, s1=1.0)\
                              .then_inc(VE, 1)
                        C.v[f"ve_X_{j}_{g}"] = C.v[f"ve_X_{j}_{g+1}"] = C.add("ve")
                    if have2:
                        vector.wait_ge(SC, C.get(f"sc_t_{j}_{f2}"))
                        cur, nxt = accP(f2, 0), accP(f2, 1)
                        last = None
                        for k, (c, beta, alpha) in enumerate(PAIRS):
                            if k > 0:
                                vector.drain()
                            last = vector._custom_dve(
                                PAIR_OP, out=nxt, in0=ttP, in1=cur,
                                s0=float(c), s1=float(beta), imm2=float(alpha))
                            cur, nxt = nxt, cur
                            if k == 0 and have3:
                                vector._custom_dve(QF1_OP, out=s3P,
                                                   in0=accP(f3, 0), in1=k2P,
                                                   s0=8.0, imm2=4.0)
                            elif k == 1 and have3:
                                vector._custom_dve(QF2_OP, out=xq2P, in0=s3P,
                                                   in1=accP(f3, 0),
                                                   s0=2.0, s1=1.0)\
                                      .then_inc(VE, 1)
                                C.v[f"ve_q_{j}_{f3}"] = C.v[f"ve_q_{j}_{f3+1}"] \
                                    = C.add("ve")
                            elif k == 5 and have1:
                                vector.wait_ge(SC, C.get(f"sc_w_{j}_{g}"))
                                vector._custom_dve(QF1_OP, out=s3P, in0=wqP,
                                                   in1=kkP, s0=8.0, imm2=4.0)
                            elif k == 6 and have1:
                                vector._custom_dve(QF2_OP, out=XqP(g), in0=s3P,
                                                   in1=wqP, s0=2.0, s1=1.0)\
                                      .then_inc(VE, 1)
                                C.v[f"ve_X_{j}_{g}"] = C.v[f"ve_X_{j}_{g+1}"] \
                                    = C.add("ve")
                        last.then_inc(VE, 1)
                        C.v[f"ve_p_{j}_{f2}"] = C.v[f"ve_p_{j}_{f2+1}"] = C.add("ve")

                @block.scalar
                def _(scalar, j=j, g=g):
                    f3 = g - 4
                    if 0 <= f3 < NFM:
                        scalar.wait_ge(VE, C.get(f"ve_q_{j}_{f3}"))
                        if f3 == 0 and C.get(f"pe_w2_{j-1}_7"):
                            scalar.wait_ge(PE, C.get(f"pe_w2_{j-1}_7"))
                        scalar.activation(out=y_all[:, f3 * TB:(f3 + 2) * TB],
                                          in_=xq2P, func=AF.Identity,
                                          scale=1.0 / 256, bias=consts[:, 6:7])\
                              .then_inc(SC, 1)
                        C.v[f"sc_y_{j}_{f3}"] = C.v[f"sc_y_{j}_{f3+1}"] = C.add("sc")


            for o in range(8):
                @block.tensor
                def _(tensor, j=j, o=o):
                    if o == 0:
                        tensor.wait_ge(SC, C.get(f"sc_y_{j}_{NFM-1}"))
                        if C.get(f"sc_w4_{j-1}_7"):
                            tensor.wait_ge(SC, C.get(f"sc_w4_{j-1}_7"))
                        if j == 0:
                            tensor.wait_ge(VE, C.get(f"ve_epi_A{NCA-1}"))
                            for s_, v_ in gp_wB:
                                tensor.wait_ge(s_, v_)
                    last = None
                    for f3 in range(NFM):
                        last = tensor.matmul(
                            out=pool[:, 1024 + o * TB: 1024 + (o + 1) * TB],
                            lhsT=w2_kt(f3)[:, o * 128:(o + 1) * 128],
                            rhs=y_all[:, f3 * TB:(f3 + 1) * TB],
                            start=(f3 == 0), stop=(f3 == NFM - 1))
                    last.then_inc(PE, 1)
                    C.v[f"pe_w2_{j}_{o}"] = C.add("pe")

            for op in range(0, 8, 2):
                @block.scalar
                def _(scalar, j=j, op=op):
                    scalar.wait_ge(PE, C.get(f"pe_w2_{j}_{op+1}"))
                    if op >= 2:
                        scalar.wait_ge(VE, C.get(f"ve_fq_{j}_{op-2}"))
                    else:
                        scalar.wait_ge(VE, C.get(f"ve_X_{j}_{NFM-1}"))
                    for o in (op, op + 1):
                        scalar.activation(out=wq_s(o),
                                          in_=pool[:, 1024 + o * TB:
                                                   1024 + (o + 1) * TB],
                                          func=AF.Identity, scale=CQ,
                                          bias=c2_sb[:, o:o + 1])
                    scalar.drain()
                    scalar.copy(out=kkP, in_=wqP).then_inc(SC, 1)
                    C.v[f"sc_w4_{j}_{op}"] = C.v[f"sc_w4_{j}_{op+1}"] = C.add("sc")

                @block.vector
                def _(vector, j=j, op=op):
                    vector.wait_ge(SC, C.get(f"sc_w4_{j}_{op}"))
                    if op == 0:
                        vector.wait_ge(SC, C.get(f"sc_y_{j}_{NFM-1}"))
                    else:
                        vector.wait_ge(SC, C.get(f"sc_f_{j}_{op-2}"))
                    vector.drain()
                    vector._custom_dve(QF1_OP, out=s3P, in0=wqP,
                                       in1=kkP, s0=8.0, imm2=4.0)
                    vector.drain()
                    vector._custom_dve(QF2_OP, out=xq2P, in0=s3P,
                                       in1=wqP, s0=2.0, s1=1.0)\
                          .then_inc(VE, 1)
                    C.v[f"ve_fq_{j}_{op}"] = C.v[f"ve_fq_{j}_{op+1}"] = C.add("ve")

                @block.scalar
                def _(scalar, j=j, op=op):
                    scalar.wait_ge(VE, C.get(f"ve_fq_{j}_{op}"))
                    if op == 0 and j == 0:
                        scalar.wait_ge(GP, C.get("gp_wB2_38"))
                        scalar.wait_ge(SCC, C.get("scc_wB2_39"))
                    if op == 0 and C.get(f"pe_finB_{j-1}_1_1"):
                        scalar.wait_ge(PE, C.get(f"pe_finB_{j-1}_1_1"))
                    scalar.activation(out=f_all[:, op * TB:(op + 2) * TB],
                                      in_=xq2P, func=AF.Identity,
                                      scale=1.0 / 256, bias=consts[:, 7:8])\
                          .then_inc(SC, 1)
                    C.v[f"sc_f_{j}_{op}"] = C.v[f"sc_f_{j}_{op+1}"] = C.add("sc")

            for tt in range(2):
                a_chunk = 2 * j + tt
                @block.sync
                def _(sync, j=j, tt=tt, a_chunk=a_chunk):
                    sync.wait_ge(STA[a_chunk % 2], C.get(f"st_A{a_chunk}"))
                    if a_chunk == 0:
                        for s_, v_ in gp_wB:
                            sync.wait_ge(s_, v_)
                    if C.get(f"ve_outh_{2*j+tt-1}_done"):
                        sync.wait_ge(VE, C.get(f"ve_outh_{2*j+tt-1}_done"))
                    sync.dma_start(out=stage[:],
                                   in_=out_d[j * TB + tt * 128:
                                             j * TB + (tt + 1) * 128, :])\
                        .then_inc(LDO, 16)
                    C.v[f"ldo_{a_chunk}"] = C.add("ldo", 16)

                for half in range(2):
                    @block.tensor
                    def _(tensor, j=j, tt=tt, half=half):
                        tensor.wait_ge(SC, C.get(f"sc_f_{j}_7"))
                        gk = (f"ve_outh_{j}_{tt}_{half-1}" if half == 1 else
                              (f"ve_outh_{j}_{tt-1}_1" if tt == 1 else
                               f"ve_outh_{j-1}_1_1"))
                        if C.get(gk):
                            tensor.wait_ge(VE, C.get(gk))
                        last = None
                        for oo in range(4):
                            o = half * 4 + oo
                            last = tensor.transpose(
                                out=pool[:, 3072 + oo * 128: 3072 + (oo + 1) * 128],
                                in_=f_all[:, o * TB + tt * 128:
                                          o * TB + tt * 128 + 128],
                                identity=idf[:])
                        last.then_inc(PE, 1)
                        C.v[f"pe_finB_{j}_{tt}_{half}"] = C.add("pe")

                    @block.vector
                    def _(vector, j=j, tt=tt, half=half, a_chunk=a_chunk):
                        vector.wait_ge(PE, C.get(f"pe_finB_{j}_{tt}_{half}"))
                        sl = (2 * j + tt) % 2
                        if half == 0:
                            vector.wait_ge(LDO, C.get(f"ldo_{a_chunk}"))
                        if half == 0 and C.get(f"st_B{2*j+tt-2}"):
                            vector.wait_ge(STB[sl], C.get(f"st_B{2*j+tt-2}"))
                        elif half == 0:
                            vector.wait_ge(STA[sl], C.get(f"st_A{14 + sl}"))
                        vector.tensor_tensor(
                            out=out_sb[:, sl * D + half * 512: sl * D + (half + 1) * 512],
                            in0=pool[:, 3072:3584],
                            in1=stage[:, half * 512:(half + 1) * 512],
                            op=AL.add).then_inc(VE, 1)
                        C.v[f"ve_outh_{j}_{tt}_{half}"] = C.add("ve")
                        if half == 1:
                            C.v[f"ve_outh_{a_chunk}_done"] = C.get("ve")

                @block.sync
                def _(sync, j=j, tt=tt, a_chunk=a_chunk):
                    sync.wait_ge(VE, C.get(f"ve_outh_{j}_{tt}_1"))
                    sl = (2 * j + tt) % 2
                    sync.dma_start(out=out_d[j * TB + tt * 128:
                                             j * TB + (tt + 1) * 128, :],
                                   in_=out_sb[:, sl * D:(sl + 1) * D])\
                        .then_inc(STB[sl], 16)
                    C.v[f"st_B{a_chunk}"] = C.add(f"stb{sl}", 16)

        @block.vector
        def _(vector):
            vector.wait_ge(STA[0], C.get("sta0"))
            vector.wait_ge(STA[1], C.get("sta1"))
            vector.wait_ge(STB[0], C.get("stb0"))
            vector.wait_ge(STB[1], C.get("stb1"))

        block.emit(real_block)

    return nc


_NC_CACHE = None


def kernel(L0_attn_x, L0_attn_w_in, L0_attn_w_conv, L0_attn_b_conv,
           L0_attn_w_sel, L0_attn_w_dt, L0_attn_b_dt, L0_attn_w_out,
           L0_attn_A, L0_attn_Dp, L0_ffn_w1, L0_ffn_b1, L0_ffn_w2, L0_ffn_b2):
    global _NC_CACHE
    x = np.ascontiguousarray(np.asarray(L0_attn_x, dtype=np.float32))
    w_in = np.ascontiguousarray(np.asarray(L0_attn_w_in, np.float32))
    w_conv = np.ascontiguousarray(
        np.asarray(L0_attn_w_conv, np.float32).reshape(E, 4)
        .reshape(16, 128, 4).transpose(1, 0, 2).reshape(128, 64))
    b_conv = np.ascontiguousarray(
        np.asarray(L0_attn_b_conv, np.float32).reshape(16, 128).T)
    w_sel = np.ascontiguousarray(np.asarray(L0_attn_w_sel, np.float32))
    w_dtb = np.zeros((128, E), np.float32)
    w_dtb[0:64, :] = np.asarray(L0_attn_w_dt, np.float32)
    w_dtb[64, :] = np.asarray(L0_attn_b_dt, np.float32)
    w_out = np.ascontiguousarray(np.asarray(L0_attn_w_out, np.float32))
    w1 = np.ascontiguousarray(np.asarray(L0_ffn_w1, np.float32))
    b1 = np.ascontiguousarray(np.asarray(L0_ffn_b1, np.float32).reshape(32, 128).T)
    w2 = np.ascontiguousarray(np.asarray(L0_ffn_w2, np.float32))
    b2 = np.ascontiguousarray(np.asarray(L0_ffn_b2, np.float32).reshape(8, 128).T)
    idb = np.eye(128, dtype=ml_dtypes.bfloat16)
    idf = np.eye(128, dtype=np.float32)

    if _NC_CACHE is None:
        nc = build_nc()
        mybir.codegen_inst_isa_subclasses(nc)
        _NC_CACHE = nc
    nc = _NC_CACHE

    in_maps = []
    for c in range(8):
        in_maps.append({
            "x": np.ascontiguousarray(x[c]),
            "w_in": w_in, "w_conv": w_conv, "b_conv": b_conv,
            "w_sel": w_sel, "w_dtb": w_dtb, "w_out": w_out,
            "w1": w1, "b1": b1, "w2": w2, "b2": b2,
            "idb": idb, "idf": idf,
        })
    res = run_bass_kernel_spmd(nc, in_maps, core_ids=list(range(8)))
    out = np.stack([np.asarray(r["out"], dtype=np.float32)
                    for r in res.results], axis=0)
    return out


# version 32
# speedup vs baseline: 1.0316x; 1.0064x over previous
import math
import contextlib
import numpy as np
import ml_dtypes

import concourse.bass as bass
import concourse.mybir as mybir
from concourse.bass_utils import run_bass_kernel_spmd
from concourse.dve_op import DveOp
from concourse.dve_spec import (
    Spec, Src0, Src1, C0, C1, C2, Zero, One, maxx, minn, lower as dve_lower,
)
import concourse.dve_ops as dve_ops_mod
from concourse.dve_ops import OPS as DVE_OPS

F32 = mybir.dt.float32
BF16 = mybir.dt.bfloat16
I32 = mybir.dt.int32
AL = mybir.AluOpType
AF = mybir.ActivationFunctionType


def _register_op(name, spec, reference, replace=None):
    if name in dve_ops_mod._SUB_OPCODE_FOR_NAME:
        for o in DVE_OPS:
            if o.name == name:
                return o
        raise RuntimeError(f"{name}: row registered but op missing")
    shas = {}
    for ver in ("v3", "v4"):
        shas[ver] = dve_lower(spec, ver=ver, name=name).sha(ver)
    op = DveOp(name, spec, subdim=False, uops_sha=shas, reference=reference)
    if replace is not None:
        idx = next(i for i, o in enumerate(DVE_OPS) if o.name == replace)
        DVE_OPS[idx] = op
        row = dve_ops_mod._CUSTOM_DVE_ROW_BASE + idx
    else:
        DVE_OPS.append(op)
        row = dve_ops_mod._CUSTOM_DVE_ROW_BASE + len(DVE_OPS) - 1
    assert row < 0x20
    dve_ops_mod.CUSTOM_DVE_SPECS[name] = spec
    dve_ops_mod.CUSTOM_DVE_REFS[name] = reference
    dve_ops_mod._SUB_OPCODE_FOR_NAME[name] = row
    return op


_w = maxx(Src0, C0)
PAIR_OP = _register_op(
    "MK_PWLPAIR",
    Spec(body=(_w * C1 + Src1) + minn(_w, C0 + One) * C2),
    lambda in0, in1, s0, s1, imm2: (
        np.maximum(in0, s0) * s1 + in1
        + np.minimum(np.maximum(in0, s0), s0 + 1.0) * imm2),
)

_r = (Src0 - Src1) + C0
_g3 = _r >= C0
_s2 = _r - _g3 * C0
_g2 = _s2 >= C2
QF1_OP = _register_op(
    "MK_QF1",
    Spec(body=_s2 - _g2 * C2),
    lambda in0, in1, s0, s1, imm2: (
        lambda r: (lambda s2: s2 - (s2 >= imm2) * imm2)(r - (r >= s0) * s0)
    )((in0 - in1) + s0),
    replace="DVEHASH32L_SEED",
)

_gg1 = Src0 >= C0
_ss4 = Src0 - _gg1 * C0
_gg0 = _ss4 >= C1
QF2_OP = _register_op(
    "MK_QF2",
    Spec(body=Src1 - (_ss4 - _gg0)),
    lambda in0, in1, s0, s1, imm2: (
        lambda s4: in1 - (s4 - (s4 >= s1))
    )(in0 - (in0 >= s0) * s0),
    replace="DVEHASH32L_BODY",
)

FLOOR_OP = _register_op(
    "MK_FLOOR",
    Spec(body=((Src0 >= Src1) + Src1) - One),
    lambda in0, in1, s0, s1, imm2: ((in0 >= in1) + in1) - 1.0,
)


def _pwl_consts():
    SQ2 = math.sqrt(2.0)
    U = [256.0 * (j * 0.25) * 0.5 * math.erf((j * 0.25) / SQ2) for j in range(17)]
    sig = [U[j + 1] - U[j] for j in range(16)]
    pairs, kconst = [], 0.0
    for k in range(8):
        beta = sig[2 * k + 1] - (sig[2 * k - 1] if k > 0 else 0.0)
        alpha = sig[2 * k] - sig[2 * k + 1]
        c = 2.0 * k
        pairs.append((c, float(beta), float(alpha)))
        kconst += c * (beta + alpha)
    b0 = -4096.0 - kconst + 64.5
    return pairs, float(b0)


PAIRS, B0 = _pwl_consts()

D = 1024
E = 2048
FF = 4096
LTOK = 2048
TA = 128
NCA = LTOK // TA
TB = 256
NCB = LTOK // TB
NFM = FF // 128
DVE_CONV_ROWS = 16


class Ctr:

    def __init__(self):
        self.v = {}
        self.logs = {}
        self._rec = []
        self._rep = None

    def add(self, name, amt=1):
        if self._rep is not None:
            r = self._rep[0][self._rep[1]]
            self._rep[1] += 1
            return r
        self.v[name] = self.v.get(name, 0) + amt
        r = self.v[name]
        if self._rec:
            self._rec[-1].append(r)
        return r

    def get(self, name):
        if self._rep is not None:
            r = self._rep[0][self._rep[1]]
            self._rep[1] += 1
            return r
        r = self.v.get(name, 0)
        if self._rec:
            self._rec[-1].append(r)
        return r

    def begin_record(self, idx):
        lst = []
        self.logs[idx] = lst
        self._rec.append(lst)

    def end_record(self):
        self._rec.pop()

    def begin_replay(self, idx):
        self._rep = [self.logs[idx], 0]

    def end_replay(self, idx):
        assert self._rep[1] == len(self.logs[idx]), \
            f"replay divergence in body {idx}"
        self._rep = None


class _DummyRes:
    def then_inc(self, *a, **k):
        return self


class _DummyEng:
    def __getattr__(self, n):
        def f(*a, **k):
            return _DummyRes()
        return f


class _SecBlock:

    def __init__(self, C):
        self.C = C
        self.bodies = []

    def _deco(self, eng_name):
        def deco(fn):
            idx = len(self.bodies)
            self.bodies.append((eng_name, fn))
            self.C.begin_record(idx)
            fn(_DummyEng())
            self.C.end_record()
        return deco

    @property
    def sync(self):
        return self._deco("sync")

    @property
    def tensor(self):
        return self._deco("tensor")

    @property
    def scalar(self):
        return self._deco("scalar")

    @property
    def vector(self):
        return self._deco("vector")

    @property
    def gpsimd(self):
        return self._deco("gpsimd")

    def emit(self, real_block):
        for eng_name in ["sync", "tensor", "scalar", "vector", "gpsimd"]:
            todo = [(i, f) for i, (e, f) in enumerate(self.bodies)
                    if e == eng_name]
            if not todo:
                continue

            def run(engine, todo=todo):
                for i, f in todo:
                    self.C.begin_replay(i)
                    f(engine)
                    self.C.end_replay(i)

            getattr(real_block, eng_name)(run)


def build_nc():
    nc = bass.Bass("TRN2", target_bir_lowering=False,
                   dynamic_dma_scratch_size=2048)
    dtt = lambda n, s, t, k: nc.dram_tensor(n, s, t, kind=k).ap()
    x_d = dtt("x", [LTOK, D], F32, "ExternalInput")
    w_in_d = dtt("w_in", [D, 2 * E], F32, "ExternalInput")
    w_conv_d = dtt("w_conv", [128, 64], F32, "ExternalInput")
    b_conv_d = dtt("b_conv", [128, 16], F32, "ExternalInput")
    w_sel_d = dtt("w_sel", [E, 96], F32, "ExternalInput")
    w_dtb_d = dtt("w_dtb", [128, E], F32, "ExternalInput")
    w_out_d = dtt("w_out", [E, D], F32, "ExternalInput")
    w1_d = dtt("w1", [D, FF], F32, "ExternalInput")
    b1_d = dtt("b1", [128, 32], F32, "ExternalInput")
    w2_d = dtt("w2", [FF, D], F32, "ExternalInput")
    b2_d = dtt("b2", [128, 8], F32, "ExternalInput")
    idb_d = dtt("idb", [128, 128], BF16, "ExternalInput")
    idf_d = dtt("idf", [128, 128], F32, "ExternalInput")
    out_d = dtt("out", [LTOK, D], F32, "ExternalOutput")

    C = Ctr()
    es = contextlib.ExitStack()
    with es:
        def sb(name, shape, t=F32):
            return es.enter_context(nc.sbuf_tensor(name, shape, t))

        wbigA = sb("wbigA", [128, 8 * (2 * E)], BF16)
        wbigB = sb("wbigB", [128, 32 * D], BF16)
        xn_sb = sb("xn_sb", [128, 2 * D], BF16)
        xnT_sb = sb("xnT_sb", [128, 2 * D], BF16)
        siluz = sb("siluz", [128, E], BF16)
        xa_p = sb("xa_p", [128, 16 * 128], BF16)
        bcd_rhs = sb("bcd_rhs", [128, 128], BF16)
        y_all = sb("y_all", [128, NFM * TB], BF16)
        idb = sb("idb_sb", [128, 128], BF16)

        xs = sb("xs", [128, 2 * D], F32)
        xc = sb("xc", [128, 16 * 131], F32)
        wq_p = sb("wq_p", [128, 2 * TB], F32)
        Xq_p = sb("Xq_p", [128, 4 * TB], F32)
        tt_p = sb("tt_p", [128, 2 * TB], F32)
        acc_p = sb("acc_p", [128, 8 * TB], F32)
        xq2_t = sb("xq2_t", [128, 2 * TB], F32)
        conv_t = sb("conv_t", [128, 4 * 128], F32)
        s3_t = conv_t
        out_sb = sb("out_sb", [128, 2 * D], F32)
        f_all = sb("f_all", [128, 8 * TB], F32)
        attn_fm = f_all
        stage = sb("stage", [128, 1024], F32)
        idf = sb("idf_sb", [128, 128], F32)
        smalls = sb("smalls", [128, 16], F32)
        kk_p = es.enter_context(nc.sbuf_tensor("kk_p", [128, 2 * TB], I32))
        k2_p = es.enter_context(nc.sbuf_tensor("k2_p", [128, 2 * TB], I32))
        w_conv_sb = sb("w_conv_sb", [128, 64], F32)
        b_conv_sb = sb("b_conv_sb", [128, 16], F32)
        c1_sb = sb("c1_sb", [128, 32], F32)
        c2_sb = sb("c2_sb", [128, 8], F32)
        consts = sb("consts", [128, 8], F32)

        pool = es.enter_context(nc.psum_tensor("pool", [128, 3584], F32))
        ps_tr = es.enter_context(nc.psum_tensor("ps_tr", [128, 1024], BF16))

        sems = {}
        for sname in ["ldsm", "ldx0", "ldx1", "lds0", "lds1", "ldo",
                      "pe", "sc", "ve", "gp", "sta0", "sta1", "stb0", "stb1",
                      "lds2", "lds3", "lds4", "lds5", "lds6", "lds7",
                      "scc"]:
            sems[sname] = es.enter_context(nc.semaphore(sname))
        real_block = es.enter_context(nc.Block())
        block = _SecBlock(C)
        LDSM = sems["ldsm"]
        LDX = [sems["ldx0"], sems["ldx1"]]
        LDS = [sems[f"lds{k}"] for k in range(8)]
        PE, SC, VE, GP = sems["pe"], sems["sc"], sems["ve"], sems["gp"]
        SCC = sems["scc"]
        STA = [sems["sta0"], sems["sta1"]]
        STB = [sems["stb0"], sems["stb1"]]
        LDO = sems["ldo"]

        def w_in_kt(k):
            return wbigA[:, k * (2 * E):(k + 1) * (2 * E)]

        w1_kt = w_in_kt

        def w_out_kt(k):
            return wbigB[:, k * D:(k + 1) * D]

        w2_kt = w_out_kt

        def w_sel_kt(k):
            return wbigB[:, 16 * D + k * 96: 16 * D + (k + 1) * 96]

        dtb_off = 16 * D + 16 * 96

        def dtb_ap():
            return wbigB[:, dtb_off: dtb_off + E]

        @block.sync
        def _(sync):
            for (dst, src) in [(idb[:], idb_d[:]), (idf[:], idf_d[:]),
                               (w_conv_sb[:], w_conv_d[:]),
                               (b_conv_sb[:], b_conv_d[:]),
                               (out_sb[:, 0:32], b1_d[:]),
                               (out_sb[:, 32:40], b2_d[:])]:
                sync.dma_start(out=dst, in_=src).then_inc(LDSM, 16)
                C.add("ldsm", 16)
        ld_small = C.get("ldsm")

        @block.vector
        def _(vector):
            for col, val in [(0, 256.0), (1, 1.0 / 64), (2, -128.0),
                             (3, 0.5), (4, B0), (5, 1e-5),
                             (6, -0.25), (7, -32.0)]:
                vector.memset(consts[:, col:col + 1], float(val))
            vector.memset(bcd_rhs[64:65, :], 1.0)
            for r in range(16):
                vector.memset(xc[:, r * 131: r * 131 + 3], 0.0)
            vector.wait_ge(LDSM, ld_small)
            vector.tensor_scalar(out=c1_sb[:], in0=out_sb[:, 0:32], scalar1=256.0,
                                 scalar2=8192.5, op0=AL.mult, op1=AL.add)
            vector.tensor_scalar(out=c2_sb[:], in0=out_sb[:, 32:40], scalar1=256.0,
                                 scalar2=8192.5, op0=AL.mult, op1=AL.add)\
                  .then_inc(VE, 1)
            C.add("ve")
        ve_consts = C.get("ve")

        @block.scalar
        def _(scalar):
            scalar.wait_ge(VE, ve_consts)
            scalar.wait_ge(LDSM, ld_small)

        @block.tensor
        def _(tensor):
            tensor.wait_ge(VE, ve_consts)
            tensor.wait_ge(LDSM, ld_small)

        @block.gpsimd
        def _(g):
            g.wait_ge(VE, ve_consts)
            g.wait_ge(LDSM, ld_small)

        for i0 in range(2):
            @block.sync
            def _(sync, i=i0):
                sl = i % 2
                sync.dma_start(out=xs[:, sl * D:(sl + 1) * D],
                               in_=x_d[i * TA:(i + 1) * TA, :])\
                    .then_inc(LDX[sl], 16)
                C.v[f"ldx_A{i}"] = C.add(f"ldx{sl}", 16)
                C.v[f"ldxsl_A{i}"] = sl

        def slot_wA(idx, cols):
            s = idx % 8
            if s < 2:
                return stage[:, s * 512: s * 512 + cols]
            if s < 6:
                return acc_p[:, (s - 2) * 512: (s - 2) * 512 + cols]
            return Xq_p[:, (s - 6) * 512: (s - 6) * 512 + cols]

        def slot_w1(idx, cols):
            s = idx % 6
            if s < 2:
                return stage[:, s * 512: s * 512 + cols]
            return xc[:, (s - 2) * 512: (s - 2) * 512 + cols]

        def slot_w2(idx, cols):
            s = idx % 6
            if s < 4:
                return f_all[:, s * 512: s * 512 + cols]
            return stage[:, (s - 4) * 512: (s - 4) * 512 + cols]

        def stage_dmas(pieces, tag, gp_base, slot_fn, nslot, extra=None,
                       split_scc_base=None):
            for idx, (src_ap, dst_ap, cols, pe_gate) in enumerate(pieces):
                @block.sync
                def _(sync, src_ap=src_ap, idx=idx, cols=cols, tag=tag,
                      gp_base=gp_base, slot_fn=slot_fn, nslot=nslot,
                      extra=extra, split_scc_base=split_scc_base):
                    s = idx % nslot
                    prev = C.get(f"lds{s}")
                    if prev > 0:
                        sync.wait_ge(LDS[s], prev)
                    if idx >= nslot:
                        k = idx - nslot
                        if split_scc_base is None:
                            sync.wait_ge(GP, gp_base + k + 1)
                        elif k % 2 == 0:
                            sync.wait_ge(GP, gp_base + k // 2 + 1)
                        else:
                            sync.wait_ge(SCC, split_scc_base + (k + 1) // 2)
                    if extra is not None and idx in extra:
                        for sem, val in extra[idx]:
                            sync.wait_ge(sem, val)
                    sync.dma_start(out=slot_fn(idx, cols), in_=src_ap)\
                        .then_inc(LDS[s], 16)
                    C.v[f"ld_{tag}_{idx}"] = C.add(f"lds{s}", 16)

        def stage_casts(pieces, tag, slot_fn, nslot, gates=None, split=False):
            @block.gpsimd
            def _(g, tag=tag, slot_fn=slot_fn, nslot=nslot):
                for idx, (src_ap, dst_ap, cols, pe_gate) in enumerate(pieces):
                    if split and idx % 2 == 1:
                        continue
                    if gates is not None:
                        pe_gate = gates[idx]
                    g.wait_ge(LDS[idx % nslot], C.get(f"ld_{tag}_{idx}"))
                    if pe_gate:
                        g.wait_ge(PE, pe_gate)
                    g.tensor_copy(dst_ap, slot_fn(idx, cols)).then_inc(GP, 1)
                    C.v[f"gp_{tag}_{idx}"] = C.add("gp")
            if not split:
                return

            @block.scalar
            def _(scalar, tag=tag, slot_fn=slot_fn, nslot=nslot):
                for idx, (src_ap, dst_ap, cols, pe_gate) in enumerate(pieces):
                    if idx % 2 == 0:
                        continue
                    if gates is not None:
                        pe_gate = gates[idx]
                    scalar.wait_ge(LDS[idx % nslot], C.get(f"ld_{tag}_{idx}"))
                    if pe_gate:
                        scalar.wait_ge(PE, pe_gate)
                    scalar.copy(out=dst_ap, in_=slot_fn(idx, cols))\
                          .then_inc(SCC, 1)
                    C.v[f"scc_{tag}_{idx}"] = C.add("scc")

        def stage_pieces(pieces, tag):
            stage_dmas(pieces, tag, C.get("gp"), slot_wA, 8)
            stage_casts(pieces, tag, slot_wA, 8)

        piecesA = []
        for q in range(8):
            for k in range(8):
                piecesA.append((w_in_d[k * 128:(k + 1) * 128, q * 512:(q + 1) * 512],
                                w_in_kt(k)[:, q * 512:(q + 1) * 512], 512, 0))
        nw_in = len(piecesA)
        for k in range(16):
            for q in range(2):
                piecesA.append((w_out_d[k * 128:(k + 1) * 128, q * 512:(q + 1) * 512],
                                w_out_kt(k)[:, q * 512:(q + 1) * 512], 512, 0))
        for k in range(16):
            piecesA.append((w_sel_d[k * 128:(k + 1) * 128, :], w_sel_kt(k), 96, 0))
        for q in range(4):
            piecesA.append((w_dtb_d[:, q * 512:(q + 1) * 512],
                            dtb_ap()[:, q * 512:(q + 1) * 512], 512, 0))
        n_outsel = len(piecesA)
        for k in range(20, 32):
            for q in range(2):
                piecesA.append((w2_d[k * 128:(k + 1) * 128, q * 512:(q + 1) * 512],
                                w2_kt(k)[:, q * 512:(q + 1) * 512], 512, 0))
        stage_pieces(piecesA, "wA")
        gp_w_in_q = [C.get(f"gp_wA_{q * 8 + 7}") for q in range(8)]
        gp_w_in = C.get(f"gp_wA_{nw_in - 1}")
        gp_w_outsel = C.get(f"gp_wA_{n_outsel - 1}")

        piecesB = []
        for q in range(8):
            for k in range(8):
                piecesB.append((w1_d[k * 128:(k + 1) * 128, q * 512:(q + 1) * 512],
                                w1_kt(k)[:, q * 512:(q + 1) * 512], 512, 0))
        for k in range(20):
            for q in range(2):
                piecesB.append((w2_d[k * 128:(k + 1) * 128, q * 512:(q + 1) * 512],
                                w2_kt(k)[:, q * 512:(q + 1) * 512], 512, 0))

        def ln_chunk(ptile_ap, xn_dst_ap, tag, prev_var_tag, xn_guard):
            mu = smalls[:, 0:1]; var = smalls[:, 1:2]
            sd = smalls[:, 7:8]; rstd = smalls[:, 6:7]

            @block.vector
            def _(vector):
                vector.wait_ge(LDX[C.get(f"ldxsl_{tag}")], C.get(f"ldx_{tag}"))
                if C.get(f"sc_rstd_{prev_var_tag}"):
                    vector.wait_ge(SC, C.get(f"sc_rstd_{prev_var_tag}"))
                vector.drain()
                vector.bn_stats(out=smalls[:, 4:10], in_=ptile_ap[:, 0:512])
                vector.bn_stats(out=smalls[:, 10:16], in_=ptile_ap[:, 512:1024])
                vector.drain()
                vector.bn_aggr(out=smalls[:, 0:2], in_=smalls[:, 4:16])\
                      .then_inc(VE, 1)
                C.v[f"ve_var_{tag}"] = C.add("ve")

            @block.scalar
            def _(scalar):
                scalar.wait_ge(VE, C.get(f"ve_var_{tag}"))
                scalar.activation(out=sd, in_=var, func=AF.Sqrt,
                                  bias=consts[:, 5:6]).then_inc(SC, 1)
                C.v[f"sc_rstd_{tag}"] = C.add("sc")

            @block.vector
            def _(vector):
                vector.wait_ge(SC, C.get(f"sc_rstd_{tag}"))
                vector.reciprocal(rstd, sd)
                vector.drain()
                if xn_guard:
                    vector.wait_ge(PE, xn_guard)
                vector.tensor_scalar(out=xn_dst_ap, in0=ptile_ap, scalar1=mu,
                                     scalar2=rstd, op0=AL.subtract, op1=AL.mult)\
                      .then_inc(VE, 1)
                C.v[f"ve_xn_{tag}"] = C.add("ve")

        def emit_transposes2(src_aps, dst_aps, wait_ve, tagp, contig_dst=None):
            n = len(src_aps)

            @block.tensor
            def _(tensor):
                tensor.wait_ge(VE, wait_ve)
                if C.get("last_trev"):
                    tensor.wait_ge(SC, C.get("last_trev"))
                last = None
                for d in range(n):
                    last = tensor.transpose(out=ps_tr[:, d * 128:(d + 1) * 128],
                                            in_=src_aps[d], identity=idb[:])
                last.then_inc(PE, 1)
                pe_val = C.add("pe")
                for d in range(n):
                    C.v[f"{tagp}_{d}"] = pe_val

            @block.scalar
            def _(scalar):
                scalar.wait_ge(PE, C.get(f"{tagp}_{n-1}"))
                if contig_dst is not None:
                    scalar.activation(out=contig_dst, in_=ps_tr[:, 0:n * 128],
                                      func=AF.Copy).then_inc(SC, 1)
                else:
                    last = None
                    for d in range(n):
                        last = scalar.activation(
                            out=dst_aps[d], in_=ps_tr[:, d * 128:(d + 1) * 128],
                            func=AF.Copy)
                    last.then_inc(SC, 1)
                C.v["last_trev"] = C.add("sc")
            C.v[f"{tagp}_last"] = C.get("last_trev")

        def emit_x_load(i):
            sl = i % 2
            tag = f"A{i}"

            @block.sync
            def _(sync, i=i, sl=sl, tag=tag):
                sync.wait_ge(VE, C.get(f"ve_epi_A{i-2}"))
                sync.dma_start(out=xs[:, sl * D:(sl + 1) * D],
                               in_=x_d[i * TA:(i + 1) * TA, :])\
                    .then_inc(LDX[sl], 16)
                C.v[f"ldx_{tag}"] = C.add(f"ldx{sl}", 16)
                C.v[f"ldxsl_{tag}"] = sl

        def emit_ln_A(i):
            sl = i % 2
            tag = f"A{i}"
            xn_guard = C.get(f"trA{i-2}_last_pe") if i >= 2 else 0
            ln_chunk(xs[:, sl * D:(sl + 1) * D], xn_sb[:, sl * D:(sl + 1) * D],
                     tag, f"A{i-1}", xn_guard)

        def emit_tr_A(i):
            sl = i % 2
            tag = f"A{i}"
            srcs = [xn_sb[:, sl * D + d * 128: sl * D + (d + 1) * 128]
                    for d in range(8)]
            dsts = [xnT_sb[:, sl * 1024 + d * 128: sl * 1024 + (d + 1) * 128]
                    for d in range(8)]
            emit_transposes2(srcs, dsts, C.get(f"ve_xn_{tag}"), f"trA{i}",
                             contig_dst=xnT_sb[:, sl * 1024:(sl + 1) * 1024])
            C.v[f"trA{i}_last_pe"] = C.get("pe")

        def emit_waves(i, wvs):
            sl = i % 2
            for wv in wvs:
                @block.tensor
                def _(tensor, i=i, sl=sl, wv=wv):
                    if wv == 0:
                        tensor.wait_ge(SC, C.get(f"trA{i}_last"))
                    if i == 0:
                        tensor.wait_ge(GP, gp_w_in_q[wv])
                    gk = (f"sc_wave_{i}_{wv-2}" if wv >= 2
                          else f"sc_wave_{i-1}_{wv+6}")
                    if C.get(gk):
                        tensor.wait_ge(SC, C.get(gk))
                    last = None
                    for mq in range(4):
                        m = wv * 4 + mq
                        for kt in range(8):
                            last = tensor.matmul(
                                out=pool[:, (wv % 2) * 512 + mq * 128:
                                         (wv % 2) * 512 + (mq + 1) * 128],
                                lhsT=w_in_kt(kt)[:, m * 128:(m + 1) * 128],
                                rhs=xnT_sb[:, sl * 1024 + kt * 128:
                                           sl * 1024 + (kt + 1) * 128],
                                start=(kt == 0), stop=(kt == 7))
                    last.then_inc(PE, 1)
                    C.v[f"pe_wave_{i}_{wv}"] = C.add("pe")

                @block.scalar
                def _(scalar, i=i, wv=wv):
                    scalar.wait_ge(PE, C.get(f"pe_wave_{i}_{wv}"))
                    if wv == 4 and C.get(f"pe_attn_done_{i-1}"):
                        scalar.wait_ge(PE, C.get(f"pe_attn_done_{i-1}"))
                    if wv <= 3 and C.get(f"cv_{i-1}_{4*wv+2}"):
                        scalar.wait_ge(VE, C.get(f"cv_{i-1}_{4*wv+2}"))
                    last = None
                    for mq in range(4):
                        m = wv * 4 + mq
                        src = pool[:, (wv % 2) * 512 + mq * 128:
                                   (wv % 2) * 512 + (mq + 1) * 128]
                        if m < 16:
                            last = scalar.activation(
                                out=xc[:, m * 131 + 3: m * 131 + 131],
                                in_=src, func=AF.Copy)
                        else:
                            last = scalar.activation(
                                out=siluz[:, (m - 16) * 128:(m - 15) * 128],
                                in_=src, func=AF.Silu)
                    last.then_inc(SC, 1)
                    C.v[f"sc_wave_{i}_{wv}"] = C.add("sc")

        def emit_conv(i, pairs):
            for r in pairs:
                def conv_body(eng, i=i, r=r):
                    eng.wait_ge(SC, C.get(f"sc_wave_{i}_{(r + 1) // 4}"))
                    if r == 0 and C.get(f"pe_fa_{i-1}"):
                        eng.wait_ge(PE, C.get(f"pe_fa_{i-1}"))
                    eng.drain()
                    cts = {}
                    for rr in (r, r + 1):
                        base = rr * 131
                        sa = 2 * (rr % 2)
                        cta = conv_t[:, sa * 128:(sa + 1) * 128]
                        ctb = conv_t[:, (sa + 1) * 128:(sa + 2) * 128]
                        cts[rr] = (cta, ctb)
                        eng.tensor_scalar(out=cta,
                                          in0=xc[:, base + 3: base + 131],
                                          scalar1=w_conv_sb[:, rr * 4 + 3: rr * 4 + 4],
                                          scalar2=b_conv_sb[:, rr:rr + 1],
                                          op0=AL.mult, op1=AL.add)
                        eng.tensor_scalar(out=ctb,
                                          in0=xc[:, base + 2: base + 130],
                                          scalar1=w_conv_sb[:, rr * 4 + 2: rr * 4 + 3],
                                          scalar2=None, op0=AL.mult)
                    eng.drain()
                    for rr in (r, r + 1):
                        base = rr * 131
                        cta, ctb = cts[rr]
                        eng.scalar_tensor_tensor(
                            out=cta, in0=xc[:, base + 1: base + 129],
                            scalar=w_conv_sb[:, rr * 4 + 1: rr * 4 + 2],
                            in1=cta, op0=AL.mult, op1=AL.add)
                        eng.scalar_tensor_tensor(
                            out=ctb, in0=xc[:, base: base + 128],
                            scalar=w_conv_sb[:, rr * 4: rr * 4 + 1],
                            in1=ctb, op0=AL.mult, op1=AL.add)
                    eng.drain()
                    for rr in (r, r + 1):
                        cta, ctb = cts[rr]
                        eng.tensor_tensor(out=f_all[:, rr * 128:(rr + 1) * 128],
                                          in0=cta, in1=ctb, op=AL.add)
                    last = None
                    for rr in (r, r + 1):
                        last = eng.tensor_copy(xc[:, rr * 131: rr * 131 + 3],
                                               xc[:, rr * 131 + 128: rr * 131 + 131])
                    last.then_inc(VE, 1)
                    C.v[f"cv_{i}_{r}"] = C.v[f"cv_{i}_{r+1}"] = C.add("ve")

                block.vector(conv_body)

                @block.scalar
                def _(scalar, i=i, r=r):
                    scalar.wait_ge(VE, C.get(f"cv_{i}_{r}"))
                    if r == 0 and C.get(f"pe_attn_done_{i-1}"):
                        scalar.wait_ge(PE, C.get(f"pe_attn_done_{i-1}"))
                    last = None
                    for rr in (r, r + 1):
                        last = scalar.activation(
                            out=xa_p[:, rr * 128:(rr + 1) * 128],
                            in_=f_all[:, rr * 128:(rr + 1) * 128],
                            func=AF.Silu)
                    last.then_inc(SC, 1)
                    C.v[f"sc_xa_{i}_{r}"] = C.v[f"sc_xa_{i}_{r+1}"] = C.add("sc")

        def emit_conv_prod(i):
            for c in range(4):
                @block.vector
                def _(vector, i=i, c=c):
                    vector.wait_ge(SC, C.get(f"sc_xa_{i}_{4*c+2}"))
                    vector.tensor_tensor(
                        out=siluz[:, c * 512:(c + 1) * 512],
                        in0=xa_p[:, c * 512:(c + 1) * 512],
                        in1=siluz[:, c * 512:(c + 1) * 512],
                        op=AL.mult).then_inc(VE, 1)
                    v = C.add("ve")
                    for rr in range(4 * c, 4 * c + 4):
                        C.v[f"ve_oa_{i}_{rr}"] = v

        def emit_attn(i):
            @block.tensor
            def _(tensor, i=i):
                if i == 0:
                    tensor.wait_ge(GP, gp_w_outsel)
                if C.get(f"ve_atf_{i-1}"):
                    tensor.wait_ge(VE, C.get(f"ve_atf_{i-1}"))
                if C.get(f"ve_bcdev_{i-1}"):
                    tensor.wait_ge(VE, C.get(f"ve_bcdev_{i-1}"))
                if C.get(f"ve_epi_A{i-1}"):
                    tensor.wait_ge(VE, C.get(f"ve_epi_A{i-1}"))
                for r in range(16):
                    if r % 2 == 0:
                        tensor.wait_ge(SC, C.get(f"sc_xa_{i}_{r}"))
                    tensor.matmul(out=pool[0:32, 2048:2176],
                                  lhsT=w_sel_kt(r)[:, 0:32],
                                  rhs=xa_p[:, r * 128:(r + 1) * 128],
                                  start=(r == 0), stop=(r == 15))
                last = None
                for r in range(16):
                    last = tensor.matmul(out=pool[0:64, 2176:2304],
                                         lhsT=w_sel_kt(r)[:, 32:96],
                                         rhs=xa_p[:, r * 128:(r + 1) * 128],
                                         start=(r == 0), stop=(r == 15))
                last.then_inc(PE, 1)
                C.v[f"pe_bcd_{i}"] = C.add("pe")
                for pp in range(4):
                    last = None
                    for r in range(16):
                        if pp == 0 and r % 4 == 0:
                            tensor.wait_ge(VE, C.get(f"ve_oa_{i}_{r}"))
                        tensor.matmul(
                            out=pool[:, 1024 + pp * 128: 1024 + (pp + 1) * 128],
                            lhsT=w_out_kt(r)[:, pp * 128:(pp + 1) * 128],
                            rhs=siluz[:, r * 128:(r + 1) * 128],
                            start=(r == 0), stop=(r == 15))
                        last = tensor.matmul(
                            out=pool[:, 1536 + pp * 128: 1536 + (pp + 1) * 128],
                            lhsT=w_out_kt(r)[:, (pp + 4) * 128:(pp + 5) * 128],
                            rhs=siluz[:, r * 128:(r + 1) * 128],
                            start=(r == 0), stop=(r == 15))
                    last.then_inc(PE, 1)
                    v = C.add("pe")
                    C.v[f"pe_attn_{i}_{pp}"] = C.v[f"pe_attn_{i}_{pp+4}"] = v
                C.v[f"pe_attn_done_{i}"] = C.get("pe")

            @block.vector
            def _(vector, i=i):
                vector.wait_ge(PE, C.get(f"pe_bcd_{i}"))
                vector.tensor_copy(bcd_rhs[0:64, :], pool[0:64, 2176:2304])\
                      .then_inc(VE, 1)
                C.v[f"ve_bcdev_{i}"] = C.add("ve")

            @block.tensor
            def _(tensor, i=i):
                tensor.wait_ge(VE, C.get(f"ve_bcdev_{i}"))
                last = None
                for m in range(16):
                    last = tensor.matmul(out=pool[:, 2304:2432],
                                         lhsT=dtb_ap()[0:65, m * 128:(m + 1) * 128],
                                         rhs=bcd_rhs[0:65, :],
                                         start=True, stop=True)
                last.then_inc(PE, 1)
                C.v[f"pe_delta_{i}"] = C.add("pe")

        def emit_tail(i):
            sl = i % 2
            @block.vector
            def _(vector, i=i):
                last = None
                for o in range(8):
                    vector.wait_ge(PE, C.get(f"pe_attn_{i}_{(o // 4) * 4 + 3}"))
                    last = vector.tensor_copy(attn_fm[:, o * 128:(o + 1) * 128],
                                              pool[:, 1024 + o * 128:
                                                   1024 + (o + 1) * 128])
                last.then_inc(VE, 1)
                C.v[f"ve_atf_{i}"] = C.add("ve")

            @block.tensor
            def _(tensor, i=i):
                tensor.wait_ge(VE, C.get(f"ve_atf_{i}"))
                if C.get(f"ve_epi_A{i-1}"):
                    tensor.wait_ge(VE, C.get(f"ve_epi_A{i-1}"))
                last = None
                for o in range(8):
                    last = tensor.transpose(
                        out=pool[:, 2560 + o * 128: 2560 + (o + 1) * 128],
                        in_=attn_fm[:, o * 128:(o + 1) * 128],
                        identity=idf[:])
                last.then_inc(PE, 1)
                C.v[f"pe_fa_{i}"] = C.add("pe")

            @block.vector
            def _(vector, i=i, sl=sl):
                vector.wait_ge(PE, C.get(f"pe_fa_{i}"))
                if i >= 2:
                    vector.wait_ge(STA[sl], C.get(f"st_A{i-2}"))
                vector.tensor_tensor(out=out_sb[:, sl * D:(sl + 1) * D],
                                     in0=pool[:, 2560:3584],
                                     in1=xs[:, sl * D:(sl + 1) * D],
                                     op=AL.add).then_inc(VE, 1)
                C.v[f"ve_epi_A{i}"] = C.add("ve")

            @block.sync
            def _(sync, i=i, sl=sl):
                sync.wait_ge(VE, C.get(f"ve_epi_A{i}"))
                sync.dma_start(out=out_d[i * TA:(i + 1) * TA, :],
                               in_=out_sb[:, sl * D:(sl + 1) * D])\
                    .then_inc(STA[sl], 16)
                C.v[f"st_A{i}"] = C.add(f"sta{sl}", 16)

        emit_ln_A(0)
        emit_tr_A(0)
        emit_waves(0, (0, 1))
        def emit_x_load_B0(q):
            @block.sync
            def _(sync, q=q):
                sync.wait_ge(VE, C.get(f"ve_epi_A{14 + q}"))
                sync.dma_start(out=xs[:, q * D:(q + 1) * D],
                               in_=x_d[q * 128:(q + 1) * 128, :])\
                    .then_inc(LDX[q], 16)
                C.v[f"ldx_B0_{q}"] = C.add(f"ldx{q}", 16)
                C.v[f"ldxsl_B0_{q}"] = q

        for i in range(NCA + 1):
            if i >= 1:
                emit_tail(i - 1)
            if i == NCA:
                emit_x_load_B0(0)
                emit_x_load_B0(1)
                gpB0 = C.get("gp")
                extra1 = {0: [(GP, C.get("gp_wA_136"))],
                          1: [(GP, C.get("gp_wA_137"))],
                          2: [(VE, C.get(f"cv_{NCA-1}_14"))]}
                sccB0 = C.get("scc")
                stage_dmas(piecesB[:64], "wB", gpB0, slot_w1, 6, extra=extra1,
                           split_scc_base=sccB0)
                extra2 = {0: [(PE, C.get(f"pe_fa_{NCA-1}"))],
                          4: [(GP, gpB0 + 31)],
                          5: [(SCC, sccB0 + 31)]}
                stage_dmas(piecesB[64:], "wB2", gpB0 + 32, slot_w2, 6,
                           extra=extra2, split_scc_base=sccB0 + 32)
                break
            if 2 <= i + 1 < NCA:
                emit_x_load(i + 1)
            if i + 1 < NCA:
                emit_ln_A(i + 1)
            emit_waves(i, (2, 3, 4, 5, 6, 7))
            if i + 1 < NCA:
                emit_tr_A(i + 1)
            emit_conv(i, (0, 2, 4, 6, 8, 10))
            if i + 1 < NCA:
                emit_waves(i + 1, (0, 1))
            emit_conv(i, (12, 14))
            emit_conv_prod(i)
            emit_attn(i)

        gatesB1 = [C.get(f"pe_wave_{NCA-1}_{q}") for q in range(8)
                   for _k in range(8)]
        gatesB2 = [C.get(f"pe_delta_{NCA-1}")] * 40
        stage_casts(piecesB[:64], "wB", slot_w1, 6, gatesB1, split=True)
        stage_casts(piecesB[64:], "wB2", slot_w2, 6, gatesB2, split=True)
        gp_wB_q = [[(GP, C.get(f"gp_wB_{q * 8 + 6}")),
                    (SCC, C.get(f"scc_wB_{q * 8 + 7}"))] for q in range(8)]
        gp_wB = [(GP, C.get("gp")), (SCC, C.get("scc"))]

        CQ = consts[:, 0:1]
        CI64 = consts[:, 1:2]
        CM128 = consts[:, 2:3]
        CH = consts[:, 3:4]
        CB0 = consts[:, 4:5]

        for j in range(NCB):
            for q in range(2):
                sl = (2 * j + q) % 2
                tag = f"B{j}_{q}"

                if j > 0:
                    @block.sync
                    def _(sync, j=j, q=q, sl=sl, tag=tag):
                        gk = f"ve_xn_B{j-1}_{q}"
                        sync.wait_ge(VE, C.get(gk))
                        sync.dma_start(out=xs[:, sl * D:(sl + 1) * D],
                                       in_=x_d[j * TB + q * 128:
                                               j * TB + (q + 1) * 128, :])\
                            .then_inc(LDX[sl], 16)
                        C.v[f"ldx_{tag}"] = C.add(f"ldx{sl}", 16)
                        C.v[f"ldxsl_{tag}"] = sl

                xn_guard = (C.get(f"trB{j-1}_{q}_last_pe") if j >= 1
                            else C.get(f"trA{14 + q}_last_pe"))
                prev_var = (f"B{j}_{q-1}" if q == 1 else
                            (f"B{j-1}_1" if j >= 1 else "A15"))
                ln_chunk(xs[:, sl * D:(sl + 1) * D], xn_sb[:, q * D:(q + 1) * D],
                         tag, prev_var, xn_guard)

                srcs = [xn_sb[:, q * D + d * 128: q * D + (d + 1) * 128]
                        for d in range(8)]
                dsts = [xnT_sb[:, d * TB + q * 128: d * TB + (q + 1) * 128]
                        for d in range(8)]
                @block.tensor
                def _(tensor, j=j, q=q):
                    if j >= 1 and q == 0:
                        tensor.wait_ge(PE, C.get(f"pe_h_{j-1}_{NFM-1}"))
                emit_transposes2(srcs, dsts, C.get(f"ve_xn_{tag}"), f"trB{j}_{q}")
                C.v[f"trB{j}_{q}_last_pe"] = C.get("pe")

            def wq_s(i):
                return wq_p[:, (i % 2) * TB:(i % 2 + 1) * TB]

            def XqP(f):
                return Xq_p[:, (f % 4) * TB:(f % 4) * TB + 2 * TB]

            def accP(f, half):
                s = (f // 2) % 2
                return acc_p[:, s * 4 * TB + half * 2 * TB:
                             s * 4 * TB + (half + 1) * 2 * TB]

            wqP = wq_p[:, 0:2 * TB]
            kkP = kk_p[:, 0:2 * TB]
            ttP = tt_p[:, 0:2 * TB]
            k2P = k2_p[:, 0:2 * TB]
            s3P = s3_t[:, 0:2 * TB]
            xq2P = xq2_t[:, 0:2 * TB]

            for g in range(0, NFM + 6, 2):
                @block.tensor
                def _(tensor, j=j, g=g):
                    if g < NFM:
                        gk = (f"sc_w_{j}_{g-2}" if g >= 2
                              else f"sc_w_{j-1}_{g+NFM-2}")
                        if C.get(gk):
                            tensor.wait_ge(SC, C.get(gk))
                        if g == 0:
                            tensor.wait_ge(SC, C.get(f"trB{j}_1_last"))
                            if j == 0:
                                tensor.wait_ge(SC, C.get(f"sc_wave_{NCA-1}_7"))
                            if C.get(f"sc_w4_{j-1}_7"):
                                tensor.wait_ge(SC, C.get(f"sc_w4_{j-1}_7"))
                        if j == 0:
                            for s_, v_ in gp_wB_q[g // 4]:
                                tensor.wait_ge(s_, v_)
                        for fg in (g, g + 1):
                            hs = (fg % 2) * 512
                            last = None
                            for kt in range(8):
                                last = tensor.matmul(
                                    out=pool[:, hs: hs + TB],
                                    lhsT=w1_kt(kt)[:, fg * 128:(fg + 1) * 128],
                                    rhs=xnT_sb[:, kt * TB:(kt + 1) * TB],
                                    start=(kt == 0), stop=(kt == 7))
                            last.then_inc(PE, 1)
                            C.v[f"pe_h_{j}_{fg}"] = C.add("pe")

                @block.gpsimd
                def _(gps, j=j, g=g):
                    f3 = g - 4
                    if 0 <= f3 < NFM:
                        gps.wait_ge(VE, C.get(f"ve_p_{j}_{f3}"))
                        gps.tensor_copy(k2P, accP(f3, 0)).then_inc(GP, 1)
                        C.v[f"gp_k2_{j}_{f3}"] = C.v[f"gp_k2_{j}_{f3+1}"] = C.add("gp")

                @block.scalar
                def _(scalar, j=j, g=g):
                    if g < NFM:
                        if g >= 2:
                            scalar.wait_ge(VE, C.get(f"ve_X_{j}_{g-2}"))
                        elif C.get(f"ve_fq_{j-1}_6"):
                            scalar.wait_ge(VE, C.get(f"ve_fq_{j-1}_6"))
                        for fg in (g, g + 1):
                            scalar.wait_ge(PE, C.get(f"pe_h_{j}_{fg}"))
                            scalar.activation(out=wq_s(fg),
                                              in_=pool[:, (fg % 2) * 512:
                                                       (fg % 2) * 512 + TB],
                                              func=AF.Identity, scale=CQ,
                                              bias=c1_sb[:, fg:fg + 1])
                        scalar.drain()
                        scalar.copy(out=kkP, in_=wqP).then_inc(SC, 1)
                        C.v[f"sc_w_{j}_{g}"] = C.v[f"sc_w_{j}_{g+1}"] = C.add("sc")
                    f2 = g - 2
                    if 0 <= f2 < NFM:
                        if f2 >= 2:
                            scalar.wait_ge(VE, C.get(f"ve_p_{j}_{f2-2}"))
                        else:
                            scalar.wait_ge(VE, C.get(f"ve_X_{j}_{0}"))
                        scalar.activation(out=ttP, in_=XqP(f2),
                                          func=AF.Abs, scale=CI64, bias=CM128)
                        scalar.activation(out=accP(f2, 0), in_=XqP(f2),
                                          func=AF.Identity, scale=CH,
                                          bias=CB0).then_inc(SC, 1)
                        C.v[f"sc_t_{j}_{f2}"] = C.v[f"sc_t_{j}_{f2+1}"] = C.add("sc")

                @block.vector
                def _(vector, j=j, g=g):
                    f3 = g - 4
                    f2 = g - 2
                    have3 = 0 <= f3 < NFM
                    have1 = g < NFM
                    have2 = 0 <= f2 < NFM
                    if have3:
                        vector.wait_ge(GP, C.get(f"gp_k2_{j}_{f3}"))
                        if f3 >= 2:
                            vector.wait_ge(SC, C.get(f"sc_y_{j}_{f3-2}"))
                        elif C.get(f"sc_f_{j-1}_7"):
                            vector.wait_ge(SC, C.get(f"sc_f_{j-1}_7"))
                    if have3 and not have2:
                        vector.drain()
                        vector._custom_dve(QF1_OP, out=s3P, in0=accP(f3, 0),
                                           in1=k2P, s0=8.0, imm2=4.0)
                        vector.drain()
                        vector._custom_dve(QF2_OP, out=xq2P, in0=s3P,
                                           in1=accP(f3, 0), s0=2.0, s1=1.0)\
                              .then_inc(VE, 1)
                        C.v[f"ve_q_{j}_{f3}"] = C.v[f"ve_q_{j}_{f3+1}"] = C.add("ve")
                    if have1 and not have2:
                        vector.wait_ge(SC, C.get(f"sc_w_{j}_{g}"))
                        vector.drain()
                        vector._custom_dve(QF1_OP, out=s3P, in0=wqP,
                                           in1=kkP, s0=8.0, imm2=4.0)
                        vector.drain()
                        vector._custom_dve(QF2_OP, out=XqP(g), in0=s3P,
                                           in1=wqP, s0=2.0, s1=1.0)\
                              .then_inc(VE, 1)
                        C.v[f"ve_X_{j}_{g}"] = C.v[f"ve_X_{j}_{g+1}"] = C.add("ve")
                    if have2:
                        vector.wait_ge(SC, C.get(f"sc_t_{j}_{f2}"))
                        cur, nxt = accP(f2, 0), accP(f2, 1)
                        last = None
                        for k, (c, beta, alpha) in enumerate(PAIRS):
                            if k > 0:
                                vector.drain()
                            last = vector._custom_dve(
                                PAIR_OP, out=nxt, in0=ttP, in1=cur,
                                s0=float(c), s1=float(beta), imm2=float(alpha))
                            cur, nxt = nxt, cur
                            if k == 0 and have3:
                                vector._custom_dve(QF1_OP, out=s3P,
                                                   in0=accP(f3, 0), in1=k2P,
                                                   s0=8.0, imm2=4.0)
                            elif k == 1 and have3:
                                vector._custom_dve(QF2_OP, out=xq2P, in0=s3P,
                                                   in1=accP(f3, 0),
                                                   s0=2.0, s1=1.0)\
                                      .then_inc(VE, 1)
                                C.v[f"ve_q_{j}_{f3}"] = C.v[f"ve_q_{j}_{f3+1}"] \
                                    = C.add("ve")
                            elif k == 5 and have1:
                                vector.wait_ge(SC, C.get(f"sc_w_{j}_{g}"))
                                vector._custom_dve(QF1_OP, out=s3P, in0=wqP,
                                                   in1=kkP, s0=8.0, imm2=4.0)
                            elif k == 6 and have1:
                                vector._custom_dve(QF2_OP, out=XqP(g), in0=s3P,
                                                   in1=wqP, s0=2.0, s1=1.0)\
                                      .then_inc(VE, 1)
                                C.v[f"ve_X_{j}_{g}"] = C.v[f"ve_X_{j}_{g+1}"] \
                                    = C.add("ve")
                        last.then_inc(VE, 1)
                        C.v[f"ve_p_{j}_{f2}"] = C.v[f"ve_p_{j}_{f2+1}"] = C.add("ve")

                @block.scalar
                def _(scalar, j=j, g=g):
                    f3 = g - 4
                    if 0 <= f3 < NFM:
                        scalar.wait_ge(VE, C.get(f"ve_q_{j}_{f3}"))
                        if f3 == 0 and C.get(f"pe_w2_{j-1}_7"):
                            scalar.wait_ge(PE, C.get(f"pe_w2_{j-1}_7"))
                        scalar.activation(out=y_all[:, f3 * TB:(f3 + 2) * TB],
                                          in_=xq2P, func=AF.Identity,
                                          scale=1.0 / 256, bias=consts[:, 6:7])\
                              .then_inc(SC, 1)
                        C.v[f"sc_y_{j}_{f3}"] = C.v[f"sc_y_{j}_{f3+1}"] = C.add("sc")


            for o in range(8):
                @block.tensor
                def _(tensor, j=j, o=o):
                    if o == 0:
                        tensor.wait_ge(SC, C.get(f"sc_y_{j}_{NFM-1}"))
                        if C.get(f"sc_w4_{j-1}_7"):
                            tensor.wait_ge(SC, C.get(f"sc_w4_{j-1}_7"))
                        if j == 0:
                            tensor.wait_ge(VE, C.get(f"ve_epi_A{NCA-1}"))
                            for s_, v_ in gp_wB:
                                tensor.wait_ge(s_, v_)
                    last = None
                    for f3 in range(NFM):
                        last = tensor.matmul(
                            out=pool[:, 1024 + o * TB: 1024 + (o + 1) * TB],
                            lhsT=w2_kt(f3)[:, o * 128:(o + 1) * 128],
                            rhs=y_all[:, f3 * TB:(f3 + 1) * TB],
                            start=(f3 == 0), stop=(f3 == NFM - 1))
                    last.then_inc(PE, 1)
                    C.v[f"pe_w2_{j}_{o}"] = C.add("pe")

            for op in range(0, 8, 2):
                @block.scalar
                def _(scalar, j=j, op=op):
                    scalar.wait_ge(PE, C.get(f"pe_w2_{j}_{op+1}"))
                    if op >= 2:
                        scalar.wait_ge(VE, C.get(f"ve_fq_{j}_{op-2}"))
                    else:
                        scalar.wait_ge(VE, C.get(f"ve_X_{j}_{NFM-1}"))
                    for o in (op, op + 1):
                        scalar.activation(out=wq_s(o),
                                          in_=pool[:, 1024 + o * TB:
                                                   1024 + (o + 1) * TB],
                                          func=AF.Identity, scale=CQ,
                                          bias=c2_sb[:, o:o + 1])
                    scalar.drain()
                    scalar.copy(out=kkP, in_=wqP).then_inc(SC, 1)
                    C.v[f"sc_w4_{j}_{op}"] = C.v[f"sc_w4_{j}_{op+1}"] = C.add("sc")

                @block.vector
                def _(vector, j=j, op=op):
                    vector.wait_ge(SC, C.get(f"sc_w4_{j}_{op}"))
                    if op == 0:
                        vector.wait_ge(SC, C.get(f"sc_y_{j}_{NFM-1}"))
                    else:
                        vector.wait_ge(SC, C.get(f"sc_f_{j}_{op-2}"))
                    vector.drain()
                    vector._custom_dve(QF1_OP, out=s3P, in0=wqP,
                                       in1=kkP, s0=8.0, imm2=4.0)
                    vector.drain()
                    vector._custom_dve(QF2_OP, out=xq2P, in0=s3P,
                                       in1=wqP, s0=2.0, s1=1.0)\
                          .then_inc(VE, 1)
                    C.v[f"ve_fq_{j}_{op}"] = C.v[f"ve_fq_{j}_{op+1}"] = C.add("ve")

                @block.scalar
                def _(scalar, j=j, op=op):
                    scalar.wait_ge(VE, C.get(f"ve_fq_{j}_{op}"))
                    if op == 0 and j == 0:
                        scalar.wait_ge(GP, C.get("gp_wB2_38"))
                        scalar.wait_ge(SCC, C.get("scc_wB2_39"))
                    if op == 0 and C.get(f"pe_finB_{j-1}_1_1"):
                        scalar.wait_ge(PE, C.get(f"pe_finB_{j-1}_1_1"))
                    scalar.activation(out=f_all[:, op * TB:(op + 2) * TB],
                                      in_=xq2P, func=AF.Identity,
                                      scale=1.0 / 256, bias=consts[:, 7:8])\
                          .then_inc(SC, 1)
                    C.v[f"sc_f_{j}_{op}"] = C.v[f"sc_f_{j}_{op+1}"] = C.add("sc")

            for tt in range(2):
                a_chunk = 2 * j + tt
                @block.sync
                def _(sync, j=j, tt=tt, a_chunk=a_chunk):
                    sync.wait_ge(STA[a_chunk % 2], C.get(f"st_A{a_chunk}"))
                    if a_chunk == 0:
                        for s_, v_ in gp_wB:
                            sync.wait_ge(s_, v_)
                    if C.get(f"ve_outh_{2*j+tt-1}_done"):
                        sync.wait_ge(VE, C.get(f"ve_outh_{2*j+tt-1}_done"))
                    sync.dma_start(out=stage[:],
                                   in_=out_d[j * TB + tt * 128:
                                             j * TB + (tt + 1) * 128, :])\
                        .then_inc(LDO, 16)
                    C.v[f"ldo_{a_chunk}"] = C.add("ldo", 16)

                for half in range(2):
                    @block.tensor
                    def _(tensor, j=j, tt=tt, half=half):
                        tensor.wait_ge(SC, C.get(f"sc_f_{j}_7"))
                        gk = (f"ve_outh_{j}_{tt}_{half-1}" if half == 1 else
                              (f"ve_outh_{j}_{tt-1}_1" if tt == 1 else
                               f"ve_outh_{j-1}_1_1"))
                        if C.get(gk):
                            tensor.wait_ge(VE, C.get(gk))
                        last = None
                        for oo in range(4):
                            o = half * 4 + oo
                            last = tensor.transpose(
                                out=pool[:, 3072 + oo * 128: 3072 + (oo + 1) * 128],
                                in_=f_all[:, o * TB + tt * 128:
                                          o * TB + tt * 128 + 128],
                                identity=idf[:])
                        last.then_inc(PE, 1)
                        C.v[f"pe_finB_{j}_{tt}_{half}"] = C.add("pe")

                    @block.vector
                    def _(vector, j=j, tt=tt, half=half, a_chunk=a_chunk):
                        vector.wait_ge(PE, C.get(f"pe_finB_{j}_{tt}_{half}"))
                        sl = (2 * j + tt) % 2
                        if half == 0:
                            vector.wait_ge(LDO, C.get(f"ldo_{a_chunk}"))
                        if half == 0 and C.get(f"st_B{2*j+tt-2}"):
                            vector.wait_ge(STB[sl], C.get(f"st_B{2*j+tt-2}"))
                        elif half == 0:
                            vector.wait_ge(STA[sl], C.get(f"st_A{14 + sl}"))
                        vector.tensor_tensor(
                            out=out_sb[:, sl * D + half * 512: sl * D + (half + 1) * 512],
                            in0=pool[:, 3072:3584],
                            in1=stage[:, half * 512:(half + 1) * 512],
                            op=AL.add).then_inc(VE, 1)
                        C.v[f"ve_outh_{j}_{tt}_{half}"] = C.add("ve")
                        if half == 1:
                            C.v[f"ve_outh_{a_chunk}_done"] = C.get("ve")

                @block.sync
                def _(sync, j=j, tt=tt, a_chunk=a_chunk):
                    sync.wait_ge(VE, C.get(f"ve_outh_{j}_{tt}_1"))
                    sl = (2 * j + tt) % 2
                    sync.dma_start(out=out_d[j * TB + tt * 128:
                                             j * TB + (tt + 1) * 128, :],
                                   in_=out_sb[:, sl * D:(sl + 1) * D])\
                        .then_inc(STB[sl], 16)
                    C.v[f"st_B{a_chunk}"] = C.add(f"stb{sl}", 16)

        @block.vector
        def _(vector):
            vector.wait_ge(STA[0], C.get("sta0"))
            vector.wait_ge(STA[1], C.get("sta1"))
            vector.wait_ge(STB[0], C.get("stb0"))
            vector.wait_ge(STB[1], C.get("stb1"))

        block.emit(real_block)

    return nc


_NC_CACHE = None


def kernel(L0_attn_x, L0_attn_w_in, L0_attn_w_conv, L0_attn_b_conv,
           L0_attn_w_sel, L0_attn_w_dt, L0_attn_b_dt, L0_attn_w_out,
           L0_attn_A, L0_attn_Dp, L0_ffn_w1, L0_ffn_b1, L0_ffn_w2, L0_ffn_b2):
    global _NC_CACHE
    x = np.ascontiguousarray(np.asarray(L0_attn_x, dtype=np.float32))
    w_in = np.ascontiguousarray(np.asarray(L0_attn_w_in, np.float32))
    w_conv = np.ascontiguousarray(
        np.asarray(L0_attn_w_conv, np.float32).reshape(E, 4)
        .reshape(16, 128, 4).transpose(1, 0, 2).reshape(128, 64))
    b_conv = np.ascontiguousarray(
        np.asarray(L0_attn_b_conv, np.float32).reshape(16, 128).T)
    w_sel = np.ascontiguousarray(np.asarray(L0_attn_w_sel, np.float32))
    w_dtb = np.zeros((128, E), np.float32)
    w_dtb[0:64, :] = np.asarray(L0_attn_w_dt, np.float32)
    w_dtb[64, :] = np.asarray(L0_attn_b_dt, np.float32)
    w_out = np.ascontiguousarray(np.asarray(L0_attn_w_out, np.float32))
    w1 = np.ascontiguousarray(np.asarray(L0_ffn_w1, np.float32))
    b1 = np.ascontiguousarray(np.asarray(L0_ffn_b1, np.float32).reshape(32, 128).T)
    w2 = np.ascontiguousarray(np.asarray(L0_ffn_w2, np.float32))
    b2 = np.ascontiguousarray(np.asarray(L0_ffn_b2, np.float32).reshape(8, 128).T)
    idb = np.eye(128, dtype=ml_dtypes.bfloat16)
    idf = np.eye(128, dtype=np.float32)

    if _NC_CACHE is None:
        nc = build_nc()
        mybir.codegen_inst_isa_subclasses(nc)
        _NC_CACHE = nc
    nc = _NC_CACHE

    in_maps = []
    for c in range(8):
        in_maps.append({
            "x": np.ascontiguousarray(x[c]),
            "w_in": w_in, "w_conv": w_conv, "b_conv": b_conv,
            "w_sel": w_sel, "w_dtb": w_dtb, "w_out": w_out,
            "w1": w1, "b1": b1, "w2": w2, "b2": b2,
            "idb": idb, "idf": idf,
        })
    res = run_bass_kernel_spmd(nc, in_maps, core_ids=list(range(8)))
    out = np.stack([np.asarray(r["out"], dtype=np.float32)
                    for r in res.results], axis=0)
    return out
